# version 10
# baseline (speedup 1.0000x reference)
import math
import numpy as np
import ml_dtypes

def _ensure_device():
    import os
    if os.environ.get("TRN_TERMINAL_POOL_IPS"):
        return
    os.environ["TRN_TERMINAL_POOL_IPS"] = os.environ.get("_KERNEL_AXON_IPS", "127.0.0.1")
    os.environ.setdefault("AXON_POOL_SVC_OVERRIDE", "127.0.0.1")
    os.environ.setdefault("AXON_LOOPBACK_RELAY", "1")
    try:
        from trn_agent_boot.trn_boot import boot
        boot(os.environ["TRN_TERMINAL_PRECOMPUTED_JSON"], "/opt/axon/libaxon_pjrt.so")
    except Exception:
        pass


import concourse.bass as bass
import concourse.mybir as mybir
from concourse.bass_utils import run_bass_kernel_spmd

F32 = mybir.dt.float32
BF16 = mybir.dt.bfloat16
AF = mybir.ActivationFunctionType
ALU = mybir.AluOpType
AX = mybir.AxisListType

NCORES = 8
T = 8192
D = 2048
NH = 16
DHR = 64
LAT = 512
DE = 8192
NKT = 16
NSLOT = 8
SM_SCALE = 1.0 / math.sqrt(128.0)
BMARGIN = 30.0 * math.sqrt(128.0)
NEG = -1.0e9
LN_EPS = 1.0e-5
MEAN_DIV = 1.0 / (DE * D)

ENGS = ("T", "V", "S", "P", "Y")

DEBUG_SCAL = False
PHASES = 99


class Buf:
    __slots__ = ("name", "writer", "readers", "excl", "accessors")

    def __init__(self, name, excl=False):
        self.name = name
        self.writer = None
        self.readers = []
        self.excl = excl
        self.accessors = []


class Rec:
    __slots__ = ("key", "fn", "deps", "need_inc", "val", "sem", "inc_amt", "gidx")

    def __init__(self, key, fn):
        self.key = key
        self.fn = fn
        self.deps = []
        self.need_inc = False
        self.val = None
        self.sem = None
        self.inc_amt = 1
        self.gidx = -1


class Prog:
    def __init__(self, nc):
        self.nc = nc
        self.recs = {k: [] for k in ENGS}
        self.esem = {k: nc.alloc_semaphore("sem_" + k) for k in ENGS}
        self.dsems = {}
        self.stream_recs = {}
        self.gctr = 0

    def _dep(self, rec, dep):
        if dep is None or dep is rec:
            return
        if dep.key == rec.key and dep.sem is None and rec.key == "T":
            return
        dep.need_inc = True
        rec.deps.append(dep)

    def emit(self, key, fn, reads=(), writes=(), dma_stream=None):
        rec = Rec(key, fn)
        rec.gidx = self.gctr
        self.gctr += 1
        for b in reads:
            if b.excl:
                for (a, w) in b.accessors:
                    if w or a.key != rec.key:
                        self._dep(rec, a)
            else:
                self._dep(rec, b.writer)
        for b in writes:
            if b.excl:
                for (a, _w) in b.accessors:
                    self._dep(rec, a)
            else:
                self._dep(rec, b.writer)
                for r in b.readers:
                    self._dep(rec, r)
        if dma_stream is not None:
            if dma_stream not in self.dsems:
                self.dsems[dma_stream] = [self.nc.alloc_semaphore("dq_" + dma_stream), 0, key]
            ent = self.dsems[dma_stream]
            assert ent[2] == key, f"stream {dma_stream}: {key} vs {ent[2]}"
            ent[1] += 16
            rec.sem = dma_stream
            rec.val = ent[1]
            rec.inc_amt = 16
            rec.need_inc = True
            self.stream_recs.setdefault(dma_stream, []).append(rec)
        for b in reads:
            if b.excl:
                b.accessors.append((rec, False))
            else:
                b.readers.append(rec)
        for b in writes:
            if b.excl:
                b.accessors.append((rec, True))
            else:
                b.writer = rec
                b.readers = []
        self.recs[key].append(rec)
        return rec

    def barrier(self, keys=ENGS):
        dma_lasts = [v[-1] for v in self.stream_recs.values() if v]
        arr = {}
        for k in keys:
            rec = Rec(k, lambda eng: eng.nop(nofuse=True))
            rec.gidx = self.gctr
            self.gctr += 1
            rec.need_inc = True
            if self.recs[k]:
                prev = self.recs[k][-1]
                prev.need_inc = True
                rec.deps.append(prev)
            self.recs[k].append(rec)
            arr[k] = rec
        col = Rec("Y", lambda eng: eng.nop(nofuse=True))
        col.gidx = self.gctr
        self.gctr += 1
        col.need_inc = True
        for k in keys:
            if k != "Y":
                col.deps.append(arr[k])
        for d in dma_lasts:
            col.deps.append(d)
        self.recs["Y"].append(col)
        for k in keys:
            if k == "Y":
                continue
            rec = Rec(k, lambda eng: eng.nop(nofuse=True))
            rec.gidx = self.gctr
            self.gctr += 1
            rec.deps.append(col)
            self.recs[k].append(rec)

    def finalize_and_replay(self):
        for k in ENGS:
            cnt = 0
            for rec in self.recs[k]:
                if rec.sem is None and rec.need_inc:
                    cnt += 1
                    rec.val = cnt
        self.bumped = {}
        for k in ENGS:
            recs = self.recs[k]
            n = len(recs)
            idx = {id(r): ii for ii, r in enumerate(recs)}
            for ii, r in enumerate(recs):
                if r.sem is not None or not r.need_inc:
                    continue
                v = r.val
                jj = ii + 1
                while jj < n and not recs[jj].deps and recs[jj].sem is None:
                    if recs[jj].need_inc:
                        v = recs[jj].val
                    jj += 1
                self.bumped[id(r)] = v

        def mk_body(key):
            def body(eng):
                watermark = {}
                for rec in self.recs[key]:
                    waits = {}
                    for dep in rec.deps:
                        if dep.val is None:
                            continue
                        if dep.sem is not None:
                            sem, nm = self.dsems[dep.sem][0], "d_" + dep.sem
                            srecs = self.stream_recs[dep.sem]
                            n = 0
                            for r2 in srecs:
                                if r2.gidx < rec.gidx:
                                    n += 1
                                else:
                                    break
                            val = max(dep.val, 16 * n)
                        else:
                            sem, nm = self.esem[dep.key], "e_" + dep.key
                            val = self.bumped.get(id(dep), dep.val)
                        if waits.get(nm, (None, -1))[1] < val:
                            waits[nm] = (sem, val)
                    todo = []
                    for nm, (sem, v) in sorted(waits.items()):
                        if watermark.get(nm, -1) >= v:
                            continue
                        todo.append((sem, v))
                        watermark[nm] = v
                    for sem, v in todo[:-1]:
                        eng.wait_ge(sem, v)
                    inst = rec.fn(eng)
                    if todo:
                        sem, v = todo[-1]
                        if inst is not None:
                            inst._wait_ge(sem, v)
                        else:
                            eng.wait_ge(sem, v)
                    if rec.need_inc and inst is not None:
                        if rec.sem is not None:
                            inst.then_inc(self.dsems[rec.sem][0], 16)
                        else:
                            inst.then_inc(self.esem[rec.key], 1)
            return body

        with self.nc.Block() as block:
            block.tensor(mk_body("T"))
            block.vector(mk_body("V"))
            block.scalar(mk_body("S"))
            block.gpsimd(mk_body("P"))
            block.sync(mk_body("Y"))


class Arena:
    def __init__(self, nc, name, cols, dtype):
        self.t = nc.alloc_sbuf_tensor(name, [128, cols], dtype)
        self.free = [(0, cols)]
        self.live = {}

    def alloc(self, name, cols, align=32):
        for idx, (st, sz) in enumerate(self.free):
            a = (st + align - 1) // align * align
            if a + cols <= st + sz:
                repl = []
                if a - st > 0:
                    repl.append((st, a - st))
                if (st + sz) - (a + cols) > 0:
                    repl.append((a + cols, (st + sz) - (a + cols)))
                self.free[idx:idx + 1] = repl
                self.live[name] = (a, cols)
                return self.t[:, a:a + cols]
        raise RuntimeError(f"arena OOM: {name} needs {cols}, free={self.free}")

    def release(self, *names):
        for name in names:
            st, sz = self.live.pop(name)
            self.free.append((st, sz))
        self.free.sort()
        out = []
        for seg in self.free:
            if out and out[-1][0] + out[-1][1] == seg[0]:
                out[-1] = (out[-1][0], out[-1][1] + seg[1])
            else:
                out.append(seg)
        self.free = out


def build_nc():
    nc = bass.Bass("TRN2", target_bir_lowering=False)

    def din(name, shape, dtype=F32):
        return nc.dram_tensor(name, list(shape), dtype, kind="ExternalInput").ap()

    x = din("x", [T, D])
    own_x = din("own_x", [NSLOT * 128, D])
    wdown = din("wdown", [D, 2 * LAT])
    bdown = din("bdown", [128, 8])
    wukv = din("wukv", [LAT, 2 * D])
    wuq = din("wuq", [LAT, D])
    buq = din("buq", [128, 16])
    wkr = din("wkr", [D, DHR])
    bkr = din("bkr", [DHR])
    wqr = din("wqr", [D, NH * DHR])
    bqr = din("bqr", [DHR, 16])
    wout = din("wout", [D, D])
    wupT = din("wupT", [D, DE])
    wgateT = din("wgateT", [D, DE])
    wdownT = din("wdownT", [DE, D])
    wupW = din("wupW", [16384, 1024])
    wgtW = din("wgtW", [16384, 1024])
    wdnW = din("wdnW", [16384, 1024])
    identf = din("identf", [128, 128])
    pswapf = din("pswapf", [DHR, DHR])
    ropekc = din("ropekc", [DHR, T])
    ropeks = din("ropeks", [DHR, T])
    ropeqc = din("ropeqc", [NH * DHR, 1024])
    ropeqs = din("ropeqs", [NH * DHR, 1024])
    maskadd = din("maskadd", [NSLOT * 2 * 128, 512], BF16)
    vbias_in = din("vbias_in", [128, D])
    woutb_in = din("woutb_in", [128, D])

    out_own = nc.dram_tensor("out_own", [NSLOT * 128, D], F32, kind="ExternalOutput").ap()
    dbg_scal = nc.dram_tensor("dbg_scal", [128, 12], F32, kind="ExternalOutput").ap() if DEBUG_SCAL else None
    dbg_q = nc.dram_tensor("dbg_q", [128, 3 * 2048], F32, kind="ExternalOutput").ap() if DEBUG_SCAL else None

    kscr = nc.dram_tensor("kscr", [NH * 128 + DHR, T], BF16, kind="Internal").ap()
    vscr = nc.dram_tensor("vscr", [T, NH * 129], BF16, kind="Internal").ap()
    oscr = nc.dram_tensor("oscr", [NSLOT * 128, D], BF16, kind="Internal").ap()
    qmscr = nc.dram_tensor("qmscr", [NH * 128, NSLOT * 128], BF16, kind="Internal").ap()
    qrscr = nc.dram_tensor("qrscr", [NH * (DHR + 1), NSLOT * 128], BF16, kind="Internal").ap()
    yscr = nc.dram_tensor("yscr", [NSLOT * 128, D], F32, kind="Internal").ap()

    p = Prog(nc)
    a16 = Arena(nc, "a16", 58624, BF16)
    a32 = Arena(nc, "a32", 20480, F32)

    ident_sb = nc.alloc_sbuf_tensor("ident_sb", [128, 128], BF16).ap()
    identf_sb = nc.alloc_sbuf_tensor("identf_sb", [128, 128], F32).ap()
    pswap_sb = nc.alloc_sbuf_tensor("pswap_sb", [DHR, DHR], BF16).ap()
    ones_sb = nc.alloc_sbuf_tensor("ones_sb", [128, 128], F32).ap()
    bdown_sb = nc.alloc_sbuf_tensor("bdown_sb", [128, 8], F32).ap()
    buq_sb = nc.alloc_sbuf_tensor("buq_sb", [128, 16], F32).ap()
    bkr_sb = nc.alloc_sbuf_tensor("bkr_sb", [DHR, 1], F32).ap()
    bqr_sb = nc.alloc_sbuf_tensor("bqr_sb", [DHR, 16], F32).ap()
    kr0_sb = nc.alloc_sbuf_tensor("kr0_sb", [DHR + 1, 128], BF16).ap()
    krext_bld = nc.alloc_sbuf_tensor("krext_bld", [DHR + 1, 512], BF16).ap()
    krext_st = [nc.alloc_sbuf_tensor(f"krext_st{i}", [DHR + 1, 512], BF16).ap() for i in range(2)]
    krraw_sb2 = [nc.alloc_sbuf_tensor(f"krraw{i}_sb", [DHR, 512], BF16).ap() for i in range(2)]
    krt1_sb2 = [nc.alloc_sbuf_tensor(f"krt1{i}_sb", [DHR, 512], F32).ap() for i in range(2)]
    stats_sb = nc.alloc_sbuf_tensor("stats_sb", [128, 32], F32).ap()
    lcols_sb = nc.alloc_sbuf_tensor("lcols_sb", [128, 16], F32).ap()
    linv_sb = nc.alloc_sbuf_tensor("linv_sb", [128, 16], F32).ap()
    s0all_sb = nc.alloc_sbuf_tensor("s0all_sb", [128, NSLOT * 16], F32).ap()
    scal_sb = nc.alloc_sbuf_tensor("scal_sb", [128, 12], F32).ap()
    eps_sb = nc.alloc_sbuf_tensor("eps_sb", [128, 1], F32).ap()

    PS = nc.alloc_psum_tensor("PS", [128, 4096], F32).ap()
    ps_a = PS[:, 0:512]
    ps_b = PS[:, 512:1024]
    ps_kr = PS[0:DHR, 1024:1536]
    ps_t = PS[:, 1536:2048].bitcast(BF16)[:, 0:512]
    ps_o = PS[:, 0:3072]
    ps_s = [PS[:, 3072:3584], PS[:, 3584:4096]]

    bb = {}

    def buf(name, excl=False):
        if name not in bb:
            bb[name] = Buf(name, excl)
        return bb[name]

    b_ps_a = buf("ps_a", True)
    b_ps_b = buf("ps_b", True)
    b_ps_t = buf("ps_t", True)
    b_ps_kr = buf("ps_kr", True)
    b_ps_o = buf("ps_o", True)
    b_ps_s = [buf("ps_s0", True), buf("ps_s1", True)]

    p.emit("P", lambda g: g.dma_start(out=ident_sb, in_=identf[:, :]), writes=[buf("ident")], dma_stream="cst")
    p.emit("Y", lambda g: g.dma_start(out=identf_sb, in_=identf[:, :]), writes=[buf("identf")], dma_stream="ycst")
    p.emit("P", lambda g: g.dma_start(out=pswap_sb, in_=pswapf[:, :]), writes=[buf("pswap")], dma_stream="cst")
    p.emit("Y", lambda g: g.dma_start(out=bdown_sb, in_=bdown[:, :]),
           writes=[buf("bdown")], dma_stream="ycst")
    p.emit("Y", lambda g: g.dma_start(out=buq_sb, in_=buq[:, :]),
           writes=[buf("buq")], dma_stream="ycst")
    p.emit("Y", lambda g: g.dma_start(out=bkr_sb, in_=bkr[:, None]), writes=[buf("bkr")], dma_stream="ycst")
    p.emit("Y", lambda g: g.dma_start(out=bqr_sb, in_=bqr[:, :]),
           writes=[buf("bqr")], dma_stream="ycst")
    p.emit("V", lambda v: v.memset(ones_sb, 1.0), writes=[buf("ones")])
    p.emit("V", lambda v: v.memset(eps_sb, LN_EPS), writes=[buf("eps")])
    p.emit("V", lambda v: v.memset(krext_bld[DHR:DHR + 1, :], 1.0), writes=[buf("krextd")])
    for par in range(2):
        p.emit("V", lambda v, par=par: v.memset(krext_st[par][DHR:DHR + 1, :], 1.0),
               writes=[buf(f"krst{par}")])

    wdown_sb = a16.alloc("wdown", 16 * 512)
    wukv_sb = a16.alloc("wukv", 4 * 4096)
    xnT_sb = a16.alloc("xnT", 16 * 512)
    KT_sb = a16.alloc("KT", NH * 512)
    V_sb = a16.alloc("Vb", 4 * 2064)
    xntok_sb = a16.alloc("xntok", D)
    ckvT_sb = a16.alloc("ckvT", 4 * 512)
    wkr_sb = a16.alloc("wkr", 16 * DHR)
    junk_sb = a16.alloc("junk", 2048)
    k0_sb = a16.alloc("k0", NH * 128)
    xin_sb = a32.alloc("xin", 2 * D)
    mean_sb = a32.alloc("meanbuf", 2 * 1024)
    hcoll_sb = a32.alloc("hcoll", 128)
    ht_sb = a32.alloc("ht", 128)
    h2_sb = a32.alloc("h2", 1024)
    l2r_sb = a32.alloc("l2r", 64)
    brow_sb = a32.alloc("brow", 1024)[0:NH, :]
    ropek_sb = a32.alloc("ropek", 1024)[0:DHR, :]
    ropeq_sb = a32.alloc("ropeq", 1024)[0:DHR, :]
    ropeq2_sb = a32.alloc("ropeq2", 1024)[0:DHR, :]

    p.emit("P", lambda g: g.dma_start(out=wdown_sb.rearrange("p (c l) -> p c l", l=512),
                                      in_=wdown[:, 0:512].rearrange("(c p) l -> p c l", p=128)),
           writes=[buf("wdown")], dma_stream="wA")
    p.emit("P", lambda g: g.dma_start(out=wukv_sb.rearrange("p (c l) -> p c l", l=4096),
                                      in_=wukv.rearrange("(c p) l -> p c l", p=128)),
           writes=[buf("wukv")], dma_stream="wA")
    p.emit("P", lambda g: g.dma_start(out=wkr_sb.rearrange("p (c l) -> p c l", l=DHR),
                                      in_=wkr.rearrange("(c p) l -> p c l", p=128)),
           writes=[buf("wkr")], dma_stream="wA")
    for s in range(4):
        p.emit("V", lambda v, s=s: v.memset(V_sb[:, 2064 * s + 128:2064 * (s + 1):129], 1.0),
               writes=[buf(f"Vb{s}")])

    def ln_stats(src_ap, b_src, slot):
        base = 8 * slot
        s1 = stats_sb[:, base:base + 1]
        s2 = stats_sb[:, base + 1:base + 2]
        bs = buf(f"lnstats{slot}")
        p.emit("V", lambda v: v.reduce_sum(s1, src_ap, AX.X), reads=[b_src], writes=[bs])
        p.emit("S", lambda sc: sc.activation(junk_sb, src_ap, AF.Square, accum_out=s2),
               reads=[b_src], writes=[bs, buf("junk")])

    def ln_math(slot):
        base = 8 * slot
        s1 = stats_sb[:, base:base + 1]
        s2 = stats_sb[:, base + 1:base + 2]
        mm = stats_sb[:, base + 2:base + 3]
        tmp = stats_sb[:, base + 3:base + 4]
        var = stats_sb[:, base + 4:base + 5]
        inv = stats_sb[:, base + 5:base + 6]
        ngm = stats_sb[:, base + 6:base + 7]
        bs = buf(f"lnstats{slot}")
        p.emit("V", lambda v: v.tensor_scalar(mm, s1, 1.0 / D, None, ALU.mult), reads=[bs], writes=[bs])
        p.emit("V", lambda v: v.tensor_tensor(tmp, mm, mm, ALU.mult), reads=[bs], writes=[bs])
        p.emit("V", lambda v: v.scalar_tensor_tensor(var, s2, 1.0 / D, tmp, ALU.mult, ALU.subtract),
               reads=[bs], writes=[bs])
        p.emit("S", lambda sc: sc.activation(tmp, var, AF.Ln, bias=eps_sb), reads=[bs, buf("eps")], writes=[bs])
        p.emit("S", lambda sc: sc.activation(inv, tmp, AF.Exp, scale=-0.5), reads=[bs], writes=[bs])
        p.emit("V", lambda v: v.scalar_tensor_tensor(ngm, mm, -1.0, inv, ALU.mult, ALU.mult),
               reads=[bs], writes=[bs])

    def ln_apply(src_ap, b_src, dst_ap, b_dst, slot):
        base = 8 * slot
        inv = stats_sb[:, base + 5:base + 6]
        ngm = stats_sb[:, base + 6:base + 7]
        bs = buf(f"lnstats{slot}")
        p.emit("S", lambda sc: sc.activation(dst_ap, src_ap, AF.Identity, bias=ngm, scale=inv),
               reads=[b_src, bs], writes=[b_dst])

    def ln_tile(src_ap, b_src, dst_ap=None, b_dst=None, out_f32=None, b_outf=None, slot=0):
        ln_stats(src_ap, b_src, slot)
        ln_math(slot)
        if dst_ap is not None:
            ln_apply(src_ap, b_src, dst_ap, b_dst, slot)
        if out_f32 is not None:
            ln_apply(src_ap, b_src, out_f32, b_outf, slot)

    def transpose_2048(src_ap, b_src, dst_ap_fn, b_dst):
        for g in range(4):
            for c in range(4):
                cc = 4 * g + c
                p.emit("T", lambda t, cc=cc, c=c: t.transpose(
                    ps_t[:, 128 * c:128 * (c + 1)], src_ap[:, 128 * cc:128 * (cc + 1)], ident_sb),
                    reads=[b_src, buf("ident")], writes=[b_ps_t])
            p.emit("V", lambda v, g=g: v.tensor_copy(
                dst_ap_fn(g), ps_t.rearrange("p (c q) -> p c q", q=128)),
                reads=[b_ps_t], writes=[b_dst])

    def rope_combine(par, psum2, b_psum2, bias_ap, b_bias, cos_ap, sin_ap, b_tab, out_ap, b_out):
        kr_raw, kr_t1 = krraw_sb2[par], krt1_sb2[par]
        braw, bt1 = buf(f"krraw{par}"), buf(f"krt1{par}")
        p.emit("S", lambda sc: sc.activation(kr_raw, psum2, AF.Identity, bias=bias_ap),
               reads=[b_psum2, b_bias], writes=[braw])
        p.emit("T", lambda t: t.matmul(psum2, pswap_sb, kr_raw, start=True, stop=True),
               reads=[buf("pswap"), braw], writes=[b_psum2])
        p.emit("V", lambda v: v.tensor_tensor(kr_t1, kr_raw, cos_ap, ALU.mult),
               reads=[braw, b_tab], writes=[bt1])
        p.emit("V", lambda v: v.tensor_tensor(psum2, psum2, sin_ap, ALU.mult),
               reads=[b_psum2, b_tab], writes=[b_psum2])
        p.emit("V", lambda v: v.tensor_tensor(out_ap, kr_t1, psum2, ALU.add),
               reads=[bt1, b_psum2], writes=[b_out])

    mean_jobs = [(which, w_ap, t) for which, w_ap in ((0, wupW), (1, wgtW), (2, wdnW))
                 for t in range(128)]
    mean_idx = [0]
    SCOL = {0: (0, 1), 1: (2, 3), 2: (4, 5)}

    def finalize_mean(which):
        bh = buf("hcoll")
        p.emit("T", lambda t: t.transpose(PS[:, 0:128], hcoll_sb, identf_sb),
               reads=[bh, buf("identf")], writes=[b_ps_a, b_ps_o])
        p.emit("V", lambda v: v.tensor_copy(ht_sb, PS[:, 0:128]),
               reads=[b_ps_a, b_ps_o], writes=[buf("ht")])
        for w2 in range(16):
            p.emit("P", lambda g, w2=w2: g.dma_start(
                out=h2_sb[w2:w2 + 1, :], in_=ht_sb[8 * w2:8 * w2 + 8, :]),
                reads=[buf("ht")], writes=[buf("h2")], dma_stream="mfl")
        p.emit("S", lambda sc: sc.activation(junk_sb[0:16, 0:1024], h2_sb[0:16, :],
                                             AF.Abs, accum_out=l2r_sb[0:16, 0:1]),
               reads=[buf("h2")], writes=[buf("l2c"), buf("junk")])
        p.emit("P", lambda g: g.dma_start(out=l2r_sb[0:1, 16:32], in_=l2r_sb[0:16, 0:1]),
               reads=[buf("l2c")], writes=[buf("l2r")], dma_stream="mfl")
        p.emit("V", lambda v: v.tensor_tensor(l2r_sb[0:1, 32:40], l2r_sb[0:1, 16:24],
                                              l2r_sb[0:1, 24:32], ALU.add),
               reads=[buf("l2r")], writes=[buf("l2r")])
        p.emit("V", lambda v: v.tensor_tensor(l2r_sb[0:1, 40:44], l2r_sb[0:1, 32:36],
                                              l2r_sb[0:1, 36:40], ALU.add),
               reads=[buf("l2r")], writes=[buf("l2r")])
        p.emit("V", lambda v: v.tensor_tensor(l2r_sb[0:1, 44:46], l2r_sb[0:1, 40:42],
                                              l2r_sb[0:1, 42:44], ALU.add),
               reads=[buf("l2r")], writes=[buf("l2r")])
        p.emit("V", lambda v: v.tensor_tensor(l2r_sb[0:1, 46:47], l2r_sb[0:1, 44:45],
                                              l2r_sb[0:1, 45:46], ALU.add),
               reads=[buf("l2r")], writes=[buf("l2r")])
        p.emit("V", lambda v: v.tensor_scalar(l2r_sb[0:1, 47:48], l2r_sb[0:1, 46:47],
                                              float(np.float32(1.0 / 16777216.0)),
                                              float(np.float32(1.0e-8)), ALU.mult, ALU.add),
               reads=[buf("l2r")], writes=[buf("l2r")])
        p.emit("V", lambda v: v.reciprocal(l2r_sb[0:1, 48:49], l2r_sb[0:1, 47:48]),
               reads=[buf("l2r")], writes=[buf("l2r")])
        cs, cr = SCOL[which]
        p.emit("T", lambda t: t.matmul(PS[:, 0:2], ones_sb[0:1, :],
                                       l2r_sb[0:1, 47:49], start=True, stop=True),
               reads=[buf("l2r"), buf("ones")], writes=[b_ps_a, b_ps_o])
        p.emit("V", lambda v, cs=cs, cr=cr: v.tensor_copy(
            scal_sb[:, cs:cs + 1], PS[:, 0:1]), reads=[b_ps_a, b_ps_o], writes=[buf("scal")])
        p.emit("V", lambda v, cs=cs, cr=cr: v.tensor_copy(
            scal_sb[:, cr:cr + 1], PS[:, 1:2]), reads=[b_ps_a, b_ps_o], writes=[buf("scal")])

    def pump_mean(n, fin_sink=None):
        for _ in range(n):
            if fin_sink:
                return
            if mean_idx[0] >= len(mean_jobs):
                return
            which, w_ap, t = mean_jobs[mean_idx[0]]
            mean_idx[0] += 1
            par = mean_idx[0] % 2
            mb = mean_sb[:, 1024 * par:1024 * (par + 1)]
            bmb = buf(f"meanbuf{par}")
            p.emit("Y", lambda y, w_ap=w_ap, t=t, mb=mb: y.dma_start(
                out=mb, in_=w_ap[128 * t:128 * (t + 1), :]),
                writes=[bmb], dma_stream=f"mean{par}")
            p.emit("S", lambda sc, t=t, mb=mb: sc.activation(
                junk_sb[:, 0:1024], mb, AF.Abs, accum_out=hcoll_sb[:, t:t + 1]),
                reads=[bmb], writes=[buf("hcoll"), buf("junk")])
            if t == 127:
                if fin_sink is not None:
                    fin_sink.append(which)
                else:
                    finalize_mean(which)

    def phase_a():
        b_rk = buf("ropek")
        for kt in range(NKT):
            p.emit("Y", lambda y, kt=kt: y.dma_start(
                out=ropek_sb[:, 0:512], in_=ropekc[:, 512 * kt:512 * (kt + 1)]),
                writes=[b_rk], dma_stream="rk")
            p.emit("Y", lambda y, kt=kt: y.dma_start(
                out=ropek_sb[:, 512:1024], in_=ropeks[:, 512 * kt:512 * (kt + 1)]),
                writes=[b_rk], dma_stream="rk")
            for sub in range(4):
                par = sub % 2
                xb = buf(f"xin{par}")
                xa = xin_sb[:, D * par:D * (par + 1)]
                row0 = 512 * kt + 128 * sub
                p.emit("Y", lambda y, row0=row0, xa=xa: y.dma_start(
                    out=xa, in_=x[row0:row0 + 128, :]), writes=[xb], dma_stream=f"xA{par}")
                ln_stats(xa, xb, sub % 2)
                ln_math(sub % 2)
                ln_apply(xa, xb, xntok_sb, buf("xntok"), sub % 2)

                def dst_fn(g, sub=sub):
                    return xnT_sb.rearrange("p (c q) -> p c q", q=512)[:, 4 * g:4 * g + 4,
                                                                      128 * sub:128 * (sub + 1)]
                transpose_2048(xntok_sb, buf("xntok"), dst_fn, buf("xnT"))
            for lc in range(4):
                pp, bpp = (ps_a, b_ps_a) if lc % 2 == 0 else (ps_b, b_ps_b)
                for dc in range(16):
                    p.emit("T", lambda t, lc=lc, dc=dc, pp=pp: t.matmul(
                        pp, wdown_sb[:, 512 * dc + 128 * lc:512 * dc + 128 * (lc + 1)],
                        xnT_sb[:, 512 * dc:512 * (dc + 1)],
                        start=(dc == 0), stop=(dc == 15)),
                        reads=[buf("wdown"), buf("xnT")], writes=[bpp])
                p.emit("S", lambda s, lc=lc, pp=pp: s.activation(
                    ckvT_sb[:, 512 * lc:512 * (lc + 1)], pp, AF.Identity,
                    bias=bdown_sb[:, lc:lc + 1]),
                    reads=[bpp, buf("bdown")], writes=[buf("ckvT")])
            for h in range(NH):
                pp, bpp = (ps_a, b_ps_a) if h % 2 == 0 else (ps_b, b_ps_b)
                for rc in range(4):
                    p.emit("T", lambda t, h=h, rc=rc, pp=pp: t.matmul(
                        pp, wukv_sb[:, 4096 * rc + 128 * h:4096 * rc + 128 * (h + 1)],
                        ckvT_sb[:, 512 * rc:512 * (rc + 1)],
                        start=(rc == 0), stop=(rc == 3)),
                        reads=[buf("wukv"), buf("ckvT")], writes=[bpp])
                p.emit("S", lambda s, h=h, pp=pp: s.copy(KT_sb[:, 512 * h:512 * (h + 1)], pp),
                       reads=[bpp], writes=[buf(f"KT{h}")])
                p.emit("P", lambda g, kt=kt, h=h: g.dma_start(
                    out=kscr[0:NH * 128, :].rearrange("(h p) c -> p h c", p=128)
                    [:, h:h + 1, 512 * kt:512 * (kt + 1)],
                    in_=KT_sb.rearrange("p (h c) -> p h c", c=512)[:, h:h + 1, :]),
                    reads=[buf(f"KT{h}")], dma_stream=f"spK{h % 2}")
            for sub in range(4):
                for q4 in range(4):
                    pp, bpp = (ps_a, b_ps_a) if (sub * 4 + q4) % 2 == 0 else (ps_b, b_ps_b)
                    for rc in range(4):
                        p.emit("T", lambda t, sub=sub, q4=q4, rc=rc, pp=pp: t.matmul(
                            pp, ckvT_sb[:, 512 * rc + 128 * sub:512 * rc + 128 * (sub + 1)],
                            wukv_sb[:, 4096 * rc + 2048 + 512 * q4:4096 * rc + 2048 + 512 * (q4 + 1)],
                            start=(rc == 0), stop=(rc == 3)),
                            reads=[buf("wukv"), buf("ckvT")], writes=[bpp])
                    p.emit("V", lambda v, sub=sub, q4=q4, pp=pp: v.tensor_copy(
                        V_sb.rearrange("p (s c) -> p s c", c=2064)[:, sub:sub + 1, :]
                        .rearrange("p s (h c) -> p (s h) c", c=129)[:, 4 * q4:4 * (q4 + 1), 0:128],
                        pp.rearrange("p (h c) -> p h c", c=128)),
                        reads=[bpp], writes=[buf(f"Vb{sub}")])
                p.emit("P", lambda g, kt=kt, sub=sub: g.dma_start(
                    out=vscr.rearrange("(n p) c -> p n c", p=128)
                    [:, 4 * kt + sub:4 * kt + sub + 1, :],
                    in_=V_sb.rearrange("p (s c) -> p s c", c=2064)[:, sub:sub + 1, :]),
                    reads=[buf(f"Vb{sub}")], dma_stream=f"spV{sub % 2}")
            for dc in range(16):
                p.emit("T", lambda t, dc=dc: t.matmul(
                    ps_kr, wkr_sb[:, DHR * dc:DHR * (dc + 1)], xnT_sb[:, 512 * dc:512 * (dc + 1)],
                    start=(dc == 0), stop=(dc == 15)),
                    reads=[buf("wkr"), buf("xnT")], writes=[b_ps_kr])
            rope_combine(kt % 2, ps_kr, b_ps_kr, bkr_sb, buf("bkr"),
                         ropek_sb[:, 0:512], ropek_sb[:, 512:1024], b_rk,
                         krext_bld[0:DHR, :], buf("krextd"))
            if kt == 0:
                p.emit("V", lambda v: v.tensor_copy(
                    k0_sb.rearrange("p (h c) -> p h c", c=128),
                    KT_sb.rearrange("p (h c) -> p h c", c=512)[:, :, 0:128]),
                    reads=[buf(f"KT{h}") for h in range(NH)], writes=[buf("k0")])
                p.emit("V", lambda v: v.tensor_copy(kr0_sb[0:DHR, :], krext_bld[0:DHR, 0:128]),
                       reads=[buf("krextd")], writes=[buf("kr0")])
                p.emit("V", lambda v: v.memset(kr0_sb[DHR:DHR + 1, :], 1.0), writes=[buf("kr0")])
            p.emit("P", lambda g, kt=kt: g.dma_start(
                out=kscr[NH * 128:NH * 128 + DHR, 512 * kt:512 * (kt + 1)],
                in_=krext_bld[0:DHR, :]),
                reads=[buf("krextd")], dma_stream="spK0")
            pump_mean(12)

    def phase_a2():
        a16.release("wdown", "wukv", "xnT", "KT", "Vb", "ckvT", "xntok", "wkr")
        ownxnT = a16.alloc("ownxnT", 16 * 1024)
        qtok = a16.alloc("qtok", D)
        b_own = buf("ownxnT")
        for i in range(NSLOT):
            par = i % 2
            xb = buf(f"xin{par}")
            xa = xin_sb[:, D * par:D * (par + 1)]
            p.emit("Y", lambda y, i=i, xa=xa: y.dma_start(
                out=xa, in_=own_x[128 * i:128 * (i + 1), :]), writes=[xb], dma_stream=f"xq{par}")
            ln_tile(xa, xb, dst_ap=qtok, b_dst=buf("qtok"), slot=i % 2)

            def dst_fn(g, i=i):
                return ownxnT.rearrange("p (c q) -> p c q", q=1024)[:, 4 * g:4 * g + 4,
                                                                   128 * i:128 * (i + 1)]
            transpose_2048(qtok, buf("qtok"), dst_fn, b_own)
        wqr_sb = a16.alloc("wqr", 16 * 1024)
        qrope = a16.alloc("qrope", NH * 1024)
        p.emit("P", lambda g: g.dma_start(out=wqr_sb.rearrange("p (c l) -> p c l", l=1024),
                                          in_=wqr.rearrange("(c p) l -> p c l", p=128)),
               writes=[buf("wqr")], dma_stream="wA")
        rq2 = [buf("ropeq0"), buf("ropeq1")]
        for h in range(NH):
            for th in range(2):
                par = (h * 2 + th) % 2
                pk = ps_kr if par == 0 else ps_a[0:DHR, :]
                bpk = b_ps_kr if par == 0 else b_ps_a
                rqa = ropeq_sb[:, 1024 * 0 + 512 * par:512 * par + 512] if False else None
                ca = ropeq_sb[:, 512 * par:512 * (par + 1)]
                p.emit("Y", lambda y, h=h, th=th, par=par: y.dma_start(
                    out=ropeq_sb[:, 512 * par:512 * (par + 1)].rearrange("p c -> p c"),
                    in_=ropeqc[DHR * h:DHR * (h + 1), 512 * th:512 * (th + 1)]),
                    writes=[rq2[par]], dma_stream=f"rq{par}")
                p.emit("Y", lambda y, h=h, th=th, par=par: y.dma_start(
                    out=ropeq2_sb[:, 512 * par:512 * (par + 1)],
                    in_=ropeqs[DHR * h:DHR * (h + 1), 512 * th:512 * (th + 1)]),
                    writes=[rq2[par]], dma_stream=f"rq{par}")
                for dc in range(16):
                    p.emit("T", lambda t, h=h, th=th, dc=dc, pk=pk: t.matmul(
                        pk, wqr_sb[:, 1024 * dc + DHR * h:1024 * dc + DHR * (h + 1)],
                        ownxnT[:, 1024 * dc + 512 * th:1024 * dc + 512 * (th + 1)],
                        start=(dc == 0), stop=(dc == 15)),
                        reads=[buf("wqr"), b_own], writes=[bpk])
                rope_combine(par, pk, bpk, bqr_sb[:, h:h + 1], buf("bqr"),
                             ropeq_sb[:, 512 * par:512 * (par + 1)],
                             ropeq2_sb[:, 512 * par:512 * (par + 1)], rq2[par],
                             qrope[0:DHR, 1024 * h + 512 * th:1024 * h + 512 * (th + 1)],
                             buf("qrope"))
        p.barrier()
        a16.release("wqr", "qtok")
        wdq_sb = a16.alloc("wdownq", 16 * 512)
        cqT_sb = a16.alloc("cqT", 4 * 1024)
        p.emit("P", lambda g: g.dma_start(out=wdq_sb.rearrange("p (c l) -> p c l", l=512),
                                          in_=wdown[:, 512:1024].rearrange("(c p) l -> p c l", p=128)),
               writes=[buf("wdownq")], dma_stream="wA")
        for lc in range(4):
            for th in range(2):
                pp, bpp = (ps_a, b_ps_a) if (lc * 2 + th) % 2 == 0 else (ps_b, b_ps_b)
                for dc in range(16):
                    p.emit("T", lambda t, lc=lc, th=th, dc=dc, pp=pp: t.matmul(
                        pp, wdq_sb[:, 512 * dc + 128 * lc:512 * dc + 128 * (lc + 1)],
                        ownxnT[:, 1024 * dc + 512 * th:1024 * dc + 512 * (th + 1)],
                        start=(dc == 0), stop=(dc == 15)),
                        reads=[buf("wdownq"), b_own], writes=[bpp])
                p.emit("S", lambda s, lc=lc, th=th, pp=pp: s.activation(
                    cqT_sb[:, 1024 * lc + 512 * th:1024 * lc + 512 * (th + 1)], pp,
                    AF.Identity, bias=bdown_sb[:, 4 + lc:5 + lc]),
                    reads=[bpp, buf("bdown")], writes=[buf("cqT")])
        p.barrier()
        a16.release("ownxnT", "wdownq")
        wuq_sb = a16.alloc("wuq", 4 * 2048)
        qmainT = a16.alloc("qmainT", NH * 1024)
        p.emit("P", lambda g: g.dma_start(out=wuq_sb.rearrange("p (c l) -> p c l", l=2048),
                                          in_=wuq.rearrange("(c p) l -> p c l", p=128)),
               writes=[buf("wuq")], dma_stream="wA")
        for h in range(NH):
            for th in range(2):
                pp, bpp = (ps_a, b_ps_a) if (h * 2 + th) % 2 == 0 else (ps_b, b_ps_b)
                for rc in range(4):
                    p.emit("T", lambda t, h=h, th=th, rc=rc, pp=pp: t.matmul(
                        pp, wuq_sb[:, 2048 * rc + 128 * h:2048 * rc + 128 * (h + 1)],
                        cqT_sb[:, 1024 * rc + 512 * th:1024 * rc + 512 * (th + 1)],
                        start=(rc == 0), stop=(rc == 3)),
                        reads=[buf("wuq"), buf("cqT")], writes=[bpp])
                p.emit("S", lambda s, h=h, th=th, pp=pp: s.activation(
                    qmainT[:, 1024 * h + 512 * th:1024 * h + 512 * (th + 1)], pp,
                    AF.Identity, bias=buq_sb[:, h:h + 1]),
                    reads=[bpp, buf("buq")], writes=[buf("qmainT")])
        p.barrier()
        a16.release("wuq", "cqT")
        a32.release("ropek", "ropeq", "ropeq2")
        return qmainT, qrope

    def phase_a3(qmainT, qrope):
        for i in range(NSLOT):
            for h in range(NH):
                pp, bpp = ps_s[h % 2], b_ps_s[h % 2]
                p.emit("T", lambda t, i=i, h=h, pp=pp: t.matmul(
                    pp[:, 0:128], qmainT[:, 1024 * h + 128 * i:1024 * h + 128 * (i + 1)],
                    k0_sb[:, 128 * h:128 * (h + 1)], start=True, stop=False),
                    reads=[buf("qmainT"), buf("k0")], writes=[bpp])
                p.emit("T", lambda t, i=i, h=h, pp=pp: t.matmul(
                    pp[:, 0:128], qrope[0:DHR, 1024 * h + 128 * i:1024 * h + 128 * (i + 1)],
                    kr0_sb[0:DHR, :], start=False, stop=True),
                    reads=[buf("qrope"), buf("kr0")], writes=[bpp])
                p.emit("V", lambda v, i=i, h=h, pp=pp: v.reduce_max(
                    s0all_sb[:, 16 * i + h:16 * i + h + 1], pp[:, 0:128], AX.X),
                    reads=[bpp], writes=[buf("s0all")])
        for i in range(NSLOT):
            p.emit("T", lambda t, i=i: t.transpose(
                ps_a[0:16, 0:128], s0all_sb[:, 16 * i:16 * (i + 1)], identf_sb),
                reads=[buf("s0all"), buf("identf")], writes=[b_ps_a])
            p.emit("V", lambda v, i=i: v.tensor_scalar(
                brow_sb[:, 128 * i:128 * (i + 1)], ps_a[0:NH, 0:128], BMARGIN, -1.0,
                ALU.add, ALU.mult),
                reads=[b_ps_a], writes=[buf("brow")])
        p.emit("P", lambda g: g.dma_start(
            out=qrope[DHR:DHR + 1, :], in_=brow_sb),
            reads=[buf("brow")], writes=[buf("qrope")], dma_stream="brw")
        p.emit("P", lambda g: g.dma_start(
            out=qmscr.rearrange("(h p) c -> p h c", p=128),
            in_=qmainT.rearrange("p (h c) -> p h c", c=1024)),
            reads=[buf("qmainT")], dma_stream="spQ")
        p.emit("P", lambda g: g.dma_start(
            out=qrscr.rearrange("(h p) c -> p h c", p=DHR + 1),
            in_=qrope[0:DHR + 1, :].rearrange("p (h c) -> p h c", c=1024)),
            reads=[buf("qrope")], dma_stream="spQ")
        a16.release("k0", "qmainT", "qrope")

    def phase_b():
        KTs = [a16.alloc(f"KTs{par}", NH * 512) for par in range(2)]
        Vs = [a16.alloc(f"Vs{par}", 4 * 2064) for par in range(2)]
        msk = a16.alloc("msk", 2 * 1024)
        pT = [a16.alloc(f"pTb{par}", 512) for par in range(2)]
        Osb = a16.alloc("Osb", D)
        qm = [a16.alloc(f"qm{par}", NH * 128) for par in range(2)]
        qr = [a16.alloc(f"qr{par}", NH * 128) for par in range(2)]
        a32.release("brow")
        vbias_sb = a32.alloc("vbias", D)
        p.emit("Y", lambda y: y.dma_start(out=vbias_sb, in_=vbias_in[:, :]),
               writes=[buf("vbias")], dma_stream="ycst")
        visit = 0
        pv_pending = [None]
        fin_pend = []

        def slot_inputs(i):
            mpar = i % 2
            p.emit("Y", lambda y, i=i, mpar=mpar: y.dma_start(
                out=msk[:, 1024 * mpar:1024 * (mpar + 1)].rearrange("p (m c) -> p m c", c=512),
                in_=maskadd[256 * i:256 * (i + 1), :].rearrange("(m p) c -> p m c", p=128)),
                writes=[buf(f"msk{mpar}")], dma_stream=f"mk{mpar}")
            p.emit("Y", lambda y, i=i, mpar=mpar: y.dma_start(
                out=qm[mpar].rearrange("p (h c) -> p h c", c=128),
                in_=qmscr.rearrange("(h p) c -> p h c", p=128)[:, :, 128 * i:128 * (i + 1)]),
                writes=[buf(f"qm{mpar}")], dma_stream=f"qm{mpar}")
            p.emit("Y", lambda y, i=i, mpar=mpar: y.dma_start(
                out=qr[mpar][0:DHR + 1, :].rearrange("p (h c) -> p h c", c=128),
                in_=qrscr.rearrange("(h p) c -> p h c", p=DHR + 1)[:, :, 128 * i:128 * (i + 1)]),
                writes=[buf(f"qr{mpar}")], dma_stream=f"qr{mpar}")

        slot_inputs(0)
        slot_inputs(1)
        for i in range(NSLOT):
            ci = 2 * (i + 1)
            mpar = i % 2
            bmk = buf(f"msk{mpar}")
            bqm = buf(f"qm{mpar}")
            bqr2 = buf(f"qr{mpar}")
            for kt in range(ci):
                par = visit % 2
                bK = buf(f"KTs{par}")
                bV = buf(f"Vs{par}")
                bkrs = buf(f"krst{par}")
                p.emit("Y", lambda y, kt=kt, par=par: y.dma_start(
                    out=KTs[par].rearrange("p (h c) -> p h c", c=512),
                    in_=kscr[0:NH * 128, :].rearrange("(h p) c -> p h c", p=128)[:, :, 512 * kt:512 * (kt + 1)]),
                    writes=[bK], dma_stream=f"kv{par}")
                p.emit("Y", lambda y, kt=kt, par=par: y.dma_start(
                    out=krext_st[par][0:DHR, :],
                    in_=kscr[NH * 128:NH * 128 + DHR, 512 * kt:512 * (kt + 1)]),
                    writes=[bkrs], dma_stream=f"kv{par}")
                p.emit("Y", lambda y, kt=kt, par=par: y.dma_start(
                    out=Vs[par].rearrange("p (s c) -> p s c", c=2064),
                    in_=vscr.rearrange("(n p) c -> p n c", p=128)[:, 4 * kt:4 * (kt + 1), :]),
                    writes=[bV], dma_stream=f"kv{par}")
                masked = kt >= ci - 2
                for h in range(NH):
                    sp, bsp = ps_s[h % 2], b_ps_s[h % 2]
                    ppar = h % 2
                    bpT = buf(f"pT{ppar}")
                    for s in range(4):
                        p.emit("T", lambda t, i=i, h=h, s=s, par=par, sp=sp, mpar=mpar: t.matmul(
                            sp[:, 128 * s:128 * (s + 1)],
                            KTs[par][:, 512 * h + 128 * s:512 * h + 128 * (s + 1)],
                            qm[mpar][:, 128 * h:128 * (h + 1)],
                            start=True, stop=False),
                            reads=[bK, bqm], writes=[bsp])
                        p.emit("T", lambda t, i=i, h=h, s=s, par=par, sp=sp, mpar=mpar: t.matmul(
                            sp[:, 128 * s:128 * (s + 1)],
                            krext_st[par][:, 128 * s:128 * (s + 1)],
                            qr[mpar][0:DHR + 1, 128 * h:128 * (h + 1)],
                            start=False, stop=True),
                            reads=[bkrs, bqr2], writes=[bsp])
                    if masked:
                        mi = kt - (ci - 2)
                        p.emit("V", lambda v, mi=mi, mpar=mpar, sp=sp: v.tensor_tensor(
                            sp, sp, msk[:, 1024 * mpar + 512 * mi:1024 * mpar + 512 * (mi + 1)],
                            ALU.add),
                            reads=[bmk, bsp], writes=[bsp])
                    p.emit("S", lambda s_, ppar=ppar, sp=sp: s_.activation(
                        pT[ppar], sp, AF.Exp, scale=SM_SCALE),
                        reads=[bsp], writes=[bpT])
                    if pv_pending[0] is not None:
                        pv_pending[0]()
                    bnk = h // 3
                    sslot = h % 3
                    ob = 512 * bnk + 129 * sslot
                    last_h = min(3 * bnk + 2, NH - 1)

                    def mk_pv(kt=kt, h=h, ppar=ppar, par=par, ob=ob, bnk=bnk,
                              sslot=sslot, last_h=last_h, ci=ci, bpT=bpT, bV=bV):
                        for s in range(4):
                            first = (kt == 0 and sslot == 0 and s == 0)
                            last = (kt == ci - 1 and h == last_h and s == 3)
                            p.emit("T", lambda t, s=s, ppar=ppar, par=par, h=h, ob=ob,
                                   first=first, last=last: t.matmul(
                                ps_o[:, ob:ob + 129],
                                pT[ppar][:, 128 * s:128 * (s + 1)],
                                Vs[par][:, 2064 * s + 129 * h:2064 * s + 129 * (h + 1)],
                                start=first, stop=last),
                                reads=[bpT, bV], writes=[b_ps_o])
                    pv_pending[0] = mk_pv
                visit += 1
                pump_mean(2, fin_pend)
            if pv_pending[0] is not None:
                pv_pending[0]()
                pv_pending[0] = None
            p.emit("V", lambda v: v.tensor_copy(
                lcols_sb[:, 0:15].rearrange("p (b s) -> p b s", s=3),
                ps_o.rearrange("p (b c) -> p b c", c=512)[:, 0:5, 128:388:129]),
                reads=[b_ps_o], writes=[buf("lcols")])
            p.emit("V", lambda v: v.tensor_copy(lcols_sb[:, 15:16],
                                                ps_o[:, 512 * 5 + 128:512 * 5 + 129]),
                   reads=[b_ps_o], writes=[buf("lcols")])
            p.emit("V", lambda v: v.reciprocal(linv_sb, lcols_sb),
                   reads=[buf("lcols")], writes=[buf("linv")])
            for h in range(NH):
                ob = 512 * (h // 3) + 129 * (h % 3)
                p.emit("V", lambda v, h=h, ob=ob: v.scalar_tensor_tensor(
                    Osb[:, 128 * h:128 * (h + 1)], ps_o[:, ob:ob + 128], linv_sb[:, h:h + 1],
                    vbias_sb[:, 128 * h:128 * (h + 1)], ALU.mult, ALU.add),
                    reads=[b_ps_o, buf("linv"), buf("vbias")], writes=[buf("Osb")])
            p.emit("P", lambda g, i=i: g.dma_start(out=oscr[128 * i:128 * (i + 1), :], in_=Osb),
                   reads=[buf("Osb")], dma_stream="spO")
            for w in fin_pend:
                finalize_mean(w)
            fin_pend.clear()
            if i + 2 < NSLOT:
                slot_inputs(i + 2)
        a16.release("KTs0", "KTs1", "Vs0", "Vs1", "msk", "pTb0", "pTb1", "Osb",
                    "qm0", "qm1", "qr0", "qr1")
        a32.release("vbias")

    def phase_c():
        wout_sb = a16.alloc("wout", 16 * 2048)
        Oin = a16.alloc("Oin", D)
        OT = a16.alloc("OT", D)
        xn2T = a16.alloc("xn2T", 16 * 1024)
        xn2tok = OT
        att_sb = a32.alloc("att", D)
        yt_sb = a32.alloc("yt", D)
        woutb_sb = a32.alloc("woutb", D)
        p.emit("Y", lambda y: y.dma_start(out=woutb_sb, in_=woutb_in[:, :]),
               writes=[buf("woutb")], dma_stream="ycst")
        p.emit("P", lambda g: g.dma_start(out=wout_sb.rearrange("p (c l) -> p c l", l=2048),
                                          in_=wout.rearrange("(c p) l -> p c l", p=128)),
               writes=[buf("wout")], dma_stream="wA")
        for i in range(NSLOT):
            par = i % 2
            bO = buf("Oin")
            p.emit("Y", lambda y, i=i: y.dma_start(
                out=Oin, in_=oscr[128 * i:128 * (i + 1), :]), writes=[bO], dma_stream="oin")

            def dst_fn(g):
                return OT.rearrange("p (c q) -> p c q", q=128)[:, 4 * g:4 * g + 4, :]
            transpose_2048(Oin, bO, dst_fn, buf("OT"))
            for q4 in range(4):
                pp, bpp = (ps_a, b_ps_a) if q4 % 2 == 0 else (ps_b, b_ps_b)
                for dc in range(16):
                    p.emit("T", lambda t, q4=q4, dc=dc, pp=pp: t.matmul(
                        pp, OT[:, 128 * dc:128 * (dc + 1)],
                        wout_sb[:, 2048 * dc + 512 * q4:2048 * dc + 512 * (q4 + 1)],
                        start=(dc == 0), stop=(dc == 15)),
                        reads=[buf("OT"), buf("wout")], writes=[bpp])
                p.emit("V", lambda v, q4=q4, pp=pp: v.tensor_tensor(
                    att_sb[:, 512 * q4:512 * (q4 + 1)], pp,
                    woutb_sb[:, 512 * q4:512 * (q4 + 1)], ALU.add),
                    reads=[bpp, buf("woutb")], writes=[buf("att")])
            xb = buf(f"xin{par}")
            xa = xin_sb[:, D * par:D * (par + 1)]
            p.emit("Y", lambda y, i=i, xa=xa: y.dma_start(
                out=xa, in_=own_x[128 * i:128 * (i + 1), :]), writes=[xb], dma_stream=f"xq{par}")
            ln_tile(att_sb, buf("att"), out_f32=yt_sb, b_outf=buf("yt"), slot=0)
            p.emit("V", lambda v, xa=xa: v.tensor_tensor(yt_sb, yt_sb, xa, ALU.add),
                   reads=[xb, buf("yt")], writes=[buf("yt")])
            p.emit("P", lambda g, i=i: g.dma_start(out=yscr[128 * i:128 * (i + 1), :], in_=yt_sb),
                   reads=[buf("yt")], dma_stream="spY")
            ln_tile(yt_sb, buf("yt"), dst_ap=xn2tok, b_dst=buf("OT"), slot=1)

            def dst2_fn(g, i=i):
                return xn2T.rearrange("p (c q) -> p c q", q=1024)[:, 4 * g:4 * g + 4,
                                                                 128 * i:128 * (i + 1)]
            transpose_2048(xn2tok, buf("OT"), dst2_fn, buf("xn2T"))
            pump_mean(8)
        a16.release("wout", "Oin", "OT")
        a32.release("att", "yt", "woutb")
        return xn2T

    def phase_d(xn2T):
        pump_mean(len(mean_jobs))
        a32.release("xin", "meanbuf", "hcoll", "ht", "h2", "l2r")
        bscal = buf("scal")
        p.emit("V", lambda v: v.tensor_tensor(scal_sb[:, 9:10], scal_sb[:, 2:3],
                                              scal_sb[:, 4:5], ALU.mult),
               reads=[bscal], writes=[bscal])

        if DEBUG_SCAL:
            p.emit("Y", lambda y: y.dma_start(out=dbg_scal[:, :], in_=scal_sb),
                   reads=[buf("scal")], dma_stream="dbg")
        assert mean_idx[0] == len(mean_jobs), "mean jobs must be drained before D"
        a16.release("junk")
        outacc = a32.alloc("outacc", 16 * 1024)
        wst = [a32.alloc(f"wst{j}", 512) for j in range(8)]
        tupAB = [a16.alloc("tupA", 16 * 512), a16.alloc("tupB", 16 * 512)]
        tgt = a16.alloc("tgt", 16 * 512)
        tdn = a16.alloc("tdn", 4 * 2048)
        hid = a16.alloc("hid", 4 * 1024)
        q1 = a16.alloc("q1", 512)
        q2 = a16.alloc("q2", 512)
        silu_sb2 = [a16.alloc(f"silu{j}", 512) for j in range(2)]

        pd = [PS[:, 1024 + 512 * k:1536 + 512 * k] for k in range(6)]
        b_pd = [buf(f"pd{k}", True) for k in range(6)]

        def quant(dst_ap, b_dst, src_ap, b_src, rcol):
            p.emit("V", lambda v: v.tensor_scalar(
                q1, src_ap, scal_sb[:, rcol:rcol + 1], 0.5, ALU.mult, ALU.is_gt),
                reads=[b_src, bscal], writes=[buf("q1")])
            p.emit("V", lambda v: v.tensor_scalar(
                q2, src_ap, scal_sb[:, rcol:rcol + 1], -0.5, ALU.mult, ALU.is_lt),
                reads=[b_src, bscal], writes=[buf("q2")])
            p.emit("V", lambda v: v.tensor_tensor(dst_ap, q1, q2, ALU.subtract),
                   reads=[buf("q1"), buf("q2")], writes=[b_dst])

        st_ctr = [0]

        def quant_tile(j, g):
            st = st_ctr[0] % 8
            st_ctr[0] += 1
            wa, wb = wst[st], buf(f"wst{st}")
            if j < 32:
                wT = wupT if j < 16 else wgateT
                dc = j % 16
                src = lambda: wT[128 * dc:128 * (dc + 1), 512 * g:512 * (g + 1)]
                if j < 16:
                    dst = tupAB[g % 2][:, 512 * dc:512 * (dc + 1)]
                    bdst = buf(f"tup{g % 2}_{dc}")
                    rcol = 1
                else:
                    dst = tgt[:, 512 * dc:512 * (dc + 1)]
                    bdst = buf(f"tgt{dc}")
                    rcol = 3
            else:
                k = j - 32
                rc, s2 = k // 4, k % 4
                src = lambda: wdownT[512 * g + 128 * rc:512 * g + 128 * (rc + 1),
                                     512 * s2:512 * (s2 + 1)]
                dst = tdn[:, 2048 * rc + 512 * s2:2048 * rc + 512 * (s2 + 1)]
                bdst = buf(f"tdn{rc}_{s2}")
                rcol = 5
            p.emit("Y", lambda y: y.dma_start(out=wa, in_=src()),
                   writes=[wb], dma_stream=f"wD{st}")
            quant(dst, bdst, wa, wb, rcol)

        for j in range(48):
            quant_tile(j, 0)
        for g in range(16):
            tup = tupAB[g % 2]
            up_tiles = [buf(f"tup{g % 2}_{dc}") for dc in range(16)]
            gt_tiles = [buf(f"tgt{dc}") for dc in range(16)]
            for hc in range(4):
                for th in range(2):
                    idx = hc * 2 + th
                    sl, bsl = silu_sb2[idx % 2], buf(f"silu{idx % 2}")
                    for dc in range(16):
                        rds = ([buf("xn2T")] + up_tiles) if dc in (0, 15) else []
                        p.emit("T", lambda t, hc=hc, th=th, dc=dc, tup=tup: t.matmul(
                            ps_a, tup[:, 512 * dc + 128 * hc:512 * dc + 128 * (hc + 1)],
                            xn2T[:, 1024 * dc + 512 * th:1024 * dc + 512 * (th + 1)],
                            start=(dc == 0), stop=(dc == 15)),
                            reads=rds, writes=[b_ps_a])
                    for dc in range(16):
                        rds = ([buf("xn2T")] + gt_tiles) if dc in (0, 15) else []
                        p.emit("T", lambda t, hc=hc, th=th, dc=dc: t.matmul(
                            ps_b, tgt[:, 512 * dc + 128 * hc:512 * dc + 128 * (hc + 1)],
                            xn2T[:, 1024 * dc + 512 * th:1024 * dc + 512 * (th + 1)],
                            start=(dc == 0), stop=(dc == 15)),
                            reads=rds, writes=[b_ps_b])
                    p.emit("S", lambda sc, sl=sl: sc.activation(
                        sl, ps_a, AF.Silu, scale=scal_sb[:, 0:1]),
                        reads=[b_ps_a, bscal], writes=[bsl])
                    p.emit("V", lambda v, hc=hc, th=th, sl=sl: v.tensor_tensor(
                        hid[:, 1024 * hc + 512 * th:1024 * hc + 512 * (th + 1)],
                        sl, ps_b, ALU.mult),
                        reads=[bsl, b_ps_b], writes=[buf(f"hid{idx}")])
                    if g + 1 < 16:
                        quant_tile(2 * idx, g + 1)
                        quant_tile(2 * idx + 1, g + 1)
            for dm in range(16):
                for th in range(2):
                    it = dm * 2 + th
                    k = it % 6
                    dn_tiles = ([buf(f"tdn{rc}_{dm // 4}") for rc in range(4)] +
                                [buf(f"hid{rc * 2 + th}") for rc in range(4)])
                    for rc in range(4):
                        rds = dn_tiles if rc in (0, 3) else []
                        p.emit("T", lambda t, dm=dm, th=th, rc=rc, k=k: t.matmul(
                            pd[k], tdn[:, 2048 * rc + 128 * dm:2048 * rc + 128 * (dm + 1)],
                            hid[:, 1024 * rc + 512 * th:1024 * rc + 512 * (th + 1)],
                            start=(rc == 0), stop=(rc == 3)),
                            reads=rds, writes=[b_pd[k]])
                    if th == 1:
                        k2 = (2 * dm) % 6
                        if g == 0:
                            p.emit("V", lambda v, dm=dm, k2=k2: v.tensor_copy(
                                outacc[:, 1024 * dm:1024 * (dm + 1)],
                                PS[:, 1024 + 512 * k2:2048 + 512 * k2]),
                                reads=[b_pd[k2], b_pd[k2 + 1]], writes=[buf("outacc")])
                        else:
                            p.emit("V", lambda v, dm=dm, k2=k2: v.tensor_tensor(
                                outacc[:, 1024 * dm:1024 * (dm + 1)],
                                outacc[:, 1024 * dm:1024 * (dm + 1)],
                                PS[:, 1024 + 512 * k2:2048 + 512 * k2], ALU.add),
                                reads=[b_pd[k2], b_pd[k2 + 1], buf("outacc")],
                                writes=[buf("outacc")])
                    if g + 1 < 16:
                        if it < 16:
                            quant_tile(16 + it, g + 1)
                        if th == 1 and dm % 4 == 3:
                            s2 = dm // 4
                            for rc in range(4):
                                quant_tile(32 + rc * 4 + s2, g + 1)
        a16.release("tupA", "tupB", "tgt", "tdn", "hid", "q1", "q2", "silu0", "silu1", "xn2T")
        a32.release(*[f"wst{j}" for j in range(8)])
        return outacc

    def phase_e(outacc):
        ffn_sb = a32.alloc("ffn", D)
        fout = a32.alloc("fout", D)
        for i in range(NSLOT):
            p.emit("Y", lambda y, i=i: y.dma_start(
                out=fout, in_=yscr[128 * i:128 * (i + 1), :]), writes=[buf("fout")], dma_stream="yin")
            for g in range(4):
                for c in range(4):
                    dc = 4 * g + c
                    p.emit("T", lambda t, dc=dc, c=c, i=i: t.transpose(
                        ps_b[:, 128 * c:128 * (c + 1)],
                        outacc[:, 1024 * dc + 128 * i:1024 * dc + 128 * (i + 1)], identf_sb),
                        reads=[buf("outacc"), buf("identf")], writes=[b_ps_b])
                p.emit("V", lambda v, g=g: v.tensor_scalar(
                    ffn_sb[:, 512 * g:512 * (g + 1)], ps_b, scal_sb[:, 9:10], None, ALU.mult),
                    reads=[b_ps_b, buf("scal")], writes=[buf("ffn")])
            ln_tile(ffn_sb, buf("ffn"), out_f32=ffn_sb, b_outf=buf("ffn"), slot=i % 2)
            p.emit("V", lambda v: v.tensor_tensor(fout, fout, ffn_sb, ALU.add),
                   reads=[buf("ffn"), buf("fout")], writes=[buf("fout")])
            p.emit("Y", lambda y, i=i: y.dma_start(out=out_own[128 * i:128 * (i + 1), :], in_=fout),
                   reads=[buf("fout")], dma_stream="out")
        last_val = p.dsems["out"][1]
        p.recs["Y"].append(Rec("Y", lambda y: y.wait_ge(p.dsems["out"][0], last_val)))

    phase_a()
    if PHASES >= 2:
        p.barrier()
        qmainT, qrope = phase_a2()
        phase_a3(qmainT, qrope)
    if PHASES >= 3:
        p.barrier()
        phase_b()
    if PHASES >= 4:
        p.barrier()
        xn2T = phase_c()
    if PHASES >= 5:
        p.barrier()
        outacc = phase_d(xn2T)
    if PHASES >= 6:
        p.barrier()
        phase_e(outacc)

    p.finalize_and_replay()
    return nc


def _win_rearrange(W):
    R, C = W.shape
    nR, nC = R // 32, C // 32
    A = W.reshape(nR, 32, nC, 32).transpose(0, 2, 1, 3).reshape(nR, nC, 1024)
    B = A.reshape(nR // 32, 32, nC // 32, 32, 1024).transpose(0, 2, 1, 3, 4)
    return np.ascontiguousarray(B.reshape(16384, 1024))


def _rope_tables(n):
    pos = np.repeat(np.arange(n // 2, dtype=np.float64), 2)
    theta = np.exp(-2.0 * pos / n * np.log(10000.0))
    freq = np.arange(T, dtype=np.float64) + 1.0
    ang = freq[:, None] * theta[None, :]
    return np.cos(ang).astype(np.float32), np.sin(ang).astype(np.float32)


def kernel(L0_attn_x, L0_attn_W_down_k, L0_attn_W_down_b, L0_attn_W_uKV_k, L0_attn_W_uKV_b,
           L0_attn_W_uQ_k, L0_attn_W_uQ_b, L0_attn_Wkr_k, L0_attn_Wkr_b, L0_attn_Wqr_k,
           L0_attn_Wqr_b, L0_attn_Wout_k, L0_attn_Wout_b, L0_ffn_W_up, L0_ffn_W_gate,
           L0_ffn_W_down):
    _ensure_device()
    f32 = np.float32
    x = np.ascontiguousarray(np.asarray(L0_attn_x, f32).reshape(T, D))
    wdown = np.ascontiguousarray(np.asarray(L0_attn_W_down_k, f32))
    bdown = np.ascontiguousarray(np.asarray(L0_attn_W_down_b, f32))
    bukv = np.ascontiguousarray(np.asarray(L0_attn_W_uKV_b, f32))
    bout = np.ascontiguousarray(np.asarray(L0_attn_Wout_b, f32))

    identf = np.eye(128, dtype=f32)
    pswap = np.zeros((DHR, DHR), f32)
    for m in range(DHR // 2):
        pswap[2 * m + 1, 2 * m] = -1.0
        pswap[2 * m, 2 * m + 1] = 1.0
    ropekc, ropeks = _rope_tables(DHR)
    ropeqc_full, ropeqs_full = _rope_tables(NH * DHR)

    nc = build_nc()
    common = {
        "x": x, "wdown": wdown,
        "bdown": np.ascontiguousarray(bdown.reshape(8, 128).T),
        "wukv": np.ascontiguousarray(np.asarray(L0_attn_W_uKV_k, f32)),
        "wuq": np.ascontiguousarray(np.asarray(L0_attn_W_uQ_k, f32)),
        "buq": np.ascontiguousarray(np.asarray(L0_attn_W_uQ_b, f32).reshape(16, 128).T),
        "wkr": np.ascontiguousarray(np.asarray(L0_attn_Wkr_k, f32)),
        "bkr": np.ascontiguousarray(np.asarray(L0_attn_Wkr_b, f32)),
        "wqr": np.ascontiguousarray(np.asarray(L0_attn_Wqr_k, f32)),
        "bqr": np.ascontiguousarray(np.asarray(L0_attn_Wqr_b, f32).reshape(16, DHR).T),
        "wout": np.ascontiguousarray(np.asarray(L0_attn_Wout_k, f32)),
        "wupT": np.ascontiguousarray(np.asarray(L0_ffn_W_up, f32).T),
        "wgateT": np.ascontiguousarray(np.asarray(L0_ffn_W_gate, f32).T),
        "wdownT": np.ascontiguousarray(np.asarray(L0_ffn_W_down, f32).T),
        "wupW": _win_rearrange(np.asarray(L0_ffn_W_up, f32)),
        "wgtW": _win_rearrange(np.asarray(L0_ffn_W_gate, f32)),
        "wdnW": _win_rearrange(np.asarray(L0_ffn_W_down, f32)),
        "identf": identf, "pswapf": pswap,
        "ropekc": np.ascontiguousarray(ropekc.T), "ropeks": np.ascontiguousarray(ropeks.T),
        "vbias_in": np.ascontiguousarray(np.broadcast_to(bukv[D:2 * D], (128, D))),
        "woutb_in": np.ascontiguousarray(np.broadcast_to(bout, (128, D))),
    }
    in_maps = []
    for j in range(NCORES):
        tiles = [8 * i + j for i in range(NSLOT)]
        rows = np.concatenate([np.arange(128 * t_, 128 * t_ + 128) for t_ in tiles])
        mk = np.zeros((NSLOT, 2, 128, 512), f32)
        for ii in range(NSLOT):
            t_ = tiles[ii]
            ci = 2 * (ii + 1)
            qglob = 128 * t_ + np.arange(128)
            for m2 in range(2):
                kt = ci - 2 + m2
                for s in range(4):
                    kg = 512 * kt + 128 * s + np.arange(128)[:, None]
                    mk[ii, m2, :, 128 * s:128 * (s + 1)] = np.where(kg <= qglob[None, :], 0.0, NEG)
        im = dict(common)
        im["own_x"] = np.ascontiguousarray(x[rows])
        im["ropeqc"] = np.ascontiguousarray(ropeqc_full[rows].T)
        im["ropeqs"] = np.ascontiguousarray(ropeqs_full[rows].T)
        im["maskadd"] = mk.reshape(NSLOT * 2 * 128, 512).astype(ml_dtypes.bfloat16)
        in_maps.append(im)

    res = run_bass_kernel_spmd(nc, in_maps, core_ids=list(range(NCORES)))
    out = np.zeros((T, D), f32)
    for j in range(NCORES):
        r = res.results[j]["out_own"]
        for ii in range(NSLOT):
            t_ = 8 * ii + j
            out[128 * t_:128 * (t_ + 1)] = r[128 * ii:128 * (ii + 1)]
    return out.reshape(1, T, D)


# version 11
# speedup vs baseline: 1.0493x; 1.0493x over previous
import math
import numpy as np
import ml_dtypes

def _ensure_device():
    import os
    if os.environ.get("TRN_TERMINAL_POOL_IPS"):
        return
    os.environ["TRN_TERMINAL_POOL_IPS"] = os.environ.get("_KERNEL_AXON_IPS", "127.0.0.1")
    os.environ.setdefault("AXON_POOL_SVC_OVERRIDE", "127.0.0.1")
    os.environ.setdefault("AXON_LOOPBACK_RELAY", "1")
    try:
        from trn_agent_boot.trn_boot import boot
        boot(os.environ["TRN_TERMINAL_PRECOMPUTED_JSON"], "/opt/axon/libaxon_pjrt.so")
    except Exception:
        pass


import concourse.bass as bass
import concourse.mybir as mybir
from concourse.bass_utils import run_bass_kernel_spmd

F32 = mybir.dt.float32
BF16 = mybir.dt.bfloat16
AF = mybir.ActivationFunctionType
ALU = mybir.AluOpType
AX = mybir.AxisListType

NCORES = 8
T = 8192
D = 2048
NH = 16
DHR = 64
LAT = 512
DE = 8192
NKT = 16
NSLOT = 8
SM_SCALE = 1.0 / math.sqrt(128.0)
BMARGIN = 30.0 * math.sqrt(128.0)
NEG = -1.0e9
LN_EPS = 1.0e-5
MEAN_DIV = 1.0 / (DE * D)

ENGS = ("T", "V", "S", "P", "Y")

DEBUG_SCAL = False
PHASES = 99


class Buf:
    __slots__ = ("name", "writer", "readers", "excl", "accessors")

    def __init__(self, name, excl=False):
        self.name = name
        self.writer = None
        self.readers = []
        self.excl = excl
        self.accessors = []


class Rec:
    __slots__ = ("key", "fn", "deps", "need_inc", "val", "sem", "inc_amt", "gidx")

    def __init__(self, key, fn):
        self.key = key
        self.fn = fn
        self.deps = []
        self.need_inc = False
        self.val = None
        self.sem = None
        self.inc_amt = 1
        self.gidx = -1


class Prog:
    def __init__(self, nc):
        self.nc = nc
        self.recs = {k: [] for k in ENGS}
        self.esem = {k: nc.alloc_semaphore("sem_" + k) for k in ENGS}
        self.dsems = {}
        self.stream_recs = {}
        self.gctr = 0

    def _dep(self, rec, dep):
        if dep is None or dep is rec:
            return
        if dep.key == rec.key and dep.sem is None and rec.key == "T":
            return
        dep.need_inc = True
        rec.deps.append(dep)

    def emit(self, key, fn, reads=(), writes=(), dma_stream=None):
        rec = Rec(key, fn)
        rec.gidx = self.gctr
        self.gctr += 1
        for b in reads:
            if b.excl:
                for (a, w) in b.accessors:
                    if w or a.key != rec.key:
                        self._dep(rec, a)
            else:
                self._dep(rec, b.writer)
        for b in writes:
            if b.excl:
                for (a, _w) in b.accessors:
                    self._dep(rec, a)
            else:
                self._dep(rec, b.writer)
                for r in b.readers:
                    self._dep(rec, r)
        if dma_stream is not None:
            if dma_stream not in self.dsems:
                self.dsems[dma_stream] = [self.nc.alloc_semaphore("dq_" + dma_stream), 0, key]
            ent = self.dsems[dma_stream]
            assert ent[2] == key, f"stream {dma_stream}: {key} vs {ent[2]}"
            ent[1] += 16
            rec.sem = dma_stream
            rec.val = ent[1]
            rec.inc_amt = 16
            rec.need_inc = True
            self.stream_recs.setdefault(dma_stream, []).append(rec)
        for b in reads:
            if b.excl:
                b.accessors.append((rec, False))
            else:
                b.readers.append(rec)
        for b in writes:
            if b.excl:
                b.accessors.append((rec, True))
            else:
                b.writer = rec
                b.readers = []
        self.recs[key].append(rec)
        return rec

    def barrier(self, keys=ENGS):
        dma_lasts = [v[-1] for v in self.stream_recs.values() if v]
        arr = {}
        for k in keys:
            rec = Rec(k, lambda eng: eng.nop(nofuse=True))
            rec.gidx = self.gctr
            self.gctr += 1
            rec.need_inc = True
            if self.recs[k]:
                prev = self.recs[k][-1]
                prev.need_inc = True
                rec.deps.append(prev)
            self.recs[k].append(rec)
            arr[k] = rec
        col = Rec("Y", lambda eng: eng.nop(nofuse=True))
        col.gidx = self.gctr
        self.gctr += 1
        col.need_inc = True
        for k in keys:
            if k != "Y":
                col.deps.append(arr[k])
        for d in dma_lasts:
            col.deps.append(d)
        self.recs["Y"].append(col)
        for k in keys:
            if k == "Y":
                continue
            rec = Rec(k, lambda eng: eng.nop(nofuse=True))
            rec.gidx = self.gctr
            self.gctr += 1
            rec.deps.append(col)
            self.recs[k].append(rec)

    def finalize_and_replay(self):
        for k in ENGS:
            cnt = 0
            for rec in self.recs[k]:
                if rec.sem is None and rec.need_inc:
                    cnt += 1
                    rec.val = cnt
        self.bumped = {}
        for k in ENGS:
            recs = self.recs[k]
            n = len(recs)
            idx = {id(r): ii for ii, r in enumerate(recs)}
            for ii, r in enumerate(recs):
                if r.sem is not None or not r.need_inc:
                    continue
                v = r.val
                jj = ii + 1
                while jj < n and not recs[jj].deps and recs[jj].sem is None:
                    if recs[jj].need_inc:
                        v = recs[jj].val
                    jj += 1
                self.bumped[id(r)] = v

        def mk_body(key):
            def body(eng):
                watermark = {}
                for rec in self.recs[key]:
                    waits = {}
                    for dep in rec.deps:
                        if dep.val is None:
                            continue
                        if dep.sem is not None:
                            sem, nm = self.dsems[dep.sem][0], "d_" + dep.sem
                            srecs = self.stream_recs[dep.sem]
                            n = 0
                            for r2 in srecs:
                                if r2.gidx < rec.gidx:
                                    n += 1
                                else:
                                    break
                            val = max(dep.val, 16 * n)
                        else:
                            sem, nm = self.esem[dep.key], "e_" + dep.key
                            val = self.bumped.get(id(dep), dep.val)
                        if waits.get(nm, (None, -1))[1] < val:
                            waits[nm] = (sem, val)
                    todo = []
                    for nm, (sem, v) in sorted(waits.items()):
                        if watermark.get(nm, -1) >= v:
                            continue
                        todo.append((sem, v))
                        watermark[nm] = v
                    for sem, v in todo[:-1]:
                        eng.wait_ge(sem, v)
                    inst = rec.fn(eng)
                    if todo:
                        sem, v = todo[-1]
                        if inst is not None:
                            inst._wait_ge(sem, v)
                        else:
                            eng.wait_ge(sem, v)
                    if rec.need_inc and inst is not None:
                        if rec.sem is not None:
                            inst.then_inc(self.dsems[rec.sem][0], 16)
                        else:
                            inst.then_inc(self.esem[rec.key], 1)
            return body

        with self.nc.Block() as block:
            block.tensor(mk_body("T"))
            block.vector(mk_body("V"))
            block.scalar(mk_body("S"))
            block.gpsimd(mk_body("P"))
            block.sync(mk_body("Y"))


class Arena:
    def __init__(self, nc, name, cols, dtype):
        self.t = nc.alloc_sbuf_tensor(name, [128, cols], dtype)
        self.free = [(0, cols)]
        self.live = {}

    def alloc(self, name, cols, align=32):
        for idx, (st, sz) in enumerate(self.free):
            a = (st + align - 1) // align * align
            if a + cols <= st + sz:
                repl = []
                if a - st > 0:
                    repl.append((st, a - st))
                if (st + sz) - (a + cols) > 0:
                    repl.append((a + cols, (st + sz) - (a + cols)))
                self.free[idx:idx + 1] = repl
                self.live[name] = (a, cols)
                return self.t[:, a:a + cols]
        raise RuntimeError(f"arena OOM: {name} needs {cols}, free={self.free}")

    def release(self, *names):
        for name in names:
            st, sz = self.live.pop(name)
            self.free.append((st, sz))
        self.free.sort()
        out = []
        for seg in self.free:
            if out and out[-1][0] + out[-1][1] == seg[0]:
                out[-1] = (out[-1][0], out[-1][1] + seg[1])
            else:
                out.append(seg)
        self.free = out


def build_nc():
    nc = bass.Bass("TRN2", target_bir_lowering=False)

    def din(name, shape, dtype=F32):
        return nc.dram_tensor(name, list(shape), dtype, kind="ExternalInput").ap()

    x = din("x", [T, D])
    own_x = din("own_x", [NSLOT * 128, D])
    wdown = din("wdown", [D, 2 * LAT])
    bdown = din("bdown", [128, 8])
    wukv = din("wukv", [LAT, 2 * D])
    wuq = din("wuq", [LAT, D])
    buq = din("buq", [128, 16])
    wkr = din("wkr", [D, DHR])
    bkr = din("bkr", [DHR])
    wqr = din("wqr", [D, NH * DHR])
    bqr = din("bqr", [DHR, 16])
    wout = din("wout", [D, D])
    wupT = din("wupT", [D, DE])
    wgateT = din("wgateT", [D, DE])
    wdownT = din("wdownT", [DE, D])
    wupW = din("wupW", [16384, 1024])
    wgtW = din("wgtW", [16384, 1024])
    wdnW = din("wdnW", [16384, 1024])
    identf = din("identf", [128, 128])
    pswapf = din("pswapf", [DHR, DHR])
    ropekc = din("ropekc", [DHR, T])
    ropeks = din("ropeks", [DHR, T])
    ropeqc = din("ropeqc", [NH * DHR, 1024])
    ropeqs = din("ropeqs", [NH * DHR, 1024])
    maskadd = din("maskadd", [NSLOT * 2 * 128, 512], BF16)
    vbias_in = din("vbias_in", [128, D])
    woutb_in = din("woutb_in", [128, D])

    out_own = nc.dram_tensor("out_own", [NSLOT * 128, D], F32, kind="ExternalOutput").ap()
    dbg_scal = nc.dram_tensor("dbg_scal", [128, 12], F32, kind="ExternalOutput").ap() if DEBUG_SCAL else None
    dbg_q = nc.dram_tensor("dbg_q", [128, 3 * 2048], F32, kind="ExternalOutput").ap() if DEBUG_SCAL else None

    kscr = nc.dram_tensor("kscr", [NH * 128 + DHR, T], BF16, kind="Internal").ap()
    vscr = nc.dram_tensor("vscr", [T, NH * 129], BF16, kind="Internal").ap()
    oscr = nc.dram_tensor("oscr", [NSLOT * 128, D], BF16, kind="Internal").ap()
    qmscr = nc.dram_tensor("qmscr", [NH * 128, NSLOT * 128], BF16, kind="Internal").ap()
    qrscr = nc.dram_tensor("qrscr", [NH * (DHR + 1), NSLOT * 128], BF16, kind="Internal").ap()
    yscr = nc.dram_tensor("yscr", [NSLOT * 128, D], F32, kind="Internal").ap()

    p = Prog(nc)
    a16 = Arena(nc, "a16", 58624, BF16)
    a32 = Arena(nc, "a32", 20480, F32)

    ident_sb = nc.alloc_sbuf_tensor("ident_sb", [128, 128], BF16).ap()
    identf_sb = nc.alloc_sbuf_tensor("identf_sb", [128, 128], F32).ap()
    pswap_sb = nc.alloc_sbuf_tensor("pswap_sb", [DHR, DHR], BF16).ap()
    ones_sb = nc.alloc_sbuf_tensor("ones_sb", [128, 128], F32).ap()
    bdown_sb = nc.alloc_sbuf_tensor("bdown_sb", [128, 8], F32).ap()
    buq_sb = nc.alloc_sbuf_tensor("buq_sb", [128, 16], F32).ap()
    bkr_sb = nc.alloc_sbuf_tensor("bkr_sb", [DHR, 1], F32).ap()
    bqr_sb = nc.alloc_sbuf_tensor("bqr_sb", [DHR, 16], F32).ap()
    kr0_sb = nc.alloc_sbuf_tensor("kr0_sb", [DHR + 1, 128], BF16).ap()
    krext_bld = nc.alloc_sbuf_tensor("krext_bld", [DHR + 1, 512], BF16).ap()
    krext_st = [nc.alloc_sbuf_tensor(f"krext_st{i}", [DHR + 1, 512], BF16).ap() for i in range(2)]
    krraw_sb2 = [nc.alloc_sbuf_tensor(f"krraw{i}_sb", [DHR, 512], BF16).ap() for i in range(2)]
    krt1_sb2 = [nc.alloc_sbuf_tensor(f"krt1{i}_sb", [DHR, 512], F32).ap() for i in range(2)]
    stats_sb = nc.alloc_sbuf_tensor("stats_sb", [128, 32], F32).ap()
    lcols_sb = nc.alloc_sbuf_tensor("lcols_sb", [128, 16], F32).ap()
    linv_sb = nc.alloc_sbuf_tensor("linv_sb", [128, 16], F32).ap()
    s0all_sb = nc.alloc_sbuf_tensor("s0all_sb", [128, NSLOT * 16], F32).ap()
    scal_sb = nc.alloc_sbuf_tensor("scal_sb", [128, 12], F32).ap()
    eps_sb = nc.alloc_sbuf_tensor("eps_sb", [128, 1], F32).ap()

    PS = nc.alloc_psum_tensor("PS", [128, 4096], F32).ap()
    ps_a = PS[:, 0:512]
    ps_b = PS[:, 512:1024]
    ps_kr = PS[0:DHR, 1024:1536]
    ps_t = PS[:, 1536:2048].bitcast(BF16)[:, 0:512]
    ps_o = PS[:, 0:3072]
    ps_s = [PS[:, 3072:3584], PS[:, 3584:4096]]

    bb = {}

    def buf(name, excl=False):
        if name not in bb:
            bb[name] = Buf(name, excl)
        return bb[name]

    b_ps_a = buf("ps_a", True)
    b_ps_b = buf("ps_b", True)
    b_ps_t = buf("ps_t", True)
    b_ps_kr = buf("ps_kr", True)
    b_ps_o = buf("ps_o", True)
    b_ps_s = [buf("ps_s0", True), buf("ps_s1", True)]

    p.emit("P", lambda g: g.dma_start(out=ident_sb, in_=identf[:, :]), writes=[buf("ident")], dma_stream="cst")
    p.emit("Y", lambda g: g.dma_start(out=identf_sb, in_=identf[:, :]), writes=[buf("identf")], dma_stream="ycst")
    p.emit("P", lambda g: g.dma_start(out=pswap_sb, in_=pswapf[:, :]), writes=[buf("pswap")], dma_stream="cst")
    p.emit("Y", lambda g: g.dma_start(out=bdown_sb, in_=bdown[:, :]),
           writes=[buf("bdown")], dma_stream="ycst")
    p.emit("Y", lambda g: g.dma_start(out=buq_sb, in_=buq[:, :]),
           writes=[buf("buq")], dma_stream="ycst")
    p.emit("Y", lambda g: g.dma_start(out=bkr_sb, in_=bkr[:, None]), writes=[buf("bkr")], dma_stream="ycst")
    p.emit("Y", lambda g: g.dma_start(out=bqr_sb, in_=bqr[:, :]),
           writes=[buf("bqr")], dma_stream="ycst")
    p.emit("V", lambda v: v.memset(ones_sb, 1.0), writes=[buf("ones")])
    p.emit("V", lambda v: v.memset(eps_sb, LN_EPS), writes=[buf("eps")])
    p.emit("V", lambda v: v.memset(krext_bld[DHR:DHR + 1, :], 1.0), writes=[buf("krextd")])
    for par in range(2):
        p.emit("V", lambda v, par=par: v.memset(krext_st[par][DHR:DHR + 1, :], 1.0),
               writes=[buf(f"krst{par}")])

    wdown_sb = a16.alloc("wdown", 16 * 512)
    wukv_sb = a16.alloc("wukv", 4 * 4096)
    xnT_sb = a16.alloc("xnT", 16 * 512)
    KT_sb = a16.alloc("KT", NH * 512)
    V_sb = a16.alloc("Vb", 4 * 2064)
    xntok_sb = a16.alloc("xntok", D)
    ckvT_sb = a16.alloc("ckvT", 4 * 512)
    wkr_sb = a16.alloc("wkr", 16 * DHR)
    junk_sb = a16.alloc("junk", 2048)
    k0_sb = a16.alloc("k0", NH * 128)
    xin_sb = a32.alloc("xin", 2 * D)
    mean_sb = a32.alloc("meanbuf", 8 * 1024)
    hcoll_sb = a32.alloc("hcoll", 128)
    ht_sb = a32.alloc("ht", 128)
    h2_sb = a32.alloc("h2", 1024)
    l2r_sb = a32.alloc("l2r", 64)
    brow_sb = a32.alloc("brow", 1024)[0:NH, :]
    ropek_sb = a32.alloc("ropek", 1024)[0:DHR, :]
    ropeq_sb = a32.alloc("ropeq", 1024)[0:DHR, :]
    ropeq2_sb = a32.alloc("ropeq2", 1024)[0:DHR, :]

    p.emit("P", lambda g: g.dma_start(out=wdown_sb.rearrange("p (c l) -> p c l", l=512),
                                      in_=wdown[:, 0:512].rearrange("(c p) l -> p c l", p=128)),
           writes=[buf("wdown")], dma_stream="wA")
    p.emit("P", lambda g: g.dma_start(out=wukv_sb.rearrange("p (c l) -> p c l", l=4096),
                                      in_=wukv.rearrange("(c p) l -> p c l", p=128)),
           writes=[buf("wukv")], dma_stream="wA")
    p.emit("P", lambda g: g.dma_start(out=wkr_sb.rearrange("p (c l) -> p c l", l=DHR),
                                      in_=wkr.rearrange("(c p) l -> p c l", p=128)),
           writes=[buf("wkr")], dma_stream="wA")
    for s in range(4):
        p.emit("V", lambda v, s=s: v.memset(V_sb[:, 2064 * s + 128:2064 * (s + 1):129], 1.0),
               writes=[buf(f"Vb{s}")])

    def ln_stats(src_ap, b_src, slot):
        base = 8 * slot
        s1 = stats_sb[:, base:base + 1]
        s2 = stats_sb[:, base + 1:base + 2]
        bs = buf(f"lnstats{slot}")
        p.emit("V", lambda v: v.reduce_sum(s1, src_ap, AX.X), reads=[b_src], writes=[bs])
        p.emit("S", lambda sc: sc.activation(junk_sb, src_ap, AF.Square, accum_out=s2),
               reads=[b_src], writes=[bs, buf("junk")])

    def ln_math(slot):
        base = 8 * slot
        s1 = stats_sb[:, base:base + 1]
        s2 = stats_sb[:, base + 1:base + 2]
        mm = stats_sb[:, base + 2:base + 3]
        tmp = stats_sb[:, base + 3:base + 4]
        var = stats_sb[:, base + 4:base + 5]
        inv = stats_sb[:, base + 5:base + 6]
        ngm = stats_sb[:, base + 6:base + 7]
        bs = buf(f"lnstats{slot}")
        p.emit("V", lambda v: v.tensor_scalar(mm, s1, 1.0 / D, None, ALU.mult), reads=[bs], writes=[bs])
        p.emit("V", lambda v: v.tensor_tensor(tmp, mm, mm, ALU.mult), reads=[bs], writes=[bs])
        p.emit("V", lambda v: v.scalar_tensor_tensor(var, s2, 1.0 / D, tmp, ALU.mult, ALU.subtract),
               reads=[bs], writes=[bs])
        p.emit("S", lambda sc: sc.activation(tmp, var, AF.Ln, bias=eps_sb), reads=[bs, buf("eps")], writes=[bs])
        p.emit("S", lambda sc: sc.activation(inv, tmp, AF.Exp, scale=-0.5), reads=[bs], writes=[bs])
        p.emit("V", lambda v: v.scalar_tensor_tensor(ngm, mm, -1.0, inv, ALU.mult, ALU.mult),
               reads=[bs], writes=[bs])

    def ln_apply(src_ap, b_src, dst_ap, b_dst, slot):
        base = 8 * slot
        inv = stats_sb[:, base + 5:base + 6]
        ngm = stats_sb[:, base + 6:base + 7]
        bs = buf(f"lnstats{slot}")
        p.emit("S", lambda sc: sc.activation(dst_ap, src_ap, AF.Identity, bias=ngm, scale=inv),
               reads=[b_src, bs], writes=[b_dst])

    def ln_tile(src_ap, b_src, dst_ap=None, b_dst=None, out_f32=None, b_outf=None, slot=0):
        ln_stats(src_ap, b_src, slot)
        ln_math(slot)
        if dst_ap is not None:
            ln_apply(src_ap, b_src, dst_ap, b_dst, slot)
        if out_f32 is not None:
            ln_apply(src_ap, b_src, out_f32, b_outf, slot)

    def transpose_2048(src_ap, b_src, dst_ap_fn, b_dst):
        for g in range(4):
            for c in range(4):
                cc = 4 * g + c
                p.emit("T", lambda t, cc=cc, c=c: t.transpose(
                    ps_t[:, 128 * c:128 * (c + 1)], src_ap[:, 128 * cc:128 * (cc + 1)], ident_sb),
                    reads=[b_src, buf("ident")], writes=[b_ps_t])
            p.emit("V", lambda v, g=g: v.tensor_copy(
                dst_ap_fn(g), ps_t.rearrange("p (c q) -> p c q", q=128)),
                reads=[b_ps_t], writes=[b_dst])

    def rope_combine(par, psum2, b_psum2, bias_ap, b_bias, cos_ap, sin_ap, b_tab, out_ap, b_out):
        kr_raw, kr_t1 = krraw_sb2[par], krt1_sb2[par]
        braw, bt1 = buf(f"krraw{par}"), buf(f"krt1{par}")
        p.emit("S", lambda sc: sc.activation(kr_raw, psum2, AF.Identity, bias=bias_ap),
               reads=[b_psum2, b_bias], writes=[braw])
        p.emit("T", lambda t: t.matmul(psum2, pswap_sb, kr_raw, start=True, stop=True),
               reads=[buf("pswap"), braw], writes=[b_psum2])
        p.emit("V", lambda v: v.tensor_tensor(kr_t1, kr_raw, cos_ap, ALU.mult),
               reads=[braw, b_tab], writes=[bt1])
        p.emit("V", lambda v: v.tensor_tensor(psum2, psum2, sin_ap, ALU.mult),
               reads=[b_psum2, b_tab], writes=[b_psum2])
        p.emit("V", lambda v: v.tensor_tensor(out_ap, kr_t1, psum2, ALU.add),
               reads=[bt1, b_psum2], writes=[b_out])

    mean_jobs = [(which, w_ap, t) for which, w_ap in ((0, wupW), (1, wgtW), (2, wdnW))
                 for t in range(128)]
    mean_idx = [0]
    SCOL = {0: (0, 1), 1: (2, 3), 2: (4, 5)}

    def finalize_mean(which):
        bh = buf("hcoll")
        p.emit("T", lambda t: t.transpose(PS[:, 0:128], hcoll_sb, identf_sb),
               reads=[bh, buf("identf")], writes=[b_ps_a, b_ps_o])
        p.emit("V", lambda v: v.tensor_copy(ht_sb, PS[:, 0:128]),
               reads=[b_ps_a, b_ps_o], writes=[buf("ht")])
        for w2 in range(16):
            p.emit("P", lambda g, w2=w2: g.dma_start(
                out=h2_sb[w2:w2 + 1, :], in_=ht_sb[8 * w2:8 * w2 + 8, :]),
                reads=[buf("ht")], writes=[buf("h2")], dma_stream="mfl")
        p.emit("S", lambda sc: sc.activation(junk_sb[0:16, 0:1024], h2_sb[0:16, :],
                                             AF.Abs, accum_out=l2r_sb[0:16, 0:1]),
               reads=[buf("h2")], writes=[buf("l2c"), buf("junk")])
        p.emit("P", lambda g: g.dma_start(out=l2r_sb[0:1, 16:32], in_=l2r_sb[0:16, 0:1]),
               reads=[buf("l2c")], writes=[buf("l2r")], dma_stream="mfl")
        p.emit("V", lambda v: v.tensor_tensor(l2r_sb[0:1, 32:40], l2r_sb[0:1, 16:24],
                                              l2r_sb[0:1, 24:32], ALU.add),
               reads=[buf("l2r")], writes=[buf("l2r")])
        p.emit("V", lambda v: v.tensor_tensor(l2r_sb[0:1, 40:44], l2r_sb[0:1, 32:36],
                                              l2r_sb[0:1, 36:40], ALU.add),
               reads=[buf("l2r")], writes=[buf("l2r")])
        p.emit("V", lambda v: v.tensor_tensor(l2r_sb[0:1, 44:46], l2r_sb[0:1, 40:42],
                                              l2r_sb[0:1, 42:44], ALU.add),
               reads=[buf("l2r")], writes=[buf("l2r")])
        p.emit("V", lambda v: v.tensor_tensor(l2r_sb[0:1, 46:47], l2r_sb[0:1, 44:45],
                                              l2r_sb[0:1, 45:46], ALU.add),
               reads=[buf("l2r")], writes=[buf("l2r")])
        p.emit("V", lambda v: v.tensor_scalar(l2r_sb[0:1, 47:48], l2r_sb[0:1, 46:47],
                                              float(np.float32(1.0 / 16777216.0)),
                                              float(np.float32(1.0e-8)), ALU.mult, ALU.add),
               reads=[buf("l2r")], writes=[buf("l2r")])
        p.emit("V", lambda v: v.reciprocal(l2r_sb[0:1, 48:49], l2r_sb[0:1, 47:48]),
               reads=[buf("l2r")], writes=[buf("l2r")])
        cs, cr = SCOL[which]
        p.emit("T", lambda t: t.matmul(PS[:, 0:2], ones_sb[0:1, :],
                                       l2r_sb[0:1, 47:49], start=True, stop=True),
               reads=[buf("l2r"), buf("ones")], writes=[b_ps_a, b_ps_o])
        p.emit("V", lambda v, cs=cs, cr=cr: v.tensor_copy(
            scal_sb[:, cs:cs + 1], PS[:, 0:1]), reads=[b_ps_a, b_ps_o], writes=[buf("scal")])
        p.emit("V", lambda v, cs=cs, cr=cr: v.tensor_copy(
            scal_sb[:, cr:cr + 1], PS[:, 1:2]), reads=[b_ps_a, b_ps_o], writes=[buf("scal")])

    def pump_mean(n, fin_sink=None):
        for _ in range(n):
            if fin_sink:
                return
            if mean_idx[0] >= len(mean_jobs):
                return
            which, w_ap, t = mean_jobs[mean_idx[0]]
            mean_idx[0] += 1
            par = mean_idx[0] % 8
            mb = mean_sb[:, 1024 * par:1024 * (par + 1)]
            bmb = buf(f"meanbuf{par}")
            p.emit("Y", lambda y, w_ap=w_ap, t=t, mb=mb: y.dma_start(
                out=mb, in_=w_ap[128 * t:128 * (t + 1), :]),
                writes=[bmb], dma_stream=f"mean{par}")
            p.emit("S", lambda sc, t=t, mb=mb: sc.activation(
                junk_sb[:, 0:1024], mb, AF.Abs, accum_out=hcoll_sb[:, t:t + 1]),
                reads=[bmb], writes=[buf("hcoll"), buf("junk")])
            if t == 127:
                if fin_sink is not None:
                    fin_sink.append(which)
                else:
                    finalize_mean(which)

    def phase_a():
        b_rk = buf("ropek")
        for kt in range(NKT):
            p.emit("Y", lambda y, kt=kt: y.dma_start(
                out=ropek_sb[:, 0:512], in_=ropekc[:, 512 * kt:512 * (kt + 1)]),
                writes=[b_rk], dma_stream="rk")
            p.emit("Y", lambda y, kt=kt: y.dma_start(
                out=ropek_sb[:, 512:1024], in_=ropeks[:, 512 * kt:512 * (kt + 1)]),
                writes=[b_rk], dma_stream="rk")
            for sub in range(4):
                par = sub % 2
                xb = buf(f"xin{par}")
                xa = xin_sb[:, D * par:D * (par + 1)]
                row0 = 512 * kt + 128 * sub
                p.emit("Y", lambda y, row0=row0, xa=xa: y.dma_start(
                    out=xa, in_=x[row0:row0 + 128, :]), writes=[xb], dma_stream=f"xA{par}")
                ln_stats(xa, xb, sub % 2)
                ln_math(sub % 2)
                ln_apply(xa, xb, xntok_sb, buf("xntok"), sub % 2)

                def dst_fn(g, sub=sub):
                    return xnT_sb.rearrange("p (c q) -> p c q", q=512)[:, 4 * g:4 * g + 4,
                                                                      128 * sub:128 * (sub + 1)]
                transpose_2048(xntok_sb, buf("xntok"), dst_fn, buf("xnT"))
            for lc in range(4):
                pp, bpp = (ps_a, b_ps_a) if lc % 2 == 0 else (ps_b, b_ps_b)
                for dc in range(16):
                    p.emit("T", lambda t, lc=lc, dc=dc, pp=pp: t.matmul(
                        pp, wdown_sb[:, 512 * dc + 128 * lc:512 * dc + 128 * (lc + 1)],
                        xnT_sb[:, 512 * dc:512 * (dc + 1)],
                        start=(dc == 0), stop=(dc == 15)),
                        reads=[buf("wdown"), buf("xnT")], writes=[bpp])
                p.emit("S", lambda s, lc=lc, pp=pp: s.activation(
                    ckvT_sb[:, 512 * lc:512 * (lc + 1)], pp, AF.Identity,
                    bias=bdown_sb[:, lc:lc + 1]),
                    reads=[bpp, buf("bdown")], writes=[buf("ckvT")])
            for h in range(NH):
                pp, bpp = (ps_a, b_ps_a) if h % 2 == 0 else (ps_b, b_ps_b)
                for rc in range(4):
                    p.emit("T", lambda t, h=h, rc=rc, pp=pp: t.matmul(
                        pp, wukv_sb[:, 4096 * rc + 128 * h:4096 * rc + 128 * (h + 1)],
                        ckvT_sb[:, 512 * rc:512 * (rc + 1)],
                        start=(rc == 0), stop=(rc == 3)),
                        reads=[buf("wukv"), buf("ckvT")], writes=[bpp])
                p.emit("S", lambda s, h=h, pp=pp: s.copy(KT_sb[:, 512 * h:512 * (h + 1)], pp),
                       reads=[bpp], writes=[buf(f"KT{h}")])
                p.emit("P", lambda g, kt=kt, h=h: g.dma_start(
                    out=kscr[0:NH * 128, :].rearrange("(h p) c -> p h c", p=128)
                    [:, h:h + 1, 512 * kt:512 * (kt + 1)],
                    in_=KT_sb.rearrange("p (h c) -> p h c", c=512)[:, h:h + 1, :]),
                    reads=[buf(f"KT{h}")], dma_stream=f"spK{h % 2}")
            for sub in range(4):
                for q4 in range(4):
                    pp, bpp = (ps_a, b_ps_a) if (sub * 4 + q4) % 2 == 0 else (ps_b, b_ps_b)
                    for rc in range(4):
                        p.emit("T", lambda t, sub=sub, q4=q4, rc=rc, pp=pp: t.matmul(
                            pp, ckvT_sb[:, 512 * rc + 128 * sub:512 * rc + 128 * (sub + 1)],
                            wukv_sb[:, 4096 * rc + 2048 + 512 * q4:4096 * rc + 2048 + 512 * (q4 + 1)],
                            start=(rc == 0), stop=(rc == 3)),
                            reads=[buf("wukv"), buf("ckvT")], writes=[bpp])
                    p.emit("V", lambda v, sub=sub, q4=q4, pp=pp: v.tensor_copy(
                        V_sb.rearrange("p (s c) -> p s c", c=2064)[:, sub:sub + 1, :]
                        .rearrange("p s (h c) -> p (s h) c", c=129)[:, 4 * q4:4 * (q4 + 1), 0:128],
                        pp.rearrange("p (h c) -> p h c", c=128)),
                        reads=[bpp], writes=[buf(f"Vb{sub}")])
                p.emit("P", lambda g, kt=kt, sub=sub: g.dma_start(
                    out=vscr.rearrange("(n p) c -> p n c", p=128)
                    [:, 4 * kt + sub:4 * kt + sub + 1, :],
                    in_=V_sb.rearrange("p (s c) -> p s c", c=2064)[:, sub:sub + 1, :]),
                    reads=[buf(f"Vb{sub}")], dma_stream=f"spV{sub % 2}")
            for dc in range(16):
                p.emit("T", lambda t, dc=dc: t.matmul(
                    ps_kr, wkr_sb[:, DHR * dc:DHR * (dc + 1)], xnT_sb[:, 512 * dc:512 * (dc + 1)],
                    start=(dc == 0), stop=(dc == 15)),
                    reads=[buf("wkr"), buf("xnT")], writes=[b_ps_kr])
            rope_combine(kt % 2, ps_kr, b_ps_kr, bkr_sb, buf("bkr"),
                         ropek_sb[:, 0:512], ropek_sb[:, 512:1024], b_rk,
                         krext_bld[0:DHR, :], buf("krextd"))
            if kt == 0:
                p.emit("V", lambda v: v.tensor_copy(
                    k0_sb.rearrange("p (h c) -> p h c", c=128),
                    KT_sb.rearrange("p (h c) -> p h c", c=512)[:, :, 0:128]),
                    reads=[buf(f"KT{h}") for h in range(NH)], writes=[buf("k0")])
                p.emit("V", lambda v: v.tensor_copy(kr0_sb[0:DHR, :], krext_bld[0:DHR, 0:128]),
                       reads=[buf("krextd")], writes=[buf("kr0")])
                p.emit("V", lambda v: v.memset(kr0_sb[DHR:DHR + 1, :], 1.0), writes=[buf("kr0")])
            p.emit("P", lambda g, kt=kt: g.dma_start(
                out=kscr[NH * 128:NH * 128 + DHR, 512 * kt:512 * (kt + 1)],
                in_=krext_bld[0:DHR, :]),
                reads=[buf("krextd")], dma_stream="spK0")
            pump_mean(12)

    def phase_a2():
        a16.release("wdown", "wukv", "xnT", "KT", "Vb", "ckvT", "xntok", "wkr")
        ownxnT = a16.alloc("ownxnT", 16 * 1024)
        qtok = a16.alloc("qtok", D)
        b_own = buf("ownxnT")
        for i in range(NSLOT):
            par = i % 2
            xb = buf(f"xin{par}")
            xa = xin_sb[:, D * par:D * (par + 1)]
            p.emit("Y", lambda y, i=i, xa=xa: y.dma_start(
                out=xa, in_=own_x[128 * i:128 * (i + 1), :]), writes=[xb], dma_stream=f"xq{par}")
            ln_tile(xa, xb, dst_ap=qtok, b_dst=buf("qtok"), slot=i % 2)

            def dst_fn(g, i=i):
                return ownxnT.rearrange("p (c q) -> p c q", q=1024)[:, 4 * g:4 * g + 4,
                                                                   128 * i:128 * (i + 1)]
            transpose_2048(qtok, buf("qtok"), dst_fn, b_own)
        wqr_sb = a16.alloc("wqr", 16 * 1024)
        qrope = a16.alloc("qrope", NH * 1024)
        p.emit("P", lambda g: g.dma_start(out=wqr_sb.rearrange("p (c l) -> p c l", l=1024),
                                          in_=wqr.rearrange("(c p) l -> p c l", p=128)),
               writes=[buf("wqr")], dma_stream="wA")
        rq2 = [buf("ropeq0"), buf("ropeq1")]
        for h in range(NH):
            for th in range(2):
                par = (h * 2 + th) % 2
                pk = ps_kr if par == 0 else ps_a[0:DHR, :]
                bpk = b_ps_kr if par == 0 else b_ps_a
                rqa = ropeq_sb[:, 1024 * 0 + 512 * par:512 * par + 512] if False else None
                ca = ropeq_sb[:, 512 * par:512 * (par + 1)]
                p.emit("Y", lambda y, h=h, th=th, par=par: y.dma_start(
                    out=ropeq_sb[:, 512 * par:512 * (par + 1)].rearrange("p c -> p c"),
                    in_=ropeqc[DHR * h:DHR * (h + 1), 512 * th:512 * (th + 1)]),
                    writes=[rq2[par]], dma_stream=f"rq{par}")
                p.emit("Y", lambda y, h=h, th=th, par=par: y.dma_start(
                    out=ropeq2_sb[:, 512 * par:512 * (par + 1)],
                    in_=ropeqs[DHR * h:DHR * (h + 1), 512 * th:512 * (th + 1)]),
                    writes=[rq2[par]], dma_stream=f"rq{par}")
                for dc in range(16):
                    p.emit("T", lambda t, h=h, th=th, dc=dc, pk=pk: t.matmul(
                        pk, wqr_sb[:, 1024 * dc + DHR * h:1024 * dc + DHR * (h + 1)],
                        ownxnT[:, 1024 * dc + 512 * th:1024 * dc + 512 * (th + 1)],
                        start=(dc == 0), stop=(dc == 15)),
                        reads=[buf("wqr"), b_own], writes=[bpk])
                rope_combine(par, pk, bpk, bqr_sb[:, h:h + 1], buf("bqr"),
                             ropeq_sb[:, 512 * par:512 * (par + 1)],
                             ropeq2_sb[:, 512 * par:512 * (par + 1)], rq2[par],
                             qrope[0:DHR, 1024 * h + 512 * th:1024 * h + 512 * (th + 1)],
                             buf("qrope"))
        p.barrier()
        a16.release("wqr", "qtok")
        wdq_sb = a16.alloc("wdownq", 16 * 512)
        cqT_sb = a16.alloc("cqT", 4 * 1024)
        p.emit("P", lambda g: g.dma_start(out=wdq_sb.rearrange("p (c l) -> p c l", l=512),
                                          in_=wdown[:, 512:1024].rearrange("(c p) l -> p c l", p=128)),
               writes=[buf("wdownq")], dma_stream="wA")
        for lc in range(4):
            for th in range(2):
                pp, bpp = (ps_a, b_ps_a) if (lc * 2 + th) % 2 == 0 else (ps_b, b_ps_b)
                for dc in range(16):
                    p.emit("T", lambda t, lc=lc, th=th, dc=dc, pp=pp: t.matmul(
                        pp, wdq_sb[:, 512 * dc + 128 * lc:512 * dc + 128 * (lc + 1)],
                        ownxnT[:, 1024 * dc + 512 * th:1024 * dc + 512 * (th + 1)],
                        start=(dc == 0), stop=(dc == 15)),
                        reads=[buf("wdownq"), b_own], writes=[bpp])
                p.emit("S", lambda s, lc=lc, th=th, pp=pp: s.activation(
                    cqT_sb[:, 1024 * lc + 512 * th:1024 * lc + 512 * (th + 1)], pp,
                    AF.Identity, bias=bdown_sb[:, 4 + lc:5 + lc]),
                    reads=[bpp, buf("bdown")], writes=[buf("cqT")])
        p.barrier()
        a16.release("ownxnT", "wdownq")
        wuq_sb = a16.alloc("wuq", 4 * 2048)
        qmainT = a16.alloc("qmainT", NH * 1024)
        p.emit("P", lambda g: g.dma_start(out=wuq_sb.rearrange("p (c l) -> p c l", l=2048),
                                          in_=wuq.rearrange("(c p) l -> p c l", p=128)),
               writes=[buf("wuq")], dma_stream="wA")
        for h in range(NH):
            for th in range(2):
                pp, bpp = (ps_a, b_ps_a) if (h * 2 + th) % 2 == 0 else (ps_b, b_ps_b)
                for rc in range(4):
                    p.emit("T", lambda t, h=h, th=th, rc=rc, pp=pp: t.matmul(
                        pp, wuq_sb[:, 2048 * rc + 128 * h:2048 * rc + 128 * (h + 1)],
                        cqT_sb[:, 1024 * rc + 512 * th:1024 * rc + 512 * (th + 1)],
                        start=(rc == 0), stop=(rc == 3)),
                        reads=[buf("wuq"), buf("cqT")], writes=[bpp])
                p.emit("S", lambda s, h=h, th=th, pp=pp: s.activation(
                    qmainT[:, 1024 * h + 512 * th:1024 * h + 512 * (th + 1)], pp,
                    AF.Identity, bias=buq_sb[:, h:h + 1]),
                    reads=[bpp, buf("buq")], writes=[buf("qmainT")])
        p.barrier()
        a16.release("wuq", "cqT")
        a32.release("ropek", "ropeq", "ropeq2")
        return qmainT, qrope

    def phase_a3(qmainT, qrope):
        for i in range(NSLOT):
            for h in range(NH):
                pp, bpp = ps_s[h % 2], b_ps_s[h % 2]
                p.emit("T", lambda t, i=i, h=h, pp=pp: t.matmul(
                    pp[:, 0:128], qmainT[:, 1024 * h + 128 * i:1024 * h + 128 * (i + 1)],
                    k0_sb[:, 128 * h:128 * (h + 1)], start=True, stop=False),
                    reads=[buf("qmainT"), buf("k0")], writes=[bpp])
                p.emit("T", lambda t, i=i, h=h, pp=pp: t.matmul(
                    pp[:, 0:128], qrope[0:DHR, 1024 * h + 128 * i:1024 * h + 128 * (i + 1)],
                    kr0_sb[0:DHR, :], start=False, stop=True),
                    reads=[buf("qrope"), buf("kr0")], writes=[bpp])
                p.emit("V", lambda v, i=i, h=h, pp=pp: v.reduce_max(
                    s0all_sb[:, 16 * i + h:16 * i + h + 1], pp[:, 0:128], AX.X),
                    reads=[bpp], writes=[buf("s0all")])
        for i in range(NSLOT):
            p.emit("T", lambda t, i=i: t.transpose(
                ps_a[0:16, 0:128], s0all_sb[:, 16 * i:16 * (i + 1)], identf_sb),
                reads=[buf("s0all"), buf("identf")], writes=[b_ps_a])
            p.emit("V", lambda v, i=i: v.tensor_scalar(
                brow_sb[:, 128 * i:128 * (i + 1)], ps_a[0:NH, 0:128], BMARGIN, -1.0,
                ALU.add, ALU.mult),
                reads=[b_ps_a], writes=[buf("brow")])
        p.emit("P", lambda g: g.dma_start(
            out=qrope[DHR:DHR + 1, :], in_=brow_sb),
            reads=[buf("brow")], writes=[buf("qrope")], dma_stream="brw")
        p.emit("P", lambda g: g.dma_start(
            out=qmscr.rearrange("(h p) c -> p h c", p=128),
            in_=qmainT.rearrange("p (h c) -> p h c", c=1024)),
            reads=[buf("qmainT")], dma_stream="spQ")
        p.emit("P", lambda g: g.dma_start(
            out=qrscr.rearrange("(h p) c -> p h c", p=DHR + 1),
            in_=qrope[0:DHR + 1, :].rearrange("p (h c) -> p h c", c=1024)),
            reads=[buf("qrope")], dma_stream="spQ")
        a16.release("k0", "qmainT", "qrope")

    def phase_b():
        KTs = [a16.alloc(f"KTs{par}", NH * 512) for par in range(2)]
        Vs = [a16.alloc(f"Vs{par}", 4 * 2064) for par in range(2)]
        msk = a16.alloc("msk", 2 * 1024)
        pT = [a16.alloc(f"pTb{par}", 512) for par in range(2)]
        Osb = a16.alloc("Osb", D)
        qm = [a16.alloc(f"qm{par}", NH * 128) for par in range(2)]
        qr = [a16.alloc(f"qr{par}", NH * 128) for par in range(2)]
        a32.release("brow")
        vbias_sb = a32.alloc("vbias", D)
        p.emit("Y", lambda y: y.dma_start(out=vbias_sb, in_=vbias_in[:, :]),
               writes=[buf("vbias")], dma_stream="ycst")
        visit = 0
        pv_pending = [None]
        fin_pend = []

        def slot_inputs(i):
            mpar = i % 2
            p.emit("Y", lambda y, i=i, mpar=mpar: y.dma_start(
                out=msk[:, 1024 * mpar:1024 * (mpar + 1)].rearrange("p (m c) -> p m c", c=512),
                in_=maskadd[256 * i:256 * (i + 1), :].rearrange("(m p) c -> p m c", p=128)),
                writes=[buf(f"msk{mpar}")], dma_stream=f"mk{mpar}")
            p.emit("Y", lambda y, i=i, mpar=mpar: y.dma_start(
                out=qm[mpar].rearrange("p (h c) -> p h c", c=128),
                in_=qmscr.rearrange("(h p) c -> p h c", p=128)[:, :, 128 * i:128 * (i + 1)]),
                writes=[buf(f"qm{mpar}")], dma_stream=f"qm{mpar}")
            p.emit("Y", lambda y, i=i, mpar=mpar: y.dma_start(
                out=qr[mpar][0:DHR + 1, :].rearrange("p (h c) -> p h c", c=128),
                in_=qrscr.rearrange("(h p) c -> p h c", p=DHR + 1)[:, :, 128 * i:128 * (i + 1)]),
                writes=[buf(f"qr{mpar}")], dma_stream=f"qr{mpar}")

        slot_inputs(0)
        slot_inputs(1)
        for i in range(NSLOT):
            ci = 2 * (i + 1)
            mpar = i % 2
            bmk = buf(f"msk{mpar}")
            bqm = buf(f"qm{mpar}")
            bqr2 = buf(f"qr{mpar}")
            for kt in range(ci):
                par = visit % 2
                bK = buf(f"KTs{par}")
                bV = buf(f"Vs{par}")
                bkrs = buf(f"krst{par}")
                p.emit("Y", lambda y, kt=kt, par=par: y.dma_start(
                    out=KTs[par].rearrange("p (h c) -> p h c", c=512),
                    in_=kscr[0:NH * 128, :].rearrange("(h p) c -> p h c", p=128)[:, :, 512 * kt:512 * (kt + 1)]),
                    writes=[bK], dma_stream=f"kv{par}")
                p.emit("Y", lambda y, kt=kt, par=par: y.dma_start(
                    out=krext_st[par][0:DHR, :],
                    in_=kscr[NH * 128:NH * 128 + DHR, 512 * kt:512 * (kt + 1)]),
                    writes=[bkrs], dma_stream=f"kv{par}")
                p.emit("Y", lambda y, kt=kt, par=par: y.dma_start(
                    out=Vs[par].rearrange("p (s c) -> p s c", c=2064),
                    in_=vscr.rearrange("(n p) c -> p n c", p=128)[:, 4 * kt:4 * (kt + 1), :]),
                    writes=[bV], dma_stream=f"kv{par}")
                masked = kt >= ci - 2
                for h in range(NH):
                    sp, bsp = ps_s[h % 2], b_ps_s[h % 2]
                    ppar = h % 2
                    bpT = buf(f"pT{ppar}")
                    for s in range(4):
                        p.emit("T", lambda t, i=i, h=h, s=s, par=par, sp=sp, mpar=mpar: t.matmul(
                            sp[:, 128 * s:128 * (s + 1)],
                            KTs[par][:, 512 * h + 128 * s:512 * h + 128 * (s + 1)],
                            qm[mpar][:, 128 * h:128 * (h + 1)],
                            start=True, stop=False),
                            reads=[bK, bqm], writes=[bsp])
                        p.emit("T", lambda t, i=i, h=h, s=s, par=par, sp=sp, mpar=mpar: t.matmul(
                            sp[:, 128 * s:128 * (s + 1)],
                            krext_st[par][:, 128 * s:128 * (s + 1)],
                            qr[mpar][0:DHR + 1, 128 * h:128 * (h + 1)],
                            start=False, stop=True),
                            reads=[bkrs, bqr2], writes=[bsp])
                    if masked:
                        mi = kt - (ci - 2)
                        p.emit("V", lambda v, mi=mi, mpar=mpar, sp=sp: v.tensor_tensor(
                            sp, sp, msk[:, 1024 * mpar + 512 * mi:1024 * mpar + 512 * (mi + 1)],
                            ALU.add),
                            reads=[bmk, bsp], writes=[bsp])
                    p.emit("S", lambda s_, ppar=ppar, sp=sp: s_.activation(
                        pT[ppar], sp, AF.Exp, scale=SM_SCALE),
                        reads=[bsp], writes=[bpT])
                    if pv_pending[0] is not None:
                        pv_pending[0]()
                    bnk = h // 3
                    sslot = h % 3
                    ob = 512 * bnk + 129 * sslot
                    last_h = min(3 * bnk + 2, NH - 1)

                    def mk_pv(kt=kt, h=h, ppar=ppar, par=par, ob=ob, bnk=bnk,
                              sslot=sslot, last_h=last_h, ci=ci, bpT=bpT, bV=bV):
                        for s in range(4):
                            first = (kt == 0 and sslot == 0 and s == 0)
                            last = (kt == ci - 1 and h == last_h and s == 3)
                            p.emit("T", lambda t, s=s, ppar=ppar, par=par, h=h, ob=ob,
                                   first=first, last=last: t.matmul(
                                ps_o[:, ob:ob + 129],
                                pT[ppar][:, 128 * s:128 * (s + 1)],
                                Vs[par][:, 2064 * s + 129 * h:2064 * s + 129 * (h + 1)],
                                start=first, stop=last),
                                reads=[bpT, bV], writes=[b_ps_o])
                    pv_pending[0] = mk_pv
                visit += 1
                pump_mean(2, fin_pend)
            if pv_pending[0] is not None:
                pv_pending[0]()
                pv_pending[0] = None
            p.emit("V", lambda v: v.tensor_copy(
                lcols_sb[:, 0:15].rearrange("p (b s) -> p b s", s=3),
                ps_o.rearrange("p (b c) -> p b c", c=512)[:, 0:5, 128:388:129]),
                reads=[b_ps_o], writes=[buf("lcols")])
            p.emit("V", lambda v: v.tensor_copy(lcols_sb[:, 15:16],
                                                ps_o[:, 512 * 5 + 128:512 * 5 + 129]),
                   reads=[b_ps_o], writes=[buf("lcols")])
            p.emit("V", lambda v: v.reciprocal(linv_sb, lcols_sb),
                   reads=[buf("lcols")], writes=[buf("linv")])
            for h in range(NH):
                ob = 512 * (h // 3) + 129 * (h % 3)
                p.emit("V", lambda v, h=h, ob=ob: v.scalar_tensor_tensor(
                    Osb[:, 128 * h:128 * (h + 1)], ps_o[:, ob:ob + 128], linv_sb[:, h:h + 1],
                    vbias_sb[:, 128 * h:128 * (h + 1)], ALU.mult, ALU.add),
                    reads=[b_ps_o, buf("linv"), buf("vbias")], writes=[buf("Osb")])
            p.emit("P", lambda g, i=i: g.dma_start(out=oscr[128 * i:128 * (i + 1), :], in_=Osb),
                   reads=[buf("Osb")], dma_stream="spO")
            for w in fin_pend:
                finalize_mean(w)
            fin_pend.clear()
            if i + 2 < NSLOT:
                slot_inputs(i + 2)
        a16.release("KTs0", "KTs1", "Vs0", "Vs1", "msk", "pTb0", "pTb1", "Osb",
                    "qm0", "qm1", "qr0", "qr1")
        a32.release("vbias")

    def phase_c():
        wout_sb = a16.alloc("wout", 16 * 2048)
        Oin = a16.alloc("Oin", D)
        OT = a16.alloc("OT", D)
        xn2T = a16.alloc("xn2T", 16 * 1024)
        xn2tok = OT
        att_sb = a32.alloc("att", D)
        yt_sb = a32.alloc("yt", D)
        woutb_sb = a32.alloc("woutb", D)
        p.emit("Y", lambda y: y.dma_start(out=woutb_sb, in_=woutb_in[:, :]),
               writes=[buf("woutb")], dma_stream="ycst")
        p.emit("P", lambda g: g.dma_start(out=wout_sb.rearrange("p (c l) -> p c l", l=2048),
                                          in_=wout.rearrange("(c p) l -> p c l", p=128)),
               writes=[buf("wout")], dma_stream="wA")
        for i in range(NSLOT):
            par = i % 2
            bO = buf("Oin")
            p.emit("Y", lambda y, i=i: y.dma_start(
                out=Oin, in_=oscr[128 * i:128 * (i + 1), :]), writes=[bO], dma_stream="oin")

            def dst_fn(g):
                return OT.rearrange("p (c q) -> p c q", q=128)[:, 4 * g:4 * g + 4, :]
            transpose_2048(Oin, bO, dst_fn, buf("OT"))
            for q4 in range(4):
                pp, bpp = (ps_a, b_ps_a) if q4 % 2 == 0 else (ps_b, b_ps_b)
                for dc in range(16):
                    p.emit("T", lambda t, q4=q4, dc=dc, pp=pp: t.matmul(
                        pp, OT[:, 128 * dc:128 * (dc + 1)],
                        wout_sb[:, 2048 * dc + 512 * q4:2048 * dc + 512 * (q4 + 1)],
                        start=(dc == 0), stop=(dc == 15)),
                        reads=[buf("OT"), buf("wout")], writes=[bpp])
                p.emit("V", lambda v, q4=q4, pp=pp: v.tensor_tensor(
                    att_sb[:, 512 * q4:512 * (q4 + 1)], pp,
                    woutb_sb[:, 512 * q4:512 * (q4 + 1)], ALU.add),
                    reads=[bpp, buf("woutb")], writes=[buf("att")])
            xb = buf(f"xin{par}")
            xa = xin_sb[:, D * par:D * (par + 1)]
            p.emit("Y", lambda y, i=i, xa=xa: y.dma_start(
                out=xa, in_=own_x[128 * i:128 * (i + 1), :]), writes=[xb], dma_stream=f"xq{par}")
            ln_tile(att_sb, buf("att"), out_f32=yt_sb, b_outf=buf("yt"), slot=0)
            p.emit("V", lambda v, xa=xa: v.tensor_tensor(yt_sb, yt_sb, xa, ALU.add),
                   reads=[xb, buf("yt")], writes=[buf("yt")])
            p.emit("P", lambda g, i=i: g.dma_start(out=yscr[128 * i:128 * (i + 1), :], in_=yt_sb),
                   reads=[buf("yt")], dma_stream="spY")
            ln_tile(yt_sb, buf("yt"), dst_ap=xn2tok, b_dst=buf("OT"), slot=1)

            def dst2_fn(g, i=i):
                return xn2T.rearrange("p (c q) -> p c q", q=1024)[:, 4 * g:4 * g + 4,
                                                                 128 * i:128 * (i + 1)]
            transpose_2048(xn2tok, buf("OT"), dst2_fn, buf("xn2T"))
            pump_mean(8)
        a16.release("wout", "Oin", "OT")
        a32.release("att", "yt", "woutb")
        return xn2T

    def phase_d(xn2T):
        pump_mean(len(mean_jobs))
        a32.release("xin", "meanbuf", "hcoll", "ht", "h2", "l2r")
        bscal = buf("scal")
        p.emit("V", lambda v: v.tensor_tensor(scal_sb[:, 9:10], scal_sb[:, 2:3],
                                              scal_sb[:, 4:5], ALU.mult),
               reads=[bscal], writes=[bscal])

        if DEBUG_SCAL:
            p.emit("Y", lambda y: y.dma_start(out=dbg_scal[:, :], in_=scal_sb),
                   reads=[buf("scal")], dma_stream="dbg")
        assert mean_idx[0] == len(mean_jobs), "mean jobs must be drained before D"
        a16.release("junk")
        outacc = a32.alloc("outacc", 16 * 1024)
        wst = [a32.alloc(f"wst{j}", 512) for j in range(8)]
        tupAB = [a16.alloc("tupA", 16 * 512), a16.alloc("tupB", 16 * 512)]
        tgt = a16.alloc("tgt", 16 * 512)
        tdn = a16.alloc("tdn", 4 * 2048)
        hid = a16.alloc("hid", 4 * 1024)
        q1 = a16.alloc("q1", 512)
        q2 = a16.alloc("q2", 512)
        silu_sb2 = [a16.alloc(f"silu{j}", 512) for j in range(2)]

        pd = [PS[:, 1024 + 512 * k:1536 + 512 * k] for k in range(6)]
        b_pd = [buf(f"pd{k}", True) for k in range(6)]

        def quant(dst_ap, b_dst, src_ap, b_src, rcol):
            p.emit("V", lambda v: v.tensor_scalar(
                q1, src_ap, scal_sb[:, rcol:rcol + 1], 0.5, ALU.mult, ALU.is_gt),
                reads=[b_src, bscal], writes=[buf("q1")])
            p.emit("V", lambda v: v.tensor_scalar(
                q2, src_ap, scal_sb[:, rcol:rcol + 1], -0.5, ALU.mult, ALU.is_lt),
                reads=[b_src, bscal], writes=[buf("q2")])
            p.emit("V", lambda v: v.tensor_tensor(dst_ap, q1, q2, ALU.subtract),
                   reads=[buf("q1"), buf("q2")], writes=[b_dst])

        st_ctr = [0]

        def quant_tile(j, g):
            st = st_ctr[0] % 8
            st_ctr[0] += 1
            wa, wb = wst[st], buf(f"wst{st}")
            if j < 32:
                wT = wupT if j < 16 else wgateT
                dc = j % 16
                src = lambda: wT[128 * dc:128 * (dc + 1), 512 * g:512 * (g + 1)]
                if j < 16:
                    dst = tupAB[g % 2][:, 512 * dc:512 * (dc + 1)]
                    bdst = buf(f"tup{g % 2}_{dc}")
                    rcol = 1
                else:
                    dst = tgt[:, 512 * dc:512 * (dc + 1)]
                    bdst = buf(f"tgt{dc}")
                    rcol = 3
            else:
                k = j - 32
                rc, s2 = k // 4, k % 4
                src = lambda: wdownT[512 * g + 128 * rc:512 * g + 128 * (rc + 1),
                                     512 * s2:512 * (s2 + 1)]
                dst = tdn[:, 2048 * rc + 512 * s2:2048 * rc + 512 * (s2 + 1)]
                bdst = buf(f"tdn{rc}_{s2}")
                rcol = 5
            p.emit("Y", lambda y: y.dma_start(out=wa, in_=src()),
                   writes=[wb], dma_stream=f"wD{st}")
            quant(dst, bdst, wa, wb, rcol)

        for j in range(48):
            quant_tile(j, 0)
        for g in range(16):
            tup = tupAB[g % 2]
            up_tiles = [buf(f"tup{g % 2}_{dc}") for dc in range(16)]
            gt_tiles = [buf(f"tgt{dc}") for dc in range(16)]
            for hc in range(4):
                for th in range(2):
                    idx = hc * 2 + th
                    sl, bsl = silu_sb2[idx % 2], buf(f"silu{idx % 2}")
                    for dc in range(16):
                        rds = ([buf("xn2T")] + up_tiles) if dc in (0, 15) else []
                        p.emit("T", lambda t, hc=hc, th=th, dc=dc, tup=tup: t.matmul(
                            ps_a, tup[:, 512 * dc + 128 * hc:512 * dc + 128 * (hc + 1)],
                            xn2T[:, 1024 * dc + 512 * th:1024 * dc + 512 * (th + 1)],
                            start=(dc == 0), stop=(dc == 15)),
                            reads=rds, writes=[b_ps_a])
                    for dc in range(16):
                        rds = ([buf("xn2T")] + gt_tiles) if dc in (0, 15) else []
                        p.emit("T", lambda t, hc=hc, th=th, dc=dc: t.matmul(
                            ps_b, tgt[:, 512 * dc + 128 * hc:512 * dc + 128 * (hc + 1)],
                            xn2T[:, 1024 * dc + 512 * th:1024 * dc + 512 * (th + 1)],
                            start=(dc == 0), stop=(dc == 15)),
                            reads=rds, writes=[b_ps_b])
                    p.emit("S", lambda sc, sl=sl: sc.activation(
                        sl, ps_a, AF.Silu, scale=scal_sb[:, 0:1]),
                        reads=[b_ps_a, bscal], writes=[bsl])
                    p.emit("V", lambda v, hc=hc, th=th, sl=sl: v.tensor_tensor(
                        hid[:, 1024 * hc + 512 * th:1024 * hc + 512 * (th + 1)],
                        sl, ps_b, ALU.mult),
                        reads=[bsl, b_ps_b], writes=[buf(f"hid{idx}")])
                    if g + 1 < 16:
                        quant_tile(2 * idx, g + 1)
                        quant_tile(2 * idx + 1, g + 1)
            for dm in range(16):
                for th in range(2):
                    it = dm * 2 + th
                    k = it % 6
                    dn_tiles = ([buf(f"tdn{rc}_{dm // 4}") for rc in range(4)] +
                                [buf(f"hid{rc * 2 + th}") for rc in range(4)])
                    for rc in range(4):
                        rds = dn_tiles if rc in (0, 3) else []
                        p.emit("T", lambda t, dm=dm, th=th, rc=rc, k=k: t.matmul(
                            pd[k], tdn[:, 2048 * rc + 128 * dm:2048 * rc + 128 * (dm + 1)],
                            hid[:, 1024 * rc + 512 * th:1024 * rc + 512 * (th + 1)],
                            start=(rc == 0), stop=(rc == 3)),
                            reads=rds, writes=[b_pd[k]])
                    if th == 1:
                        k2 = (2 * dm) % 6
                        if g == 0:
                            p.emit("V", lambda v, dm=dm, k2=k2: v.tensor_copy(
                                outacc[:, 1024 * dm:1024 * (dm + 1)],
                                PS[:, 1024 + 512 * k2:2048 + 512 * k2]),
                                reads=[b_pd[k2], b_pd[k2 + 1]], writes=[buf("outacc")])
                        else:
                            p.emit("V", lambda v, dm=dm, k2=k2: v.tensor_tensor(
                                outacc[:, 1024 * dm:1024 * (dm + 1)],
                                outacc[:, 1024 * dm:1024 * (dm + 1)],
                                PS[:, 1024 + 512 * k2:2048 + 512 * k2], ALU.add),
                                reads=[b_pd[k2], b_pd[k2 + 1], buf("outacc")],
                                writes=[buf("outacc")])
                    if g + 1 < 16:
                        if it < 16:
                            quant_tile(16 + it, g + 1)
                        if th == 1 and dm % 4 == 3:
                            s2 = dm // 4
                            for rc in range(4):
                                quant_tile(32 + rc * 4 + s2, g + 1)
        a16.release("tupA", "tupB", "tgt", "tdn", "hid", "q1", "q2", "silu0", "silu1", "xn2T")
        a32.release(*[f"wst{j}" for j in range(8)])
        return outacc

    def phase_e(outacc):
        ffn_sb = a32.alloc("ffn", D)
        fout = a32.alloc("fout", D)
        for i in range(NSLOT):
            p.emit("Y", lambda y, i=i: y.dma_start(
                out=fout, in_=yscr[128 * i:128 * (i + 1), :]), writes=[buf("fout")], dma_stream="yin")
            for g in range(4):
                for c in range(4):
                    dc = 4 * g + c
                    p.emit("T", lambda t, dc=dc, c=c, i=i: t.transpose(
                        ps_b[:, 128 * c:128 * (c + 1)],
                        outacc[:, 1024 * dc + 128 * i:1024 * dc + 128 * (i + 1)], identf_sb),
                        reads=[buf("outacc"), buf("identf")], writes=[b_ps_b])
                p.emit("V", lambda v, g=g: v.tensor_scalar(
                    ffn_sb[:, 512 * g:512 * (g + 1)], ps_b, scal_sb[:, 9:10], None, ALU.mult),
                    reads=[b_ps_b, buf("scal")], writes=[buf("ffn")])
            ln_tile(ffn_sb, buf("ffn"), out_f32=ffn_sb, b_outf=buf("ffn"), slot=i % 2)
            p.emit("V", lambda v: v.tensor_tensor(fout, fout, ffn_sb, ALU.add),
                   reads=[buf("ffn"), buf("fout")], writes=[buf("fout")])
            p.emit("Y", lambda y, i=i: y.dma_start(out=out_own[128 * i:128 * (i + 1), :], in_=fout),
                   reads=[buf("fout")], dma_stream="out")
        last_val = p.dsems["out"][1]
        p.recs["Y"].append(Rec("Y", lambda y: y.wait_ge(p.dsems["out"][0], last_val)))

    phase_a()
    if PHASES >= 2:
        p.barrier()
        qmainT, qrope = phase_a2()
        phase_a3(qmainT, qrope)
    if PHASES >= 3:
        p.barrier()
        phase_b()
    if PHASES >= 4:
        p.barrier()
        xn2T = phase_c()
    if PHASES >= 5:
        p.barrier()
        outacc = phase_d(xn2T)
    if PHASES >= 6:
        p.barrier()
        phase_e(outacc)

    p.finalize_and_replay()
    return nc


def _win_rearrange(W):
    R, C = W.shape
    nR, nC = R // 32, C // 32
    A = W.reshape(nR, 32, nC, 32).transpose(0, 2, 1, 3).reshape(nR, nC, 1024)
    B = A.reshape(nR // 32, 32, nC // 32, 32, 1024).transpose(0, 2, 1, 3, 4)
    return np.ascontiguousarray(B.reshape(16384, 1024))


def _rope_tables(n):
    pos = np.repeat(np.arange(n // 2, dtype=np.float64), 2)
    theta = np.exp(-2.0 * pos / n * np.log(10000.0))
    freq = np.arange(T, dtype=np.float64) + 1.0
    ang = freq[:, None] * theta[None, :]
    return np.cos(ang).astype(np.float32), np.sin(ang).astype(np.float32)


def kernel(L0_attn_x, L0_attn_W_down_k, L0_attn_W_down_b, L0_attn_W_uKV_k, L0_attn_W_uKV_b,
           L0_attn_W_uQ_k, L0_attn_W_uQ_b, L0_attn_Wkr_k, L0_attn_Wkr_b, L0_attn_Wqr_k,
           L0_attn_Wqr_b, L0_attn_Wout_k, L0_attn_Wout_b, L0_ffn_W_up, L0_ffn_W_gate,
           L0_ffn_W_down):
    _ensure_device()
    f32 = np.float32
    x = np.ascontiguousarray(np.asarray(L0_attn_x, f32).reshape(T, D))
    wdown = np.ascontiguousarray(np.asarray(L0_attn_W_down_k, f32))
    bdown = np.ascontiguousarray(np.asarray(L0_attn_W_down_b, f32))
    bukv = np.ascontiguousarray(np.asarray(L0_attn_W_uKV_b, f32))
    bout = np.ascontiguousarray(np.asarray(L0_attn_Wout_b, f32))

    identf = np.eye(128, dtype=f32)
    pswap = np.zeros((DHR, DHR), f32)
    for m in range(DHR // 2):
        pswap[2 * m + 1, 2 * m] = -1.0
        pswap[2 * m, 2 * m + 1] = 1.0
    ropekc, ropeks = _rope_tables(DHR)
    ropeqc_full, ropeqs_full = _rope_tables(NH * DHR)

    nc = build_nc()
    common = {
        "x": x, "wdown": wdown,
        "bdown": np.ascontiguousarray(bdown.reshape(8, 128).T),
        "wukv": np.ascontiguousarray(np.asarray(L0_attn_W_uKV_k, f32)),
        "wuq": np.ascontiguousarray(np.asarray(L0_attn_W_uQ_k, f32)),
        "buq": np.ascontiguousarray(np.asarray(L0_attn_W_uQ_b, f32).reshape(16, 128).T),
        "wkr": np.ascontiguousarray(np.asarray(L0_attn_Wkr_k, f32)),
        "bkr": np.ascontiguousarray(np.asarray(L0_attn_Wkr_b, f32)),
        "wqr": np.ascontiguousarray(np.asarray(L0_attn_Wqr_k, f32)),
        "bqr": np.ascontiguousarray(np.asarray(L0_attn_Wqr_b, f32).reshape(16, DHR).T),
        "wout": np.ascontiguousarray(np.asarray(L0_attn_Wout_k, f32)),
        "wupT": np.ascontiguousarray(np.asarray(L0_ffn_W_up, f32).T),
        "wgateT": np.ascontiguousarray(np.asarray(L0_ffn_W_gate, f32).T),
        "wdownT": np.ascontiguousarray(np.asarray(L0_ffn_W_down, f32).T),
        "wupW": _win_rearrange(np.asarray(L0_ffn_W_up, f32)),
        "wgtW": _win_rearrange(np.asarray(L0_ffn_W_gate, f32)),
        "wdnW": _win_rearrange(np.asarray(L0_ffn_W_down, f32)),
        "identf": identf, "pswapf": pswap,
        "ropekc": np.ascontiguousarray(ropekc.T), "ropeks": np.ascontiguousarray(ropeks.T),
        "vbias_in": np.ascontiguousarray(np.broadcast_to(bukv[D:2 * D], (128, D))),
        "woutb_in": np.ascontiguousarray(np.broadcast_to(bout, (128, D))),
    }
    in_maps = []
    for j in range(NCORES):
        tiles = [8 * i + j for i in range(NSLOT)]
        rows = np.concatenate([np.arange(128 * t_, 128 * t_ + 128) for t_ in tiles])
        mk = np.zeros((NSLOT, 2, 128, 512), f32)
        for ii in range(NSLOT):
            t_ = tiles[ii]
            ci = 2 * (ii + 1)
            qglob = 128 * t_ + np.arange(128)
            for m2 in range(2):
                kt = ci - 2 + m2
                for s in range(4):
                    kg = 512 * kt + 128 * s + np.arange(128)[:, None]
                    mk[ii, m2, :, 128 * s:128 * (s + 1)] = np.where(kg <= qglob[None, :], 0.0, NEG)
        im = dict(common)
        im["own_x"] = np.ascontiguousarray(x[rows])
        im["ropeqc"] = np.ascontiguousarray(ropeqc_full[rows].T)
        im["ropeqs"] = np.ascontiguousarray(ropeqs_full[rows].T)
        im["maskadd"] = mk.reshape(NSLOT * 2 * 128, 512).astype(ml_dtypes.bfloat16)
        in_maps.append(im)

    res = run_bass_kernel_spmd(nc, in_maps, core_ids=list(range(NCORES)))
    out = np.zeros((T, D), f32)
    for j in range(NCORES):
        r = res.results[j]["out_own"]
        for ii in range(NSLOT):
            t_ = 8 * ii + j
            out[128 * t_:128 * (t_ + 1)] = r[128 * ii:128 * (ii + 1)]
    return out.reshape(1, T, D)


# version 13
# speedup vs baseline: 1.0575x; 1.0078x over previous
import math
import numpy as np
import ml_dtypes

def _ensure_device():
    import os
    if os.environ.get("TRN_TERMINAL_POOL_IPS"):
        return
    os.environ["TRN_TERMINAL_POOL_IPS"] = os.environ.get("_KERNEL_AXON_IPS", "127.0.0.1")
    os.environ.setdefault("AXON_POOL_SVC_OVERRIDE", "127.0.0.1")
    os.environ.setdefault("AXON_LOOPBACK_RELAY", "1")
    try:
        from trn_agent_boot.trn_boot import boot
        boot(os.environ["TRN_TERMINAL_PRECOMPUTED_JSON"], "/opt/axon/libaxon_pjrt.so")
    except Exception:
        pass


import concourse.bass as bass
import concourse.mybir as mybir
from concourse.bass_utils import run_bass_kernel_spmd

F32 = mybir.dt.float32
BF16 = mybir.dt.bfloat16
AF = mybir.ActivationFunctionType
ALU = mybir.AluOpType
AX = mybir.AxisListType

NCORES = 8
T = 8192
D = 2048
NH = 16
DHR = 64
LAT = 512
DE = 8192
NKT = 16
NSLOT = 8
SM_SCALE = 1.0 / math.sqrt(128.0)
BMARGIN = 30.0 * math.sqrt(128.0)
NEG = -1.0e9
LN_EPS = 1.0e-5
MEAN_DIV = 1.0 / (DE * D)

ENGS = ("T", "V", "S", "P", "Y")

DEBUG_SCAL = False
PHASES = 99


class Buf:
    __slots__ = ("name", "writer", "readers", "excl", "accessors")

    def __init__(self, name, excl=False):
        self.name = name
        self.writer = None
        self.readers = []
        self.excl = excl
        self.accessors = []


class Rec:
    __slots__ = ("key", "fn", "deps", "need_inc", "val", "sem", "inc_amt", "gidx")

    def __init__(self, key, fn):
        self.key = key
        self.fn = fn
        self.deps = []
        self.need_inc = False
        self.val = None
        self.sem = None
        self.inc_amt = 1
        self.gidx = -1


class Prog:
    def __init__(self, nc):
        self.nc = nc
        self.recs = {k: [] for k in ENGS}
        self.esem = {k: nc.alloc_semaphore("sem_" + k) for k in ENGS}
        self.dsems = {}
        self.stream_recs = {}
        self.gctr = 0

    def _dep(self, rec, dep):
        if dep is None or dep is rec:
            return
        if dep.key == rec.key and dep.sem is None and rec.key == "T":
            return
        dep.need_inc = True
        rec.deps.append(dep)

    def emit(self, key, fn, reads=(), writes=(), dma_stream=None):
        rec = Rec(key, fn)
        rec.gidx = self.gctr
        self.gctr += 1
        for b in reads:
            if b.excl:
                for (a, w) in b.accessors:
                    if w or a.key != rec.key:
                        self._dep(rec, a)
            else:
                self._dep(rec, b.writer)
        for b in writes:
            if b.excl:
                for (a, _w) in b.accessors:
                    self._dep(rec, a)
            else:
                self._dep(rec, b.writer)
                for r in b.readers:
                    self._dep(rec, r)
        if dma_stream is not None:
            if dma_stream not in self.dsems:
                self.dsems[dma_stream] = [self.nc.alloc_semaphore("dq_" + dma_stream), 0, key]
            ent = self.dsems[dma_stream]
            assert ent[2] == key, f"stream {dma_stream}: {key} vs {ent[2]}"
            ent[1] += 16
            rec.sem = dma_stream
            rec.val = ent[1]
            rec.inc_amt = 16
            rec.need_inc = True
            self.stream_recs.setdefault(dma_stream, []).append(rec)
        for b in reads:
            if b.excl:
                b.accessors.append((rec, False))
            else:
                b.readers.append(rec)
        for b in writes:
            if b.excl:
                b.accessors.append((rec, True))
            else:
                b.writer = rec
                b.readers = []
        self.recs[key].append(rec)
        return rec

    def barrier(self, keys=ENGS):
        dma_lasts = [v[-1] for v in self.stream_recs.values() if v]
        arr = {}
        for k in keys:
            rec = Rec(k, lambda eng: eng.nop(nofuse=True))
            rec.gidx = self.gctr
            self.gctr += 1
            rec.need_inc = True
            if self.recs[k]:
                prev = self.recs[k][-1]
                prev.need_inc = True
                rec.deps.append(prev)
            self.recs[k].append(rec)
            arr[k] = rec
        col = Rec("Y", lambda eng: eng.nop(nofuse=True))
        col.gidx = self.gctr
        self.gctr += 1
        col.need_inc = True
        for k in keys:
            if k != "Y":
                col.deps.append(arr[k])
        for d in dma_lasts:
            col.deps.append(d)
        self.recs["Y"].append(col)
        for k in keys:
            if k == "Y":
                continue
            rec = Rec(k, lambda eng: eng.nop(nofuse=True))
            rec.gidx = self.gctr
            self.gctr += 1
            rec.deps.append(col)
            self.recs[k].append(rec)

    def finalize_and_replay(self):
        for k in ENGS:
            cnt = 0
            for rec in self.recs[k]:
                if rec.sem is None and rec.need_inc:
                    cnt += 1
                    rec.val = cnt
        self.bumped = {}
        for k in ENGS:
            recs = self.recs[k]
            n = len(recs)
            idx = {id(r): ii for ii, r in enumerate(recs)}
            for ii, r in enumerate(recs):
                if r.sem is not None or not r.need_inc:
                    continue
                v = r.val
                jj = ii + 1
                while jj < n and not recs[jj].deps and recs[jj].sem is None:
                    if recs[jj].need_inc:
                        v = recs[jj].val
                    jj += 1
                self.bumped[id(r)] = v

        def mk_body(key):
            def body(eng):
                watermark = {}
                for rec in self.recs[key]:
                    waits = {}
                    for dep in rec.deps:
                        if dep.val is None:
                            continue
                        if dep.sem is not None:
                            sem, nm = self.dsems[dep.sem][0], "d_" + dep.sem
                            srecs = self.stream_recs[dep.sem]
                            n = 0
                            for r2 in srecs:
                                if r2.gidx < rec.gidx:
                                    n += 1
                                else:
                                    break
                            val = max(dep.val, 16 * n)
                        else:
                            sem, nm = self.esem[dep.key], "e_" + dep.key
                            val = self.bumped.get(id(dep), dep.val)
                        if waits.get(nm, (None, -1))[1] < val:
                            waits[nm] = (sem, val)
                    todo = []
                    for nm, (sem, v) in sorted(waits.items()):
                        if watermark.get(nm, -1) >= v:
                            continue
                        todo.append((sem, v))
                        watermark[nm] = v
                    for sem, v in todo[:-1]:
                        eng.wait_ge(sem, v)
                    inst = rec.fn(eng)
                    if todo:
                        sem, v = todo[-1]
                        if inst is not None:
                            inst._wait_ge(sem, v)
                        else:
                            eng.wait_ge(sem, v)
                    if rec.need_inc and inst is not None:
                        if rec.sem is not None:
                            inst.then_inc(self.dsems[rec.sem][0], 16)
                        else:
                            inst.then_inc(self.esem[rec.key], 1)
            return body

        with self.nc.Block() as block:
            block.tensor(mk_body("T"))
            block.vector(mk_body("V"))
            block.scalar(mk_body("S"))
            block.gpsimd(mk_body("P"))
            block.sync(mk_body("Y"))


class Arena:
    def __init__(self, nc, name, cols, dtype):
        self.t = nc.alloc_sbuf_tensor(name, [128, cols], dtype)
        self.free = [(0, cols)]
        self.live = {}

    def alloc(self, name, cols, align=32):
        for idx, (st, sz) in enumerate(self.free):
            a = (st + align - 1) // align * align
            if a + cols <= st + sz:
                repl = []
                if a - st > 0:
                    repl.append((st, a - st))
                if (st + sz) - (a + cols) > 0:
                    repl.append((a + cols, (st + sz) - (a + cols)))
                self.free[idx:idx + 1] = repl
                self.live[name] = (a, cols)
                return self.t[:, a:a + cols]
        raise RuntimeError(f"arena OOM: {name} needs {cols}, free={self.free}")

    def release(self, *names):
        for name in names:
            st, sz = self.live.pop(name)
            self.free.append((st, sz))
        self.free.sort()
        out = []
        for seg in self.free:
            if out and out[-1][0] + out[-1][1] == seg[0]:
                out[-1] = (out[-1][0], out[-1][1] + seg[1])
            else:
                out.append(seg)
        self.free = out


def build_nc():
    nc = bass.Bass("TRN2", target_bir_lowering=False)

    def din(name, shape, dtype=F32):
        return nc.dram_tensor(name, list(shape), dtype, kind="ExternalInput").ap()

    x = din("x", [T, D])
    own_x = din("own_x", [NSLOT * 128, D])
    wdown = din("wdown", [D, 2 * LAT])
    bdown = din("bdown", [128, 8])
    wukv = din("wukv", [LAT, 2 * D])
    wuq = din("wuq", [LAT, D])
    buq = din("buq", [128, 16])
    wkr = din("wkr", [D, DHR])
    bkr = din("bkr", [DHR])
    wqr = din("wqr", [D, NH * DHR])
    bqr = din("bqr", [DHR, 16])
    wout = din("wout", [D, D])
    wupT = din("wupT", [D, DE])
    wgateT = din("wgateT", [D, DE])
    wdownT = din("wdownT", [DE, D])
    wupW = din("wupW", [16384, 1024])
    wgtW = din("wgtW", [16384, 1024])
    wdnW = din("wdnW", [16384, 1024])
    identf = din("identf", [128, 128])
    pswapf = din("pswapf", [DHR, DHR])
    ropekc = din("ropekc", [DHR, T])
    ropeks = din("ropeks", [DHR, T])
    ropeqc = din("ropeqc", [NH * DHR, 1024])
    ropeqs = din("ropeqs", [NH * DHR, 1024])
    maskadd = din("maskadd", [NSLOT * 2 * 128, 512], BF16)
    vbias_in = din("vbias_in", [128, D])
    woutb_in = din("woutb_in", [128, D])

    out_own = nc.dram_tensor("out_own", [NSLOT * 128, D], F32, kind="ExternalOutput").ap()
    dbg_scal = nc.dram_tensor("dbg_scal", [128, 12], F32, kind="ExternalOutput").ap() if DEBUG_SCAL else None
    dbg_q = nc.dram_tensor("dbg_q", [128, 3 * 2048], F32, kind="ExternalOutput").ap() if DEBUG_SCAL else None

    kscr = nc.dram_tensor("kscr", [NH * 128 + DHR, T], BF16, kind="Internal").ap()
    vscr = nc.dram_tensor("vscr", [T, NH * 129], BF16, kind="Internal").ap()
    oscr = nc.dram_tensor("oscr", [NSLOT * 128, D], BF16, kind="Internal").ap()
    qmscr = nc.dram_tensor("qmscr", [NH * 128, NSLOT * 128], BF16, kind="Internal").ap()
    qrscr = nc.dram_tensor("qrscr", [NH * (DHR + 1), NSLOT * 128], BF16, kind="Internal").ap()
    yscr = nc.dram_tensor("yscr", [NSLOT * 128, D], F32, kind="Internal").ap()

    p = Prog(nc)
    a16 = Arena(nc, "a16", 58624, BF16)
    a32 = Arena(nc, "a32", 20480, F32)

    ident_sb = nc.alloc_sbuf_tensor("ident_sb", [128, 128], BF16).ap()
    identf_sb = nc.alloc_sbuf_tensor("identf_sb", [128, 128], F32).ap()
    pswap_sb = nc.alloc_sbuf_tensor("pswap_sb", [DHR, DHR], BF16).ap()
    ones_sb = nc.alloc_sbuf_tensor("ones_sb", [128, 128], F32).ap()
    bdown_sb = nc.alloc_sbuf_tensor("bdown_sb", [128, 8], F32).ap()
    buq_sb = nc.alloc_sbuf_tensor("buq_sb", [128, 16], F32).ap()
    bkr_sb = nc.alloc_sbuf_tensor("bkr_sb", [DHR, 1], F32).ap()
    bqr_sb = nc.alloc_sbuf_tensor("bqr_sb", [DHR, 16], F32).ap()
    kr0_sb = nc.alloc_sbuf_tensor("kr0_sb", [DHR + 1, 128], BF16).ap()
    krext_bld = nc.alloc_sbuf_tensor("krext_bld", [DHR + 1, 512], BF16).ap()
    krext_st = [nc.alloc_sbuf_tensor(f"krext_st{i}", [DHR + 1, 512], BF16).ap() for i in range(2)]
    krraw_sb2 = [nc.alloc_sbuf_tensor(f"krraw{i}_sb", [DHR, 512], BF16).ap() for i in range(2)]
    krt1_sb2 = [nc.alloc_sbuf_tensor(f"krt1{i}_sb", [DHR, 512], F32).ap() for i in range(2)]
    stats_sb = nc.alloc_sbuf_tensor("stats_sb", [128, 32], F32).ap()
    lcols_sb = nc.alloc_sbuf_tensor("lcols_sb", [128, 16], F32).ap()
    linv_sb = nc.alloc_sbuf_tensor("linv_sb", [128, 16], F32).ap()
    s0all_sb = nc.alloc_sbuf_tensor("s0all_sb", [128, NSLOT * 16], F32).ap()
    scal_sb = nc.alloc_sbuf_tensor("scal_sb", [128, 12], F32).ap()
    eps_sb = nc.alloc_sbuf_tensor("eps_sb", [128, 1], F32).ap()

    PS = nc.alloc_psum_tensor("PS", [128, 4096], F32).ap()
    ps_a = PS[:, 0:512]
    ps_b = PS[:, 512:1024]
    ps_kr = PS[0:DHR, 1024:1536]
    ps_t = PS[:, 1536:2048].bitcast(BF16)[:, 0:512]
    ps_o = PS[:, 0:3072]
    ps_s = [PS[:, 3072:3584], PS[:, 3584:4096]]

    bb = {}

    def buf(name, excl=False):
        if name not in bb:
            bb[name] = Buf(name, excl)
        return bb[name]

    b_ps_a = buf("ps_a", True)
    b_ps_b = buf("ps_b", True)
    b_ps_t = buf("ps_t", True)
    b_ps_kr = buf("ps_kr", True)
    b_ps_o = buf("ps_o", True)
    b_ps_s = [buf("ps_s0", True), buf("ps_s1", True)]

    p.emit("P", lambda g: g.dma_start(out=ident_sb, in_=identf[:, :]), writes=[buf("ident")], dma_stream="cst")
    p.emit("Y", lambda g: g.dma_start(out=identf_sb, in_=identf[:, :]), writes=[buf("identf")], dma_stream="ycst")
    p.emit("P", lambda g: g.dma_start(out=pswap_sb, in_=pswapf[:, :]), writes=[buf("pswap")], dma_stream="cst")
    p.emit("Y", lambda g: g.dma_start(out=bdown_sb, in_=bdown[:, :]),
           writes=[buf("bdown")], dma_stream="ycst")
    p.emit("Y", lambda g: g.dma_start(out=buq_sb, in_=buq[:, :]),
           writes=[buf("buq")], dma_stream="ycst")
    p.emit("Y", lambda g: g.dma_start(out=bkr_sb, in_=bkr[:, None]), writes=[buf("bkr")], dma_stream="ycst")
    p.emit("Y", lambda g: g.dma_start(out=bqr_sb, in_=bqr[:, :]),
           writes=[buf("bqr")], dma_stream="ycst")
    p.emit("V", lambda v: v.memset(ones_sb, 1.0), writes=[buf("ones")])
    p.emit("V", lambda v: v.memset(eps_sb, LN_EPS), writes=[buf("eps")])
    p.emit("V", lambda v: v.memset(krext_bld[DHR:DHR + 1, :], 1.0), writes=[buf("krextd")])
    for par in range(2):
        p.emit("V", lambda v, par=par: v.memset(krext_st[par][DHR:DHR + 1, :], 1.0),
               writes=[buf(f"krst{par}")])

    wdown_sb = a16.alloc("wdown", 16 * 512)
    wukv_sb = a16.alloc("wukv", 4 * 4096)
    xnT_sb = a16.alloc("xnT", 16 * 512)
    KT_sb = a16.alloc("KT", NH * 512)
    V_sb = a16.alloc("Vb", 4 * 2064)
    xntok_sb = a16.alloc("xntok", D)
    ckvT_sb = a16.alloc("ckvT", 4 * 512)
    wkr_sb = a16.alloc("wkr", 16 * DHR)
    junk_sb = a16.alloc("junk", 2048)
    k0_sb = a16.alloc("k0", NH * 128)
    xin_sb = a32.alloc("xin", 3 * D)
    mean_sb = a32.alloc("meanbuf", 6 * 1024)
    hcoll_sb = a32.alloc("hcoll", 128)
    ht_sb = a32.alloc("ht", 128)
    h2_sb = a32.alloc("h2", 1024)
    l2r_sb = a32.alloc("l2r", 64)
    brow_sb = a32.alloc("brow", 1024)[0:NH, :]
    ropek_sb = a32.alloc("ropek", 1024)[0:DHR, :]
    ropeq_sb = a32.alloc("ropeq", 1024)[0:DHR, :]
    ropeq2_sb = a32.alloc("ropeq2", 1024)[0:DHR, :]

    p.emit("P", lambda g: g.dma_start(out=wdown_sb.rearrange("p (c l) -> p c l", l=512),
                                      in_=wdown[:, 0:512].rearrange("(c p) l -> p c l", p=128)),
           writes=[buf("wdown")], dma_stream="wA")
    p.emit("P", lambda g: g.dma_start(out=wukv_sb.rearrange("p (c l) -> p c l", l=4096),
                                      in_=wukv.rearrange("(c p) l -> p c l", p=128)),
           writes=[buf("wukv")], dma_stream="wA")
    p.emit("P", lambda g: g.dma_start(out=wkr_sb.rearrange("p (c l) -> p c l", l=DHR),
                                      in_=wkr.rearrange("(c p) l -> p c l", p=128)),
           writes=[buf("wkr")], dma_stream="wA")
    for s in range(4):
        p.emit("V", lambda v, s=s: v.memset(V_sb[:, 2064 * s + 128:2064 * (s + 1):129], 1.0),
               writes=[buf(f"Vb{s}")])

    def ln_stats(src_ap, b_src, slot):
        base = 8 * slot
        s1 = stats_sb[:, base:base + 1]
        s2 = stats_sb[:, base + 1:base + 2]
        bs = buf(f"lnstats{slot}")
        p.emit("V", lambda v: v.reduce_sum(s1, src_ap, AX.X), reads=[b_src], writes=[bs])
        p.emit("S", lambda sc: sc.activation(junk_sb, src_ap, AF.Square, accum_out=s2),
               reads=[b_src], writes=[bs, buf("junk")])

    def ln_math(slot):
        base = 8 * slot
        s1 = stats_sb[:, base:base + 1]
        s2 = stats_sb[:, base + 1:base + 2]
        mm = stats_sb[:, base + 2:base + 3]
        tmp = stats_sb[:, base + 3:base + 4]
        var = stats_sb[:, base + 4:base + 5]
        inv = stats_sb[:, base + 5:base + 6]
        ngm = stats_sb[:, base + 6:base + 7]
        bs = buf(f"lnstats{slot}")
        p.emit("V", lambda v: v.tensor_scalar(mm, s1, 1.0 / D, None, ALU.mult), reads=[bs], writes=[bs])
        p.emit("V", lambda v: v.tensor_tensor(tmp, mm, mm, ALU.mult), reads=[bs], writes=[bs])
        p.emit("V", lambda v: v.scalar_tensor_tensor(var, s2, 1.0 / D, tmp, ALU.mult, ALU.subtract),
               reads=[bs], writes=[bs])
        p.emit("S", lambda sc: sc.activation(tmp, var, AF.Ln, bias=eps_sb), reads=[bs, buf("eps")], writes=[bs])
        p.emit("S", lambda sc: sc.activation(inv, tmp, AF.Exp, scale=-0.5), reads=[bs], writes=[bs])
        p.emit("V", lambda v: v.scalar_tensor_tensor(ngm, mm, -1.0, inv, ALU.mult, ALU.mult),
               reads=[bs], writes=[bs])

    def ln_apply(src_ap, b_src, dst_ap, b_dst, slot):
        base = 8 * slot
        inv = stats_sb[:, base + 5:base + 6]
        ngm = stats_sb[:, base + 6:base + 7]
        bs = buf(f"lnstats{slot}")
        p.emit("S", lambda sc: sc.activation(dst_ap, src_ap, AF.Identity, bias=ngm, scale=inv),
               reads=[b_src, bs], writes=[b_dst])

    def ln_tile(src_ap, b_src, dst_ap=None, b_dst=None, out_f32=None, b_outf=None, slot=0):
        ln_stats(src_ap, b_src, slot)
        ln_math(slot)
        if dst_ap is not None:
            ln_apply(src_ap, b_src, dst_ap, b_dst, slot)
        if out_f32 is not None:
            ln_apply(src_ap, b_src, out_f32, b_outf, slot)

    def transpose_2048(src_ap, b_src, dst_ap_fn, b_dst):
        for g in range(4):
            for c in range(4):
                cc = 4 * g + c
                p.emit("T", lambda t, cc=cc, c=c: t.transpose(
                    ps_t[:, 128 * c:128 * (c + 1)], src_ap[:, 128 * cc:128 * (cc + 1)], ident_sb),
                    reads=[b_src, buf("ident")], writes=[b_ps_t])
            p.emit("V", lambda v, g=g: v.tensor_copy(
                dst_ap_fn(g), ps_t.rearrange("p (c q) -> p c q", q=128)),
                reads=[b_ps_t], writes=[b_dst])

    def rope_combine(par, psum2, b_psum2, bias_ap, b_bias, cos_ap, sin_ap, b_tab, out_ap, b_out):
        kr_raw, kr_t1 = krraw_sb2[par], krt1_sb2[par]
        braw, bt1 = buf(f"krraw{par}"), buf(f"krt1{par}")
        p.emit("S", lambda sc: sc.activation(kr_raw, psum2, AF.Identity, bias=bias_ap),
               reads=[b_psum2, b_bias], writes=[braw])
        p.emit("T", lambda t: t.matmul(psum2, pswap_sb, kr_raw, start=True, stop=True),
               reads=[buf("pswap"), braw], writes=[b_psum2])
        p.emit("V", lambda v: v.tensor_tensor(kr_t1, kr_raw, cos_ap, ALU.mult),
               reads=[braw, b_tab], writes=[bt1])
        p.emit("V", lambda v: v.tensor_tensor(psum2, psum2, sin_ap, ALU.mult),
               reads=[b_psum2, b_tab], writes=[b_psum2])
        p.emit("V", lambda v: v.tensor_tensor(out_ap, kr_t1, psum2, ALU.add),
               reads=[bt1, b_psum2], writes=[b_out])

    mean_jobs = [(which, w_ap, t) for which, w_ap in ((0, wupW), (1, wgtW), (2, wdnW))
                 for t in range(128)]
    mean_idx = [0]
    SCOL = {0: (0, 1), 1: (2, 3), 2: (4, 5)}

    def finalize_mean(which):
        bh = buf("hcoll")
        p.emit("T", lambda t: t.transpose(PS[:, 0:128], hcoll_sb, identf_sb),
               reads=[bh, buf("identf")], writes=[b_ps_a, b_ps_o])
        p.emit("V", lambda v: v.tensor_copy(ht_sb, PS[:, 0:128]),
               reads=[b_ps_a, b_ps_o], writes=[buf("ht")])
        for w2 in range(16):
            p.emit("P", lambda g, w2=w2: g.dma_start(
                out=h2_sb[w2:w2 + 1, :], in_=ht_sb[8 * w2:8 * w2 + 8, :]),
                reads=[buf("ht")], writes=[buf("h2")], dma_stream="mfl")
        p.emit("S", lambda sc: sc.activation(junk_sb[0:16, 0:1024], h2_sb[0:16, :],
                                             AF.Abs, accum_out=l2r_sb[0:16, 0:1]),
               reads=[buf("h2")], writes=[buf("l2c"), buf("junk")])
        p.emit("P", lambda g: g.dma_start(out=l2r_sb[0:1, 16:32], in_=l2r_sb[0:16, 0:1]),
               reads=[buf("l2c")], writes=[buf("l2r")], dma_stream="mfl")
        p.emit("V", lambda v: v.tensor_tensor(l2r_sb[0:1, 32:40], l2r_sb[0:1, 16:24],
                                              l2r_sb[0:1, 24:32], ALU.add),
               reads=[buf("l2r")], writes=[buf("l2r")])
        p.emit("V", lambda v: v.tensor_tensor(l2r_sb[0:1, 40:44], l2r_sb[0:1, 32:36],
                                              l2r_sb[0:1, 36:40], ALU.add),
               reads=[buf("l2r")], writes=[buf("l2r")])
        p.emit("V", lambda v: v.tensor_tensor(l2r_sb[0:1, 44:46], l2r_sb[0:1, 40:42],
                                              l2r_sb[0:1, 42:44], ALU.add),
               reads=[buf("l2r")], writes=[buf("l2r")])
        p.emit("V", lambda v: v.tensor_tensor(l2r_sb[0:1, 46:47], l2r_sb[0:1, 44:45],
                                              l2r_sb[0:1, 45:46], ALU.add),
               reads=[buf("l2r")], writes=[buf("l2r")])
        p.emit("V", lambda v: v.tensor_scalar(l2r_sb[0:1, 47:48], l2r_sb[0:1, 46:47],
                                              float(np.float32(1.0 / 16777216.0)),
                                              float(np.float32(1.0e-8)), ALU.mult, ALU.add),
               reads=[buf("l2r")], writes=[buf("l2r")])
        p.emit("V", lambda v: v.reciprocal(l2r_sb[0:1, 48:49], l2r_sb[0:1, 47:48]),
               reads=[buf("l2r")], writes=[buf("l2r")])
        cs, cr = SCOL[which]
        p.emit("T", lambda t: t.matmul(PS[:, 0:2], ones_sb[0:1, :],
                                       l2r_sb[0:1, 47:49], start=True, stop=True),
               reads=[buf("l2r"), buf("ones")], writes=[b_ps_a, b_ps_o])
        p.emit("V", lambda v, cs=cs, cr=cr: v.tensor_copy(
            scal_sb[:, cs:cs + 1], PS[:, 0:1]), reads=[b_ps_a, b_ps_o], writes=[buf("scal")])
        p.emit("V", lambda v, cs=cs, cr=cr: v.tensor_copy(
            scal_sb[:, cr:cr + 1], PS[:, 1:2]), reads=[b_ps_a, b_ps_o], writes=[buf("scal")])

    def pump_mean(n, fin_sink=None):
        for _ in range(n):
            if fin_sink:
                return
            if mean_idx[0] >= len(mean_jobs):
                return
            which, w_ap, t = mean_jobs[mean_idx[0]]
            mean_idx[0] += 1
            par = mean_idx[0] % 6
            mb = mean_sb[:, 1024 * par:1024 * (par + 1)]
            bmb = buf(f"meanbuf{par}")
            p.emit("Y", lambda y, w_ap=w_ap, t=t, mb=mb: y.dma_start(
                out=mb, in_=w_ap[128 * t:128 * (t + 1), :]),
                writes=[bmb], dma_stream=f"mean{par}")
            p.emit("S", lambda sc, t=t, mb=mb: sc.activation(
                junk_sb[:, 0:1024], mb, AF.Abs, accum_out=hcoll_sb[:, t:t + 1]),
                reads=[bmb], writes=[buf("hcoll"), buf("junk")])
            if t == 127:
                if fin_sink is not None:
                    fin_sink.append(which)
                else:
                    finalize_mean(which)

    def phase_a():
        b_rk = buf("ropek")
        for kt in range(NKT):
            p.emit("Y", lambda y, kt=kt: y.dma_start(
                out=ropek_sb[:, 0:512], in_=ropekc[:, 512 * kt:512 * (kt + 1)]),
                writes=[b_rk], dma_stream="rk")
            p.emit("Y", lambda y, kt=kt: y.dma_start(
                out=ropek_sb[:, 512:1024], in_=ropeks[:, 512 * kt:512 * (kt + 1)]),
                writes=[b_rk], dma_stream="rk")
            for sub in range(4):
                par = (4 * kt + sub) % 3
                xb = buf(f"xin{par}")
                xa = xin_sb[:, D * par:D * (par + 1)]
                row0 = 512 * kt + 128 * sub
                p.emit("Y", lambda y, row0=row0, xa=xa: y.dma_start(
                    out=xa, in_=x[row0:row0 + 128, :]), writes=[xb], dma_stream=f"xA{par}")
                ln_stats(xa, xb, sub % 2)
                ln_math(sub % 2)
                ln_apply(xa, xb, xntok_sb, buf("xntok"), sub % 2)
                pump_mean(3)

                def dst_fn(g, sub=sub):
                    return xnT_sb.rearrange("p (c q) -> p c q", q=512)[:, 4 * g:4 * g + 4,
                                                                      128 * sub:128 * (sub + 1)]
                transpose_2048(xntok_sb, buf("xntok"), dst_fn, buf("xnT"))
            for lc in range(4):
                pp, bpp = (ps_a, b_ps_a) if lc % 2 == 0 else (ps_b, b_ps_b)
                for dc in range(16):
                    p.emit("T", lambda t, lc=lc, dc=dc, pp=pp: t.matmul(
                        pp, wdown_sb[:, 512 * dc + 128 * lc:512 * dc + 128 * (lc + 1)],
                        xnT_sb[:, 512 * dc:512 * (dc + 1)],
                        start=(dc == 0), stop=(dc == 15)),
                        reads=[buf("wdown"), buf("xnT")], writes=[bpp])
                p.emit("S", lambda s, lc=lc, pp=pp: s.activation(
                    ckvT_sb[:, 512 * lc:512 * (lc + 1)], pp, AF.Identity,
                    bias=bdown_sb[:, lc:lc + 1]),
                    reads=[bpp, buf("bdown")], writes=[buf("ckvT")])
            for h in range(NH):
                pp, bpp = (ps_a, b_ps_a) if h % 2 == 0 else (ps_b, b_ps_b)
                for rc in range(4):
                    p.emit("T", lambda t, h=h, rc=rc, pp=pp: t.matmul(
                        pp, wukv_sb[:, 4096 * rc + 128 * h:4096 * rc + 128 * (h + 1)],
                        ckvT_sb[:, 512 * rc:512 * (rc + 1)],
                        start=(rc == 0), stop=(rc == 3)),
                        reads=[buf("wukv"), buf("ckvT")], writes=[bpp])
                p.emit("S", lambda s, h=h, pp=pp: s.copy(KT_sb[:, 512 * h:512 * (h + 1)], pp),
                       reads=[bpp], writes=[buf(f"KT{h}")])
                p.emit("P", lambda g, kt=kt, h=h: g.dma_start(
                    out=kscr[0:NH * 128, :].rearrange("(h p) c -> p h c", p=128)
                    [:, h:h + 1, 512 * kt:512 * (kt + 1)],
                    in_=KT_sb.rearrange("p (h c) -> p h c", c=512)[:, h:h + 1, :]),
                    reads=[buf(f"KT{h}")], dma_stream=f"spK{h % 2}")
            for sub in range(4):
                for q4 in range(4):
                    pp, bpp = (ps_a, b_ps_a) if (sub * 4 + q4) % 2 == 0 else (ps_b, b_ps_b)
                    for rc in range(4):
                        p.emit("T", lambda t, sub=sub, q4=q4, rc=rc, pp=pp: t.matmul(
                            pp, ckvT_sb[:, 512 * rc + 128 * sub:512 * rc + 128 * (sub + 1)],
                            wukv_sb[:, 4096 * rc + 2048 + 512 * q4:4096 * rc + 2048 + 512 * (q4 + 1)],
                            start=(rc == 0), stop=(rc == 3)),
                            reads=[buf("wukv"), buf("ckvT")], writes=[bpp])
                    p.emit("V", lambda v, sub=sub, q4=q4, pp=pp: v.tensor_copy(
                        V_sb.rearrange("p (s c) -> p s c", c=2064)[:, sub:sub + 1, :]
                        .rearrange("p s (h c) -> p (s h) c", c=129)[:, 4 * q4:4 * (q4 + 1), 0:128],
                        pp.rearrange("p (h c) -> p h c", c=128)),
                        reads=[bpp], writes=[buf(f"Vb{sub}")])
                p.emit("P", lambda g, kt=kt, sub=sub: g.dma_start(
                    out=vscr.rearrange("(n p) c -> p n c", p=128)
                    [:, 4 * kt + sub:4 * kt + sub + 1, :],
                    in_=V_sb.rearrange("p (s c) -> p s c", c=2064)[:, sub:sub + 1, :]),
                    reads=[buf(f"Vb{sub}")], dma_stream=f"spV{sub % 2}")
            for dc in range(16):
                p.emit("T", lambda t, dc=dc: t.matmul(
                    ps_kr, wkr_sb[:, DHR * dc:DHR * (dc + 1)], xnT_sb[:, 512 * dc:512 * (dc + 1)],
                    start=(dc == 0), stop=(dc == 15)),
                    reads=[buf("wkr"), buf("xnT")], writes=[b_ps_kr])
            rope_combine(kt % 2, ps_kr, b_ps_kr, bkr_sb, buf("bkr"),
                         ropek_sb[:, 0:512], ropek_sb[:, 512:1024], b_rk,
                         krext_bld[0:DHR, :], buf("krextd"))
            if kt == 0:
                p.emit("V", lambda v: v.tensor_copy(
                    k0_sb.rearrange("p (h c) -> p h c", c=128),
                    KT_sb.rearrange("p (h c) -> p h c", c=512)[:, :, 0:128]),
                    reads=[buf(f"KT{h}") for h in range(NH)], writes=[buf("k0")])
                p.emit("V", lambda v: v.tensor_copy(kr0_sb[0:DHR, :], krext_bld[0:DHR, 0:128]),
                       reads=[buf("krextd")], writes=[buf("kr0")])
                p.emit("V", lambda v: v.memset(kr0_sb[DHR:DHR + 1, :], 1.0), writes=[buf("kr0")])
            p.emit("P", lambda g, kt=kt: g.dma_start(
                out=kscr[NH * 128:NH * 128 + DHR, 512 * kt:512 * (kt + 1)],
                in_=krext_bld[0:DHR, :]),
                reads=[buf("krextd")], dma_stream="spK0")

    def phase_a2():
        a16.release("wdown", "wukv", "xnT", "KT", "Vb", "ckvT", "xntok", "wkr")
        ownxnT = a16.alloc("ownxnT", 16 * 1024)
        qtok = a16.alloc("qtok", D)
        b_own = buf("ownxnT")
        for i in range(NSLOT):
            par = i % 2
            xb = buf(f"xin{par}")
            xa = xin_sb[:, D * par:D * (par + 1)]
            p.emit("Y", lambda y, i=i, xa=xa: y.dma_start(
                out=xa, in_=own_x[128 * i:128 * (i + 1), :]), writes=[xb], dma_stream=f"xq{par}")
            ln_tile(xa, xb, dst_ap=qtok, b_dst=buf("qtok"), slot=i % 2)

            def dst_fn(g, i=i):
                return ownxnT.rearrange("p (c q) -> p c q", q=1024)[:, 4 * g:4 * g + 4,
                                                                   128 * i:128 * (i + 1)]
            transpose_2048(qtok, buf("qtok"), dst_fn, b_own)
        wqr_sb = a16.alloc("wqr", 16 * 1024)
        qrope = a16.alloc("qrope", NH * 1024)
        p.emit("P", lambda g: g.dma_start(out=wqr_sb.rearrange("p (c l) -> p c l", l=1024),
                                          in_=wqr.rearrange("(c p) l -> p c l", p=128)),
               writes=[buf("wqr")], dma_stream="wA")
        rq2 = [buf("ropeq0"), buf("ropeq1")]
        for h in range(NH):
            for th in range(2):
                par = (h * 2 + th) % 2
                pk = ps_kr if par == 0 else ps_a[0:DHR, :]
                bpk = b_ps_kr if par == 0 else b_ps_a
                rqa = ropeq_sb[:, 1024 * 0 + 512 * par:512 * par + 512] if False else None
                ca = ropeq_sb[:, 512 * par:512 * (par + 1)]
                p.emit("Y", lambda y, h=h, th=th, par=par: y.dma_start(
                    out=ropeq_sb[:, 512 * par:512 * (par + 1)].rearrange("p c -> p c"),
                    in_=ropeqc[DHR * h:DHR * (h + 1), 512 * th:512 * (th + 1)]),
                    writes=[rq2[par]], dma_stream=f"rq{par}")
                p.emit("Y", lambda y, h=h, th=th, par=par: y.dma_start(
                    out=ropeq2_sb[:, 512 * par:512 * (par + 1)],
                    in_=ropeqs[DHR * h:DHR * (h + 1), 512 * th:512 * (th + 1)]),
                    writes=[rq2[par]], dma_stream=f"rq{par}")
                for dc in range(16):
                    p.emit("T", lambda t, h=h, th=th, dc=dc, pk=pk: t.matmul(
                        pk, wqr_sb[:, 1024 * dc + DHR * h:1024 * dc + DHR * (h + 1)],
                        ownxnT[:, 1024 * dc + 512 * th:1024 * dc + 512 * (th + 1)],
                        start=(dc == 0), stop=(dc == 15)),
                        reads=[buf("wqr"), b_own], writes=[bpk])
                rope_combine(par, pk, bpk, bqr_sb[:, h:h + 1], buf("bqr"),
                             ropeq_sb[:, 512 * par:512 * (par + 1)],
                             ropeq2_sb[:, 512 * par:512 * (par + 1)], rq2[par],
                             qrope[0:DHR, 1024 * h + 512 * th:1024 * h + 512 * (th + 1)],
                             buf("qrope"))
        p.barrier()
        a16.release("wqr", "qtok")
        wdq_sb = a16.alloc("wdownq", 16 * 512)
        cqT_sb = a16.alloc("cqT", 4 * 1024)
        p.emit("P", lambda g: g.dma_start(out=wdq_sb.rearrange("p (c l) -> p c l", l=512),
                                          in_=wdown[:, 512:1024].rearrange("(c p) l -> p c l", p=128)),
               writes=[buf("wdownq")], dma_stream="wA")
        for lc in range(4):
            for th in range(2):
                pp, bpp = (ps_a, b_ps_a) if (lc * 2 + th) % 2 == 0 else (ps_b, b_ps_b)
                for dc in range(16):
                    p.emit("T", lambda t, lc=lc, th=th, dc=dc, pp=pp: t.matmul(
                        pp, wdq_sb[:, 512 * dc + 128 * lc:512 * dc + 128 * (lc + 1)],
                        ownxnT[:, 1024 * dc + 512 * th:1024 * dc + 512 * (th + 1)],
                        start=(dc == 0), stop=(dc == 15)),
                        reads=[buf("wdownq"), b_own], writes=[bpp])
                p.emit("S", lambda s, lc=lc, th=th, pp=pp: s.activation(
                    cqT_sb[:, 1024 * lc + 512 * th:1024 * lc + 512 * (th + 1)], pp,
                    AF.Identity, bias=bdown_sb[:, 4 + lc:5 + lc]),
                    reads=[bpp, buf("bdown")], writes=[buf("cqT")])
        p.barrier()
        a16.release("ownxnT", "wdownq")
        wuq_sb = a16.alloc("wuq", 4 * 2048)
        qmainT = a16.alloc("qmainT", NH * 1024)
        p.emit("P", lambda g: g.dma_start(out=wuq_sb.rearrange("p (c l) -> p c l", l=2048),
                                          in_=wuq.rearrange("(c p) l -> p c l", p=128)),
               writes=[buf("wuq")], dma_stream="wA")
        for h in range(NH):
            for th in range(2):
                pp, bpp = (ps_a, b_ps_a) if (h * 2 + th) % 2 == 0 else (ps_b, b_ps_b)
                for rc in range(4):
                    p.emit("T", lambda t, h=h, th=th, rc=rc, pp=pp: t.matmul(
                        pp, wuq_sb[:, 2048 * rc + 128 * h:2048 * rc + 128 * (h + 1)],
                        cqT_sb[:, 1024 * rc + 512 * th:1024 * rc + 512 * (th + 1)],
                        start=(rc == 0), stop=(rc == 3)),
                        reads=[buf("wuq"), buf("cqT")], writes=[bpp])
                p.emit("S", lambda s, h=h, th=th, pp=pp: s.activation(
                    qmainT[:, 1024 * h + 512 * th:1024 * h + 512 * (th + 1)], pp,
                    AF.Identity, bias=buq_sb[:, h:h + 1]),
                    reads=[bpp, buf("buq")], writes=[buf("qmainT")])
        p.barrier()
        a16.release("wuq", "cqT")
        a32.release("ropek", "ropeq", "ropeq2")
        return qmainT, qrope

    def phase_a3(qmainT, qrope):
        for i in range(NSLOT):
            for h in range(NH):
                pp, bpp = ps_s[h % 2], b_ps_s[h % 2]
                p.emit("T", lambda t, i=i, h=h, pp=pp: t.matmul(
                    pp[:, 0:128], qmainT[:, 1024 * h + 128 * i:1024 * h + 128 * (i + 1)],
                    k0_sb[:, 128 * h:128 * (h + 1)], start=True, stop=False),
                    reads=[buf("qmainT"), buf("k0")], writes=[bpp])
                p.emit("T", lambda t, i=i, h=h, pp=pp: t.matmul(
                    pp[:, 0:128], qrope[0:DHR, 1024 * h + 128 * i:1024 * h + 128 * (i + 1)],
                    kr0_sb[0:DHR, :], start=False, stop=True),
                    reads=[buf("qrope"), buf("kr0")], writes=[bpp])
                p.emit("V", lambda v, i=i, h=h, pp=pp: v.reduce_max(
                    s0all_sb[:, 16 * i + h:16 * i + h + 1], pp[:, 0:128], AX.X),
                    reads=[bpp], writes=[buf("s0all")])
        for i in range(NSLOT):
            p.emit("T", lambda t, i=i: t.transpose(
                ps_a[0:16, 0:128], s0all_sb[:, 16 * i:16 * (i + 1)], identf_sb),
                reads=[buf("s0all"), buf("identf")], writes=[b_ps_a])
            p.emit("V", lambda v, i=i: v.tensor_scalar(
                brow_sb[:, 128 * i:128 * (i + 1)], ps_a[0:NH, 0:128], BMARGIN, -1.0,
                ALU.add, ALU.mult),
                reads=[b_ps_a], writes=[buf("brow")])
        p.emit("P", lambda g: g.dma_start(
            out=qrope[DHR:DHR + 1, :], in_=brow_sb),
            reads=[buf("brow")], writes=[buf("qrope")], dma_stream="brw")
        p.emit("P", lambda g: g.dma_start(
            out=qmscr.rearrange("(h p) c -> p h c", p=128),
            in_=qmainT.rearrange("p (h c) -> p h c", c=1024)),
            reads=[buf("qmainT")], dma_stream="spQ")
        p.emit("P", lambda g: g.dma_start(
            out=qrscr.rearrange("(h p) c -> p h c", p=DHR + 1),
            in_=qrope[0:DHR + 1, :].rearrange("p (h c) -> p h c", c=1024)),
            reads=[buf("qrope")], dma_stream="spQ")
        a16.release("k0", "qmainT", "qrope")

    def phase_b():
        KTs = [a16.alloc(f"KTs{par}", NH * 512) for par in range(2)]
        Vs = [a16.alloc(f"Vs{par}", 4 * 2064) for par in range(2)]
        msk = a16.alloc("msk", 2 * 1024)
        pT = [a16.alloc(f"pTb{par}", 512) for par in range(2)]
        Osb = a16.alloc("Osb", D)
        qm = [a16.alloc(f"qm{par}", NH * 128) for par in range(2)]
        qr = [a16.alloc(f"qr{par}", NH * 128) for par in range(2)]
        a32.release("brow")
        vbias_sb = a32.alloc("vbias", D)
        p.emit("Y", lambda y: y.dma_start(out=vbias_sb, in_=vbias_in[:, :]),
               writes=[buf("vbias")], dma_stream="ycst")
        visit = 0
        pv_pending = [None]
        fin_pend = []

        def slot_inputs(i):
            mpar = i % 2
            p.emit("Y", lambda y, i=i, mpar=mpar: y.dma_start(
                out=msk[:, 1024 * mpar:1024 * (mpar + 1)].rearrange("p (m c) -> p m c", c=512),
                in_=maskadd[256 * i:256 * (i + 1), :].rearrange("(m p) c -> p m c", p=128)),
                writes=[buf(f"msk{mpar}")], dma_stream=f"mk{mpar}")
            p.emit("Y", lambda y, i=i, mpar=mpar: y.dma_start(
                out=qm[mpar].rearrange("p (h c) -> p h c", c=128),
                in_=qmscr.rearrange("(h p) c -> p h c", p=128)[:, :, 128 * i:128 * (i + 1)]),
                writes=[buf(f"qm{mpar}")], dma_stream=f"qm{mpar}")
            p.emit("Y", lambda y, i=i, mpar=mpar: y.dma_start(
                out=qr[mpar][0:DHR + 1, :].rearrange("p (h c) -> p h c", c=128),
                in_=qrscr.rearrange("(h p) c -> p h c", p=DHR + 1)[:, :, 128 * i:128 * (i + 1)]),
                writes=[buf(f"qr{mpar}")], dma_stream=f"qr{mpar}")

        slot_inputs(0)
        slot_inputs(1)
        for i in range(NSLOT):
            ci = 2 * (i + 1)
            mpar = i % 2
            bmk = buf(f"msk{mpar}")
            bqm = buf(f"qm{mpar}")
            bqr2 = buf(f"qr{mpar}")
            for kt in range(ci):
                par = visit % 2
                bK = buf(f"KTs{par}")
                bV = buf(f"Vs{par}")
                bkrs = buf(f"krst{par}")
                p.emit("Y", lambda y, kt=kt, par=par: y.dma_start(
                    out=KTs[par].rearrange("p (h c) -> p h c", c=512),
                    in_=kscr[0:NH * 128, :].rearrange("(h p) c -> p h c", p=128)[:, :, 512 * kt:512 * (kt + 1)]),
                    writes=[bK], dma_stream=f"kv{par}")
                p.emit("Y", lambda y, kt=kt, par=par: y.dma_start(
                    out=krext_st[par][0:DHR, :],
                    in_=kscr[NH * 128:NH * 128 + DHR, 512 * kt:512 * (kt + 1)]),
                    writes=[bkrs], dma_stream=f"kv{par}")
                p.emit("Y", lambda y, kt=kt, par=par: y.dma_start(
                    out=Vs[par].rearrange("p (s c) -> p s c", c=2064),
                    in_=vscr.rearrange("(n p) c -> p n c", p=128)[:, 4 * kt:4 * (kt + 1), :]),
                    writes=[bV], dma_stream=f"kv{par}")
                masked = kt >= ci - 2
                for h in range(NH):
                    sp, bsp = ps_s[h % 2], b_ps_s[h % 2]
                    ppar = h % 2
                    bpT = buf(f"pT{ppar}")
                    for s in range(4):
                        p.emit("T", lambda t, i=i, h=h, s=s, par=par, sp=sp, mpar=mpar: t.matmul(
                            sp[:, 128 * s:128 * (s + 1)],
                            KTs[par][:, 512 * h + 128 * s:512 * h + 128 * (s + 1)],
                            qm[mpar][:, 128 * h:128 * (h + 1)],
                            start=True, stop=False),
                            reads=[bK, bqm], writes=[bsp])
                        p.emit("T", lambda t, i=i, h=h, s=s, par=par, sp=sp, mpar=mpar: t.matmul(
                            sp[:, 128 * s:128 * (s + 1)],
                            krext_st[par][:, 128 * s:128 * (s + 1)],
                            qr[mpar][0:DHR + 1, 128 * h:128 * (h + 1)],
                            start=False, stop=True),
                            reads=[bkrs, bqr2], writes=[bsp])
                    if masked:
                        mi = kt - (ci - 2)
                        p.emit("V", lambda v, mi=mi, mpar=mpar, sp=sp: v.tensor_tensor(
                            sp, sp, msk[:, 1024 * mpar + 512 * mi:1024 * mpar + 512 * (mi + 1)],
                            ALU.add),
                            reads=[bmk, bsp], writes=[bsp])
                    p.emit("S", lambda s_, ppar=ppar, sp=sp: s_.activation(
                        pT[ppar], sp, AF.Exp, scale=SM_SCALE),
                        reads=[bsp], writes=[bpT])
                    if pv_pending[0] is not None:
                        pv_pending[0]()
                    bnk = h // 3
                    sslot = h % 3
                    ob = 512 * bnk + 129 * sslot
                    last_h = min(3 * bnk + 2, NH - 1)

                    def mk_pv(kt=kt, h=h, ppar=ppar, par=par, ob=ob, bnk=bnk,
                              sslot=sslot, last_h=last_h, ci=ci, bpT=bpT, bV=bV):
                        for s in range(4):
                            first = (kt == 0 and sslot == 0 and s == 0)
                            last = (kt == ci - 1 and h == last_h and s == 3)
                            p.emit("T", lambda t, s=s, ppar=ppar, par=par, h=h, ob=ob,
                                   first=first, last=last: t.matmul(
                                ps_o[:, ob:ob + 129],
                                pT[ppar][:, 128 * s:128 * (s + 1)],
                                Vs[par][:, 2064 * s + 129 * h:2064 * s + 129 * (h + 1)],
                                start=first, stop=last),
                                reads=[bpT, bV], writes=[b_ps_o])
                    pv_pending[0] = mk_pv
                visit += 1
                pump_mean(2, fin_pend)
            if pv_pending[0] is not None:
                pv_pending[0]()
                pv_pending[0] = None
            p.emit("V", lambda v: v.tensor_copy(
                lcols_sb[:, 0:15].rearrange("p (b s) -> p b s", s=3),
                ps_o.rearrange("p (b c) -> p b c", c=512)[:, 0:5, 128:388:129]),
                reads=[b_ps_o], writes=[buf("lcols")])
            p.emit("V", lambda v: v.tensor_copy(lcols_sb[:, 15:16],
                                                ps_o[:, 512 * 5 + 128:512 * 5 + 129]),
                   reads=[b_ps_o], writes=[buf("lcols")])
            p.emit("V", lambda v: v.reciprocal(linv_sb, lcols_sb),
                   reads=[buf("lcols")], writes=[buf("linv")])
            for h in range(NH):
                ob = 512 * (h // 3) + 129 * (h % 3)
                p.emit("V", lambda v, h=h, ob=ob: v.scalar_tensor_tensor(
                    Osb[:, 128 * h:128 * (h + 1)], ps_o[:, ob:ob + 128], linv_sb[:, h:h + 1],
                    vbias_sb[:, 128 * h:128 * (h + 1)], ALU.mult, ALU.add),
                    reads=[b_ps_o, buf("linv"), buf("vbias")], writes=[buf("Osb")])
            p.emit("P", lambda g, i=i: g.dma_start(out=oscr[128 * i:128 * (i + 1), :], in_=Osb),
                   reads=[buf("Osb")], dma_stream="spO")
            for w in fin_pend:
                finalize_mean(w)
            fin_pend.clear()
            if i + 2 < NSLOT:
                slot_inputs(i + 2)
        a16.release("KTs0", "KTs1", "Vs0", "Vs1", "msk", "pTb0", "pTb1", "Osb",
                    "qm0", "qm1", "qr0", "qr1")
        a32.release("vbias")

    def phase_c():
        wout_sb = a16.alloc("wout", 16 * 2048)
        Oin = a16.alloc("Oin", D)
        OT = a16.alloc("OT", D)
        xn2T = a16.alloc("xn2T", 16 * 1024)
        xn2tok = a16.alloc("xn2tok", D)
        att_sb = a32.alloc("att", D)
        yt_sb = a32.alloc("yt", D)
        woutb_sb = a32.alloc("woutb", D)
        p.emit("Y", lambda y: y.dma_start(out=woutb_sb, in_=woutb_in[:, :]),
               writes=[buf("woutb")], dma_stream="ycst")
        p.emit("P", lambda g: g.dma_start(out=wout_sb.rearrange("p (c l) -> p c l", l=2048),
                                          in_=wout.rearrange("(c p) l -> p c l", p=128)),
               writes=[buf("wout")], dma_stream="wA")
        xt_pending = [None]
        for i in range(NSLOT):
            par = i % 2
            bO = buf("Oin")
            p.emit("Y", lambda y, i=i: y.dma_start(
                out=Oin, in_=oscr[128 * i:128 * (i + 1), :]), writes=[bO], dma_stream="oin")

            def dst_fn(g):
                return OT.rearrange("p (c q) -> p c q", q=128)[:, 4 * g:4 * g + 4, :]
            transpose_2048(Oin, bO, dst_fn, buf("OT"))
            for q4 in range(4):
                pp, bpp = (ps_a, b_ps_a) if q4 % 2 == 0 else (ps_b, b_ps_b)
                for dc in range(16):
                    p.emit("T", lambda t, q4=q4, dc=dc, pp=pp: t.matmul(
                        pp, OT[:, 128 * dc:128 * (dc + 1)],
                        wout_sb[:, 2048 * dc + 512 * q4:2048 * dc + 512 * (q4 + 1)],
                        start=(dc == 0), stop=(dc == 15)),
                        reads=[buf("OT"), buf("wout")], writes=[bpp])
                p.emit("V", lambda v, q4=q4, pp=pp: v.tensor_tensor(
                    att_sb[:, 512 * q4:512 * (q4 + 1)], pp,
                    woutb_sb[:, 512 * q4:512 * (q4 + 1)], ALU.add),
                    reads=[bpp, buf("woutb")], writes=[buf("att")])
            if xt_pending[0] is not None:
                xt_pending[0]()
            xb = buf(f"xin{par}")
            xa = xin_sb[:, D * par:D * (par + 1)]
            p.emit("Y", lambda y, i=i, xa=xa: y.dma_start(
                out=xa, in_=own_x[128 * i:128 * (i + 1), :]), writes=[xb], dma_stream=f"xq{par}")
            ln_tile(att_sb, buf("att"), out_f32=yt_sb, b_outf=buf("yt"), slot=0)
            p.emit("V", lambda v, xa=xa: v.tensor_tensor(yt_sb, yt_sb, xa, ALU.add),
                   reads=[xb, buf("yt")], writes=[buf("yt")])
            p.emit("P", lambda g, i=i: g.dma_start(out=yscr[128 * i:128 * (i + 1), :], in_=yt_sb),
                   reads=[buf("yt")], dma_stream="spY")
            ln_tile(yt_sb, buf("yt"), dst_ap=xn2tok, b_dst=buf("xn2tok"), slot=1)

            def mk_xt(i=i):
                def dst2_fn(g, i=i):
                    return xn2T.rearrange("p (c q) -> p c q", q=1024)[:, 4 * g:4 * g + 4,
                                                                     128 * i:128 * (i + 1)]
                return lambda: transpose_2048(xn2tok, buf("xn2tok"), dst2_fn, buf("xn2T"))
            xt_pending[0] = mk_xt()
            pump_mean(8)
        xt_pending[0]()
        a16.release("wout", "Oin", "OT", "xn2tok")
        a32.release("att", "yt", "woutb")
        return xn2T

    def phase_d(xn2T):
        pump_mean(len(mean_jobs))
        a32.release("xin", "meanbuf", "hcoll", "ht", "h2", "l2r")
        bscal = buf("scal")
        p.emit("V", lambda v: v.tensor_tensor(scal_sb[:, 9:10], scal_sb[:, 2:3],
                                              scal_sb[:, 4:5], ALU.mult),
               reads=[bscal], writes=[bscal])

        if DEBUG_SCAL:
            p.emit("Y", lambda y: y.dma_start(out=dbg_scal[:, :], in_=scal_sb),
                   reads=[buf("scal")], dma_stream="dbg")
        assert mean_idx[0] == len(mean_jobs), "mean jobs must be drained before D"
        a16.release("junk")
        outacc = a32.alloc("outacc", 16 * 1024)
        wst = [a32.alloc(f"wst{j}", 512) for j in range(8)]
        tupAB = [a16.alloc("tupA", 16 * 512), a16.alloc("tupB", 16 * 512)]
        tgt = a16.alloc("tgt", 16 * 512)
        tdn = a16.alloc("tdn", 4 * 2048)
        hid = a16.alloc("hid", 4 * 1024)
        q1 = a16.alloc("q1", 512)
        q2 = a16.alloc("q2", 512)
        silu_sb2 = [a16.alloc(f"silu{j}", 512) for j in range(2)]

        pd = [PS[:, 1024 + 512 * k:1536 + 512 * k] for k in range(6)]
        b_pd = [buf(f"pd{k}", True) for k in range(6)]

        def quant(dst_ap, b_dst, src_ap, b_src, rcol):
            p.emit("V", lambda v: v.tensor_scalar(
                q1, src_ap, scal_sb[:, rcol:rcol + 1], 0.5, ALU.mult, ALU.is_gt),
                reads=[b_src, bscal], writes=[buf("q1")])
            p.emit("V", lambda v: v.tensor_scalar(
                q2, src_ap, scal_sb[:, rcol:rcol + 1], -0.5, ALU.mult, ALU.is_lt),
                reads=[b_src, bscal], writes=[buf("q2")])
            p.emit("V", lambda v: v.tensor_tensor(dst_ap, q1, q2, ALU.subtract),
                   reads=[buf("q1"), buf("q2")], writes=[b_dst])

        st_ctr = [0]

        def quant_tile(j, g):
            st = st_ctr[0] % 8
            st_ctr[0] += 1
            wa, wb = wst[st], buf(f"wst{st}")
            if j < 32:
                wT = wupT if j < 16 else wgateT
                dc = j % 16
                src = lambda: wT[128 * dc:128 * (dc + 1), 512 * g:512 * (g + 1)]
                if j < 16:
                    dst = tupAB[g % 2][:, 512 * dc:512 * (dc + 1)]
                    bdst = buf(f"tup{g % 2}_{dc}")
                    rcol = 1
                else:
                    dst = tgt[:, 512 * dc:512 * (dc + 1)]
                    bdst = buf(f"tgt{dc}")
                    rcol = 3
            else:
                k = j - 32
                rc, s2 = k // 4, k % 4
                src = lambda: wdownT[512 * g + 128 * rc:512 * g + 128 * (rc + 1),
                                     512 * s2:512 * (s2 + 1)]
                dst = tdn[:, 2048 * rc + 512 * s2:2048 * rc + 512 * (s2 + 1)]
                bdst = buf(f"tdn{rc}_{s2}")
                rcol = 5
            p.emit("Y", lambda y: y.dma_start(out=wa, in_=src()),
                   writes=[wb], dma_stream=f"wD{st}")
            quant(dst, bdst, wa, wb, rcol)

        for j in range(48):
            quant_tile(j, 0)
        for g in range(16):
            tup = tupAB[g % 2]
            up_tiles = [buf(f"tup{g % 2}_{dc}") for dc in range(16)]
            gt_tiles = [buf(f"tgt{dc}") for dc in range(16)]
            for hc in range(4):
                for th in range(2):
                    idx = hc * 2 + th
                    sl, bsl = silu_sb2[idx % 2], buf(f"silu{idx % 2}")
                    for dc in range(16):
                        rds = ([buf("xn2T")] + up_tiles) if dc in (0, 15) else []
                        p.emit("T", lambda t, hc=hc, th=th, dc=dc, tup=tup: t.matmul(
                            ps_a, tup[:, 512 * dc + 128 * hc:512 * dc + 128 * (hc + 1)],
                            xn2T[:, 1024 * dc + 512 * th:1024 * dc + 512 * (th + 1)],
                            start=(dc == 0), stop=(dc == 15)),
                            reads=rds, writes=[b_ps_a])
                    for dc in range(16):
                        rds = ([buf("xn2T")] + gt_tiles) if dc in (0, 15) else []
                        p.emit("T", lambda t, hc=hc, th=th, dc=dc: t.matmul(
                            ps_b, tgt[:, 512 * dc + 128 * hc:512 * dc + 128 * (hc + 1)],
                            xn2T[:, 1024 * dc + 512 * th:1024 * dc + 512 * (th + 1)],
                            start=(dc == 0), stop=(dc == 15)),
                            reads=rds, writes=[b_ps_b])
                    p.emit("S", lambda sc, sl=sl: sc.activation(
                        sl, ps_a, AF.Silu, scale=scal_sb[:, 0:1]),
                        reads=[b_ps_a, bscal], writes=[bsl])
                    p.emit("V", lambda v, hc=hc, th=th, sl=sl: v.tensor_tensor(
                        hid[:, 1024 * hc + 512 * th:1024 * hc + 512 * (th + 1)],
                        sl, ps_b, ALU.mult),
                        reads=[bsl, b_ps_b], writes=[buf(f"hid{idx}")])
                    if g + 1 < 16:
                        quant_tile(2 * idx, g + 1)
                        quant_tile(2 * idx + 1, g + 1)
            for dm in range(16):
                for th in range(2):
                    it = dm * 2 + th
                    k = it % 6
                    dn_tiles = ([buf(f"tdn{rc}_{dm // 4}") for rc in range(4)] +
                                [buf(f"hid{rc * 2 + th}") for rc in range(4)])
                    for rc in range(4):
                        rds = dn_tiles if rc in (0, 3) else []
                        p.emit("T", lambda t, dm=dm, th=th, rc=rc, k=k: t.matmul(
                            pd[k], tdn[:, 2048 * rc + 128 * dm:2048 * rc + 128 * (dm + 1)],
                            hid[:, 1024 * rc + 512 * th:1024 * rc + 512 * (th + 1)],
                            start=(rc == 0), stop=(rc == 3)),
                            reads=rds, writes=[b_pd[k]])
                    if th == 1:
                        k2 = (2 * dm) % 6
                        if g == 0:
                            p.emit("V", lambda v, dm=dm, k2=k2: v.tensor_copy(
                                outacc[:, 1024 * dm:1024 * (dm + 1)],
                                PS[:, 1024 + 512 * k2:2048 + 512 * k2]),
                                reads=[b_pd[k2], b_pd[k2 + 1]], writes=[buf("outacc")])
                        else:
                            p.emit("V", lambda v, dm=dm, k2=k2: v.tensor_tensor(
                                outacc[:, 1024 * dm:1024 * (dm + 1)],
                                outacc[:, 1024 * dm:1024 * (dm + 1)],
                                PS[:, 1024 + 512 * k2:2048 + 512 * k2], ALU.add),
                                reads=[b_pd[k2], b_pd[k2 + 1], buf("outacc")],
                                writes=[buf("outacc")])
                    if g + 1 < 16:
                        if it < 16:
                            quant_tile(16 + it, g + 1)
                        if th == 1 and dm % 4 == 3:
                            s2 = dm // 4
                            for rc in range(4):
                                quant_tile(32 + rc * 4 + s2, g + 1)
        a16.release("tupA", "tupB", "tgt", "tdn", "hid", "q1", "q2", "silu0", "silu1", "xn2T")
        a32.release(*[f"wst{j}" for j in range(8)])
        return outacc

    def phase_e(outacc):
        ffn_sb = a32.alloc("ffn", D)
        fout = a32.alloc("fout", D)
        for i in range(NSLOT):
            p.emit("Y", lambda y, i=i: y.dma_start(
                out=fout, in_=yscr[128 * i:128 * (i + 1), :]), writes=[buf("fout")], dma_stream="yin")
            for g in range(4):
                for c in range(4):
                    dc = 4 * g + c
                    p.emit("T", lambda t, dc=dc, c=c, i=i: t.transpose(
                        ps_b[:, 128 * c:128 * (c + 1)],
                        outacc[:, 1024 * dc + 128 * i:1024 * dc + 128 * (i + 1)], identf_sb),
                        reads=[buf("outacc"), buf("identf")], writes=[b_ps_b])
                p.emit("V", lambda v, g=g: v.tensor_scalar(
                    ffn_sb[:, 512 * g:512 * (g + 1)], ps_b, scal_sb[:, 9:10], None, ALU.mult),
                    reads=[b_ps_b, buf("scal")], writes=[buf("ffn")])
            ln_tile(ffn_sb, buf("ffn"), out_f32=ffn_sb, b_outf=buf("ffn"), slot=i % 2)
            p.emit("V", lambda v: v.tensor_tensor(fout, fout, ffn_sb, ALU.add),
                   reads=[buf("ffn"), buf("fout")], writes=[buf("fout")])
            p.emit("Y", lambda y, i=i: y.dma_start(out=out_own[128 * i:128 * (i + 1), :], in_=fout),
                   reads=[buf("fout")], dma_stream="out")
        last_val = p.dsems["out"][1]
        p.recs["Y"].append(Rec("Y", lambda y: y.wait_ge(p.dsems["out"][0], last_val)))

    phase_a()
    if PHASES >= 2:
        p.barrier()
        qmainT, qrope = phase_a2()
        phase_a3(qmainT, qrope)
    if PHASES >= 3:
        p.barrier()
        phase_b()
    if PHASES >= 4:
        p.barrier()
        xn2T = phase_c()
    if PHASES >= 5:
        p.barrier()
        outacc = phase_d(xn2T)
    if PHASES >= 6:
        p.barrier()
        phase_e(outacc)

    p.finalize_and_replay()
    return nc


def _win_rearrange(W):
    R, C = W.shape
    nR, nC = R // 32, C // 32
    A = W.reshape(nR, 32, nC, 32).transpose(0, 2, 1, 3).reshape(nR, nC, 1024)
    B = A.reshape(nR // 32, 32, nC // 32, 32, 1024).transpose(0, 2, 1, 3, 4)
    return np.ascontiguousarray(B.reshape(16384, 1024))


def _rope_tables(n):
    pos = np.repeat(np.arange(n // 2, dtype=np.float64), 2)
    theta = np.exp(-2.0 * pos / n * np.log(10000.0))
    freq = np.arange(T, dtype=np.float64) + 1.0
    ang = freq[:, None] * theta[None, :]
    return np.cos(ang).astype(np.float32), np.sin(ang).astype(np.float32)


def kernel(L0_attn_x, L0_attn_W_down_k, L0_attn_W_down_b, L0_attn_W_uKV_k, L0_attn_W_uKV_b,
           L0_attn_W_uQ_k, L0_attn_W_uQ_b, L0_attn_Wkr_k, L0_attn_Wkr_b, L0_attn_Wqr_k,
           L0_attn_Wqr_b, L0_attn_Wout_k, L0_attn_Wout_b, L0_ffn_W_up, L0_ffn_W_gate,
           L0_ffn_W_down):
    _ensure_device()
    f32 = np.float32
    x = np.ascontiguousarray(np.asarray(L0_attn_x, f32).reshape(T, D))
    wdown = np.ascontiguousarray(np.asarray(L0_attn_W_down_k, f32))
    bdown = np.ascontiguousarray(np.asarray(L0_attn_W_down_b, f32))
    bukv = np.ascontiguousarray(np.asarray(L0_attn_W_uKV_b, f32))
    bout = np.ascontiguousarray(np.asarray(L0_attn_Wout_b, f32))

    identf = np.eye(128, dtype=f32)
    pswap = np.zeros((DHR, DHR), f32)
    for m in range(DHR // 2):
        pswap[2 * m + 1, 2 * m] = -1.0
        pswap[2 * m, 2 * m + 1] = 1.0
    ropekc, ropeks = _rope_tables(DHR)
    ropeqc_full, ropeqs_full = _rope_tables(NH * DHR)

    nc = build_nc()
    common = {
        "x": x, "wdown": wdown,
        "bdown": np.ascontiguousarray(bdown.reshape(8, 128).T),
        "wukv": np.ascontiguousarray(np.asarray(L0_attn_W_uKV_k, f32)),
        "wuq": np.ascontiguousarray(np.asarray(L0_attn_W_uQ_k, f32)),
        "buq": np.ascontiguousarray(np.asarray(L0_attn_W_uQ_b, f32).reshape(16, 128).T),
        "wkr": np.ascontiguousarray(np.asarray(L0_attn_Wkr_k, f32)),
        "bkr": np.ascontiguousarray(np.asarray(L0_attn_Wkr_b, f32)),
        "wqr": np.ascontiguousarray(np.asarray(L0_attn_Wqr_k, f32)),
        "bqr": np.ascontiguousarray(np.asarray(L0_attn_Wqr_b, f32).reshape(16, DHR).T),
        "wout": np.ascontiguousarray(np.asarray(L0_attn_Wout_k, f32)),
        "wupT": np.ascontiguousarray(np.asarray(L0_ffn_W_up, f32).T),
        "wgateT": np.ascontiguousarray(np.asarray(L0_ffn_W_gate, f32).T),
        "wdownT": np.ascontiguousarray(np.asarray(L0_ffn_W_down, f32).T),
        "wupW": _win_rearrange(np.asarray(L0_ffn_W_up, f32)),
        "wgtW": _win_rearrange(np.asarray(L0_ffn_W_gate, f32)),
        "wdnW": _win_rearrange(np.asarray(L0_ffn_W_down, f32)),
        "identf": identf, "pswapf": pswap,
        "ropekc": np.ascontiguousarray(ropekc.T), "ropeks": np.ascontiguousarray(ropeks.T),
        "vbias_in": np.ascontiguousarray(np.broadcast_to(bukv[D:2 * D], (128, D))),
        "woutb_in": np.ascontiguousarray(np.broadcast_to(bout, (128, D))),
    }
    in_maps = []
    for j in range(NCORES):
        tiles = [8 * i + j for i in range(NSLOT)]
        rows = np.concatenate([np.arange(128 * t_, 128 * t_ + 128) for t_ in tiles])
        mk = np.zeros((NSLOT, 2, 128, 512), f32)
        for ii in range(NSLOT):
            t_ = tiles[ii]
            ci = 2 * (ii + 1)
            qglob = 128 * t_ + np.arange(128)
            for m2 in range(2):
                kt = ci - 2 + m2
                for s in range(4):
                    kg = 512 * kt + 128 * s + np.arange(128)[:, None]
                    mk[ii, m2, :, 128 * s:128 * (s + 1)] = np.where(kg <= qglob[None, :], 0.0, NEG)
        im = dict(common)
        im["own_x"] = np.ascontiguousarray(x[rows])
        im["ropeqc"] = np.ascontiguousarray(ropeqc_full[rows].T)
        im["ropeqs"] = np.ascontiguousarray(ropeqs_full[rows].T)
        im["maskadd"] = mk.reshape(NSLOT * 2 * 128, 512).astype(ml_dtypes.bfloat16)
        in_maps.append(im)

    res = run_bass_kernel_spmd(nc, in_maps, core_ids=list(range(NCORES)))
    out = np.zeros((T, D), f32)
    for j in range(NCORES):
        r = res.results[j]["out_own"]
        for ii in range(NSLOT):
            t_ = 8 * ii + j
            out[128 * t_:128 * (t_ + 1)] = r[128 * ii:128 * (ii + 1)]
    return out.reshape(1, T, D)


# version 14 (final)
# speedup vs baseline: 1.0682x; 1.0102x over previous
import math
import numpy as np
import ml_dtypes

def _ensure_device():
    import os
    if os.environ.get("TRN_TERMINAL_POOL_IPS"):
        return
    os.environ["TRN_TERMINAL_POOL_IPS"] = os.environ.get("_KERNEL_AXON_IPS", "127.0.0.1")
    os.environ.setdefault("AXON_POOL_SVC_OVERRIDE", "127.0.0.1")
    os.environ.setdefault("AXON_LOOPBACK_RELAY", "1")
    try:
        from trn_agent_boot.trn_boot import boot
        boot(os.environ["TRN_TERMINAL_PRECOMPUTED_JSON"], "/opt/axon/libaxon_pjrt.so")
    except Exception:
        pass


import concourse.bass as bass
import concourse.mybir as mybir
from concourse.bass_utils import run_bass_kernel_spmd

F32 = mybir.dt.float32
BF16 = mybir.dt.bfloat16
AF = mybir.ActivationFunctionType
ALU = mybir.AluOpType
AX = mybir.AxisListType

NCORES = 8
T = 8192
D = 2048
NH = 16
DHR = 64
LAT = 512
DE = 8192
NKT = 16
NSLOT = 8
SM_SCALE = 1.0 / math.sqrt(128.0)
BMARGIN = 30.0 * math.sqrt(128.0)
NEG = -1.0e9
LN_EPS = 1.0e-5
MEAN_DIV = 1.0 / (DE * D)

ENGS = ("T", "V", "S", "P", "Y")

DEBUG_SCAL = False
PHASES = 99


class Buf:
    __slots__ = ("name", "writer", "readers", "excl", "accessors")

    def __init__(self, name, excl=False):
        self.name = name
        self.writer = None
        self.readers = []
        self.excl = excl
        self.accessors = []


class Rec:
    __slots__ = ("key", "fn", "deps", "need_inc", "val", "sem", "inc_amt", "gidx")

    def __init__(self, key, fn):
        self.key = key
        self.fn = fn
        self.deps = []
        self.need_inc = False
        self.val = None
        self.sem = None
        self.inc_amt = 1
        self.gidx = -1


class Prog:
    def __init__(self, nc):
        self.nc = nc
        self.recs = {k: [] for k in ENGS}
        self.esem = {k: nc.alloc_semaphore("sem_" + k) for k in ENGS}
        self.dsems = {}
        self.stream_recs = {}
        self.gctr = 0

    def _dep(self, rec, dep):
        if dep is None or dep is rec:
            return
        if dep.key == rec.key and dep.sem is None and rec.key == "T":
            return
        dep.need_inc = True
        rec.deps.append(dep)

    def emit(self, key, fn, reads=(), writes=(), dma_stream=None):
        rec = Rec(key, fn)
        rec.gidx = self.gctr
        self.gctr += 1
        for b in reads:
            if b.excl:
                for (a, w) in b.accessors:
                    if w or a.key != rec.key:
                        self._dep(rec, a)
            else:
                self._dep(rec, b.writer)
        for b in writes:
            if b.excl:
                for (a, _w) in b.accessors:
                    self._dep(rec, a)
            else:
                self._dep(rec, b.writer)
                for r in b.readers:
                    self._dep(rec, r)
        if dma_stream is not None:
            if dma_stream not in self.dsems:
                self.dsems[dma_stream] = [self.nc.alloc_semaphore("dq_" + dma_stream), 0, key]
            ent = self.dsems[dma_stream]
            assert ent[2] == key, f"stream {dma_stream}: {key} vs {ent[2]}"
            ent[1] += 16
            rec.sem = dma_stream
            rec.val = ent[1]
            rec.inc_amt = 16
            rec.need_inc = True
            self.stream_recs.setdefault(dma_stream, []).append(rec)
        for b in reads:
            if b.excl:
                b.accessors.append((rec, False))
            else:
                b.readers.append(rec)
        for b in writes:
            if b.excl:
                b.accessors.append((rec, True))
            else:
                b.writer = rec
                b.readers = []
        self.recs[key].append(rec)
        return rec

    def barrier(self, keys=ENGS):
        dma_lasts = [v[-1] for v in self.stream_recs.values() if v]
        arr = {}
        for k in keys:
            rec = Rec(k, lambda eng: eng.nop(nofuse=True))
            rec.gidx = self.gctr
            self.gctr += 1
            rec.need_inc = True
            if self.recs[k]:
                prev = self.recs[k][-1]
                prev.need_inc = True
                rec.deps.append(prev)
            self.recs[k].append(rec)
            arr[k] = rec
        col = Rec("Y", lambda eng: eng.nop(nofuse=True))
        col.gidx = self.gctr
        self.gctr += 1
        col.need_inc = True
        for k in keys:
            if k != "Y":
                col.deps.append(arr[k])
        for d in dma_lasts:
            col.deps.append(d)
        self.recs["Y"].append(col)
        for k in keys:
            if k == "Y":
                continue
            rec = Rec(k, lambda eng: eng.nop(nofuse=True))
            rec.gidx = self.gctr
            self.gctr += 1
            rec.deps.append(col)
            self.recs[k].append(rec)

    def finalize_and_replay(self):
        for k in ENGS:
            cnt = 0
            for rec in self.recs[k]:
                if rec.sem is None and rec.need_inc:
                    cnt += 1
                    rec.val = cnt
        self.bumped = {}
        for k in ENGS:
            recs = self.recs[k]
            n = len(recs)
            idx = {id(r): ii for ii, r in enumerate(recs)}
            for ii, r in enumerate(recs):
                if r.sem is not None or not r.need_inc:
                    continue
                v = r.val
                jj = ii + 1
                while jj < n and not recs[jj].deps and recs[jj].sem is None:
                    if recs[jj].need_inc:
                        v = recs[jj].val
                    jj += 1
                self.bumped[id(r)] = v

        def mk_body(key):
            def body(eng):
                watermark = {}
                for rec in self.recs[key]:
                    waits = {}
                    for dep in rec.deps:
                        if dep.val is None:
                            continue
                        if dep.sem is not None:
                            sem, nm = self.dsems[dep.sem][0], "d_" + dep.sem
                            srecs = self.stream_recs[dep.sem]
                            n = 0
                            for r2 in srecs:
                                if r2.gidx < rec.gidx:
                                    n += 1
                                else:
                                    break
                            val = max(dep.val, 16 * n)
                        else:
                            sem, nm = self.esem[dep.key], "e_" + dep.key
                            val = self.bumped.get(id(dep), dep.val)
                        if waits.get(nm, (None, -1))[1] < val:
                            waits[nm] = (sem, val)
                    todo = []
                    for nm, (sem, v) in sorted(waits.items()):
                        if watermark.get(nm, -1) >= v:
                            continue
                        todo.append((sem, v))
                        watermark[nm] = v
                    for sem, v in todo[:-1]:
                        eng.wait_ge(sem, v)
                    inst = rec.fn(eng)
                    if todo:
                        sem, v = todo[-1]
                        if inst is not None:
                            inst._wait_ge(sem, v)
                        else:
                            eng.wait_ge(sem, v)
                    if rec.need_inc and inst is not None:
                        if rec.sem is not None:
                            inst.then_inc(self.dsems[rec.sem][0], 16)
                        else:
                            inst.then_inc(self.esem[rec.key], 1)
            return body

        with self.nc.Block() as block:
            block.tensor(mk_body("T"))
            block.vector(mk_body("V"))
            block.scalar(mk_body("S"))
            block.gpsimd(mk_body("P"))
            block.sync(mk_body("Y"))


class Arena:
    def __init__(self, nc, name, cols, dtype):
        self.t = nc.alloc_sbuf_tensor(name, [128, cols], dtype)
        self.free = [(0, cols)]
        self.live = {}

    def alloc(self, name, cols, align=32):
        for idx, (st, sz) in enumerate(self.free):
            a = (st + align - 1) // align * align
            if a + cols <= st + sz:
                repl = []
                if a - st > 0:
                    repl.append((st, a - st))
                if (st + sz) - (a + cols) > 0:
                    repl.append((a + cols, (st + sz) - (a + cols)))
                self.free[idx:idx + 1] = repl
                self.live[name] = (a, cols)
                return self.t[:, a:a + cols]
        raise RuntimeError(f"arena OOM: {name} needs {cols}, free={self.free}")

    def release(self, *names):
        for name in names:
            st, sz = self.live.pop(name)
            self.free.append((st, sz))
        self.free.sort()
        out = []
        for seg in self.free:
            if out and out[-1][0] + out[-1][1] == seg[0]:
                out[-1] = (out[-1][0], out[-1][1] + seg[1])
            else:
                out.append(seg)
        self.free = out


def build_nc():
    nc = bass.Bass("TRN2", target_bir_lowering=False)

    def din(name, shape, dtype=F32):
        return nc.dram_tensor(name, list(shape), dtype, kind="ExternalInput").ap()

    x = din("x", [T, D])
    own_x = din("own_x", [NSLOT * 128, D])
    wdown = din("wdown", [D, 2 * LAT])
    bdown = din("bdown", [128, 8])
    wukv = din("wukv", [LAT, 2 * D])
    wuq = din("wuq", [LAT, D])
    buq = din("buq", [128, 16])
    wkr = din("wkr", [D, DHR])
    bkr = din("bkr", [DHR])
    wqr = din("wqr", [D, NH * DHR])
    bqr = din("bqr", [DHR, 16])
    wout = din("wout", [D, D])
    wupT = din("wupT", [D, DE])
    wgateT = din("wgateT", [D, DE])
    wdownT = din("wdownT", [DE, D])
    wupW = din("wupW", [16384, 1024])
    wgtW = din("wgtW", [16384, 1024])
    wdnW = din("wdnW", [16384, 1024])
    identf = din("identf", [128, 128])
    pswapf = din("pswapf", [DHR, DHR])
    ropekc = din("ropekc", [DHR, T])
    ropeks = din("ropeks", [DHR, T])
    ropeqc = din("ropeqc", [NH * DHR, 1024])
    ropeqs = din("ropeqs", [NH * DHR, 1024])
    maskadd = din("maskadd", [NSLOT * 2 * 128, 512], BF16)
    vbias_in = din("vbias_in", [128, D])
    woutb_in = din("woutb_in", [128, D])

    out_own = nc.dram_tensor("out_own", [NSLOT * 128, D], F32, kind="ExternalOutput").ap()
    dbg_scal = nc.dram_tensor("dbg_scal", [128, 12], F32, kind="ExternalOutput").ap() if DEBUG_SCAL else None
    dbg_q = nc.dram_tensor("dbg_q", [128, 3 * 2048], F32, kind="ExternalOutput").ap() if DEBUG_SCAL else None

    kscr = nc.dram_tensor("kscr", [NH * 128 + DHR, T], BF16, kind="Internal").ap()
    vscr = nc.dram_tensor("vscr", [T, NH * 129], BF16, kind="Internal").ap()
    oscr = nc.dram_tensor("oscr", [NSLOT * 128, D], BF16, kind="Internal").ap()
    qmscr = nc.dram_tensor("qmscr", [NH * 128, NSLOT * 128], BF16, kind="Internal").ap()
    qrscr = nc.dram_tensor("qrscr", [NH * (DHR + 1), NSLOT * 128], BF16, kind="Internal").ap()
    yscr = nc.dram_tensor("yscr", [NSLOT * 128, D], F32, kind="Internal").ap()

    p = Prog(nc)
    a16 = Arena(nc, "a16", 58624, BF16)
    a32 = Arena(nc, "a32", 20480, F32)

    ident_sb = nc.alloc_sbuf_tensor("ident_sb", [128, 128], BF16).ap()
    identf_sb = nc.alloc_sbuf_tensor("identf_sb", [128, 128], F32).ap()
    pswap_sb = nc.alloc_sbuf_tensor("pswap_sb", [DHR, DHR], BF16).ap()
    ones_sb = nc.alloc_sbuf_tensor("ones_sb", [128, 128], F32).ap()
    bdown_sb = nc.alloc_sbuf_tensor("bdown_sb", [128, 8], F32).ap()
    buq_sb = nc.alloc_sbuf_tensor("buq_sb", [128, 16], F32).ap()
    bkr_sb = nc.alloc_sbuf_tensor("bkr_sb", [DHR, 1], F32).ap()
    bqr_sb = nc.alloc_sbuf_tensor("bqr_sb", [DHR, 16], F32).ap()
    kr0_sb = nc.alloc_sbuf_tensor("kr0_sb", [DHR + 1, 128], BF16).ap()
    krext_bld = nc.alloc_sbuf_tensor("krext_bld", [DHR + 1, 512], BF16).ap()
    krext_st = [nc.alloc_sbuf_tensor(f"krext_st{i}", [DHR + 1, 512], BF16).ap() for i in range(2)]
    krraw_sb2 = [nc.alloc_sbuf_tensor(f"krraw{i}_sb", [DHR, 512], BF16).ap() for i in range(2)]
    krt1_sb2 = [nc.alloc_sbuf_tensor(f"krt1{i}_sb", [DHR, 512], F32).ap() for i in range(2)]
    stats_sb = nc.alloc_sbuf_tensor("stats_sb", [128, 32], F32).ap()
    lcols_sb = nc.alloc_sbuf_tensor("lcols_sb", [128, 16], F32).ap()
    linv_sb = nc.alloc_sbuf_tensor("linv_sb", [128, 16], F32).ap()
    s0all_sb = nc.alloc_sbuf_tensor("s0all_sb", [128, NSLOT * 16], F32).ap()
    scal_sb = nc.alloc_sbuf_tensor("scal_sb", [128, 12], F32).ap()
    eps_sb = nc.alloc_sbuf_tensor("eps_sb", [128, 1], F32).ap()

    PS = nc.alloc_psum_tensor("PS", [128, 4096], F32).ap()
    ps_a = PS[:, 0:512]
    ps_b = PS[:, 512:1024]
    ps_kr = PS[0:DHR, 1024:1536]
    ps_t = PS[:, 1536:2048].bitcast(BF16)[:, 0:512]
    ps_o = PS[:, 0:3072]
    ps_s = [PS[:, 3072:3584], PS[:, 3584:4096]]

    bb = {}

    def buf(name, excl=False):
        if name not in bb:
            bb[name] = Buf(name, excl)
        return bb[name]

    b_ps_a = buf("ps_a", True)
    b_ps_b = buf("ps_b", True)
    b_ps_t = buf("ps_t", True)
    b_ps_kr = buf("ps_kr", True)
    b_ps_o = buf("ps_o", True)
    b_ps_s = [buf("ps_s0", True), buf("ps_s1", True)]

    p.emit("P", lambda g: g.dma_start(out=ident_sb, in_=identf[:, :]), writes=[buf("ident")], dma_stream="cst")
    p.emit("Y", lambda g: g.dma_start(out=identf_sb, in_=identf[:, :]), writes=[buf("identf")], dma_stream="ycst")
    p.emit("P", lambda g: g.dma_start(out=pswap_sb, in_=pswapf[:, :]), writes=[buf("pswap")], dma_stream="cst")
    p.emit("Y", lambda g: g.dma_start(out=bdown_sb, in_=bdown[:, :]),
           writes=[buf("bdown")], dma_stream="ycst")
    p.emit("Y", lambda g: g.dma_start(out=buq_sb, in_=buq[:, :]),
           writes=[buf("buq")], dma_stream="ycst")
    p.emit("Y", lambda g: g.dma_start(out=bkr_sb, in_=bkr[:, None]), writes=[buf("bkr")], dma_stream="ycst")
    p.emit("Y", lambda g: g.dma_start(out=bqr_sb, in_=bqr[:, :]),
           writes=[buf("bqr")], dma_stream="ycst")
    p.emit("V", lambda v: v.memset(ones_sb, 1.0), writes=[buf("ones")])
    p.emit("V", lambda v: v.memset(eps_sb, LN_EPS), writes=[buf("eps")])
    p.emit("V", lambda v: v.memset(krext_bld[DHR:DHR + 1, :], 1.0), writes=[buf("krextd")])
    for par in range(2):
        p.emit("V", lambda v, par=par: v.memset(krext_st[par][DHR:DHR + 1, :], 1.0),
               writes=[buf(f"krst{par}")])

    wdown_sb = a16.alloc("wdown", 16 * 512)
    wukv_sb = a16.alloc("wukv", 4 * 4096)
    xnT_sb = a16.alloc("xnT", 16 * 512)
    KT_sb = a16.alloc("KT", NH * 512)
    V_sb = a16.alloc("Vb", 4 * 2064)
    xntok_sb = a16.alloc("xntok", D)
    ckvT_sb = a16.alloc("ckvT", 4 * 512)
    wkr_sb = a16.alloc("wkr", 16 * DHR)
    junk_sb = a16.alloc("junk", 2048)
    k0_sb = a16.alloc("k0", NH * 128)
    xin_sb = a32.alloc("xin", 3 * D)
    mean_sb = a32.alloc("meanbuf", 6 * 1024)
    hcoll_sb = a32.alloc("hcoll", 128)
    ht_sb = a32.alloc("ht", 128)
    h2_sb = a32.alloc("h2", 1024)
    l2r_sb = a32.alloc("l2r", 64)
    brow_sb = a32.alloc("brow", 1024)[0:NH, :]
    ropek_sb = a32.alloc("ropek", 1024)[0:DHR, :]
    ropeq_sb = a32.alloc("ropeq", 1024)[0:DHR, :]
    ropeq2_sb = a32.alloc("ropeq2", 1024)[0:DHR, :]

    p.emit("P", lambda g: g.dma_start(out=wdown_sb.rearrange("p (c l) -> p c l", l=512),
                                      in_=wdown[:, 0:512].rearrange("(c p) l -> p c l", p=128)),
           writes=[buf("wdown")], dma_stream="wA")
    p.emit("P", lambda g: g.dma_start(out=wukv_sb.rearrange("p (c l) -> p c l", l=4096),
                                      in_=wukv.rearrange("(c p) l -> p c l", p=128)),
           writes=[buf("wukv")], dma_stream="wA")
    p.emit("P", lambda g: g.dma_start(out=wkr_sb.rearrange("p (c l) -> p c l", l=DHR),
                                      in_=wkr.rearrange("(c p) l -> p c l", p=128)),
           writes=[buf("wkr")], dma_stream="wA")
    for s in range(4):
        p.emit("V", lambda v, s=s: v.memset(V_sb[:, 2064 * s + 128:2064 * (s + 1):129], 1.0),
               writes=[buf(f"Vb{s}")])

    def ln_stats(src_ap, b_src, slot):
        base = 8 * slot
        s1 = stats_sb[:, base:base + 1]
        s2 = stats_sb[:, base + 1:base + 2]
        bs = buf(f"lnstats{slot}")
        p.emit("V", lambda v: v.reduce_sum(s1, src_ap, AX.X), reads=[b_src], writes=[bs])
        p.emit("S", lambda sc: sc.activation(junk_sb, src_ap, AF.Square, accum_out=s2),
               reads=[b_src], writes=[bs, buf("junk")])

    def ln_math(slot):
        base = 8 * slot
        s1 = stats_sb[:, base:base + 1]
        s2 = stats_sb[:, base + 1:base + 2]
        mm = stats_sb[:, base + 2:base + 3]
        tmp = stats_sb[:, base + 3:base + 4]
        var = stats_sb[:, base + 4:base + 5]
        inv = stats_sb[:, base + 5:base + 6]
        ngm = stats_sb[:, base + 6:base + 7]
        bs = buf(f"lnstats{slot}")
        p.emit("V", lambda v: v.tensor_scalar(mm, s1, 1.0 / D, None, ALU.mult), reads=[bs], writes=[bs])
        p.emit("V", lambda v: v.tensor_tensor(tmp, mm, mm, ALU.mult), reads=[bs], writes=[bs])
        p.emit("V", lambda v: v.scalar_tensor_tensor(var, s2, 1.0 / D, tmp, ALU.mult, ALU.subtract),
               reads=[bs], writes=[bs])
        p.emit("S", lambda sc: sc.activation(tmp, var, AF.Ln, bias=eps_sb), reads=[bs, buf("eps")], writes=[bs])
        p.emit("S", lambda sc: sc.activation(inv, tmp, AF.Exp, scale=-0.5), reads=[bs], writes=[bs])
        p.emit("V", lambda v: v.scalar_tensor_tensor(ngm, mm, -1.0, inv, ALU.mult, ALU.mult),
               reads=[bs], writes=[bs])

    def ln_apply(src_ap, b_src, dst_ap, b_dst, slot):
        base = 8 * slot
        inv = stats_sb[:, base + 5:base + 6]
        ngm = stats_sb[:, base + 6:base + 7]
        bs = buf(f"lnstats{slot}")
        p.emit("S", lambda sc: sc.activation(dst_ap, src_ap, AF.Identity, bias=ngm, scale=inv),
               reads=[b_src, bs], writes=[b_dst])

    def ln_tile(src_ap, b_src, dst_ap=None, b_dst=None, out_f32=None, b_outf=None, slot=0):
        ln_stats(src_ap, b_src, slot)
        ln_math(slot)
        if dst_ap is not None:
            ln_apply(src_ap, b_src, dst_ap, b_dst, slot)
        if out_f32 is not None:
            ln_apply(src_ap, b_src, out_f32, b_outf, slot)

    def transpose_2048(src_ap, b_src, dst_ap_fn, b_dst):
        for g in range(4):
            for c in range(4):
                cc = 4 * g + c
                p.emit("T", lambda t, cc=cc, c=c: t.transpose(
                    ps_t[:, 128 * c:128 * (c + 1)], src_ap[:, 128 * cc:128 * (cc + 1)], ident_sb),
                    reads=[b_src, buf("ident")], writes=[b_ps_t])
            p.emit("V", lambda v, g=g: v.tensor_copy(
                dst_ap_fn(g), ps_t.rearrange("p (c q) -> p c q", q=128)),
                reads=[b_ps_t], writes=[b_dst])

    def rope_combine(par, psum2, b_psum2, bias_ap, b_bias, cos_ap, sin_ap, b_tab, out_ap, b_out):
        kr_raw, kr_t1 = krraw_sb2[par], krt1_sb2[par]
        braw, bt1 = buf(f"krraw{par}"), buf(f"krt1{par}")
        p.emit("S", lambda sc: sc.activation(kr_raw, psum2, AF.Identity, bias=bias_ap),
               reads=[b_psum2, b_bias], writes=[braw])
        p.emit("T", lambda t: t.matmul(psum2, pswap_sb, kr_raw, start=True, stop=True),
               reads=[buf("pswap"), braw], writes=[b_psum2])
        p.emit("V", lambda v: v.tensor_tensor(kr_t1, kr_raw, cos_ap, ALU.mult),
               reads=[braw, b_tab], writes=[bt1])
        p.emit("V", lambda v: v.tensor_tensor(psum2, psum2, sin_ap, ALU.mult),
               reads=[b_psum2, b_tab], writes=[b_psum2])
        p.emit("V", lambda v: v.tensor_tensor(out_ap, kr_t1, psum2, ALU.add),
               reads=[bt1, b_psum2], writes=[b_out])

    mean_jobs = [(which, w_ap, t) for which, w_ap in ((0, wupW), (1, wgtW), (2, wdnW))
                 for t in range(128)]
    mean_idx = [0]
    SCOL = {0: (0, 1), 1: (2, 3), 2: (4, 5)}

    def finalize_mean(which):
        bh = buf("hcoll")
        p.emit("T", lambda t: t.transpose(PS[:, 0:128], hcoll_sb, identf_sb),
               reads=[bh, buf("identf")], writes=[b_ps_a, b_ps_o])
        p.emit("V", lambda v: v.tensor_copy(ht_sb, PS[:, 0:128]),
               reads=[b_ps_a, b_ps_o], writes=[buf("ht")])
        for w2 in range(16):
            p.emit("P", lambda g, w2=w2: g.dma_start(
                out=h2_sb[w2:w2 + 1, :], in_=ht_sb[8 * w2:8 * w2 + 8, :]),
                reads=[buf("ht")], writes=[buf("h2")], dma_stream="mfl")
        p.emit("S", lambda sc: sc.activation(junk_sb[0:16, 0:1024], h2_sb[0:16, :],
                                             AF.Abs, accum_out=l2r_sb[0:16, 0:1]),
               reads=[buf("h2")], writes=[buf("l2c"), buf("junk")])
        p.emit("P", lambda g: g.dma_start(out=l2r_sb[0:1, 16:32], in_=l2r_sb[0:16, 0:1]),
               reads=[buf("l2c")], writes=[buf("l2r")], dma_stream="mfl")
        p.emit("V", lambda v: v.tensor_tensor(l2r_sb[0:1, 32:40], l2r_sb[0:1, 16:24],
                                              l2r_sb[0:1, 24:32], ALU.add),
               reads=[buf("l2r")], writes=[buf("l2r")])
        p.emit("V", lambda v: v.tensor_tensor(l2r_sb[0:1, 40:44], l2r_sb[0:1, 32:36],
                                              l2r_sb[0:1, 36:40], ALU.add),
               reads=[buf("l2r")], writes=[buf("l2r")])
        p.emit("V", lambda v: v.tensor_tensor(l2r_sb[0:1, 44:46], l2r_sb[0:1, 40:42],
                                              l2r_sb[0:1, 42:44], ALU.add),
               reads=[buf("l2r")], writes=[buf("l2r")])
        p.emit("V", lambda v: v.tensor_tensor(l2r_sb[0:1, 46:47], l2r_sb[0:1, 44:45],
                                              l2r_sb[0:1, 45:46], ALU.add),
               reads=[buf("l2r")], writes=[buf("l2r")])
        p.emit("V", lambda v: v.tensor_scalar(l2r_sb[0:1, 47:48], l2r_sb[0:1, 46:47],
                                              float(np.float32(1.0 / 16777216.0)),
                                              float(np.float32(1.0e-8)), ALU.mult, ALU.add),
               reads=[buf("l2r")], writes=[buf("l2r")])
        p.emit("V", lambda v: v.reciprocal(l2r_sb[0:1, 48:49], l2r_sb[0:1, 47:48]),
               reads=[buf("l2r")], writes=[buf("l2r")])
        cs, cr = SCOL[which]
        p.emit("T", lambda t: t.matmul(PS[:, 0:2], ones_sb[0:1, :],
                                       l2r_sb[0:1, 47:49], start=True, stop=True),
               reads=[buf("l2r"), buf("ones")], writes=[b_ps_a, b_ps_o])
        p.emit("V", lambda v, cs=cs, cr=cr: v.tensor_copy(
            scal_sb[:, cs:cs + 1], PS[:, 0:1]), reads=[b_ps_a, b_ps_o], writes=[buf("scal")])
        p.emit("V", lambda v, cs=cs, cr=cr: v.tensor_copy(
            scal_sb[:, cr:cr + 1], PS[:, 1:2]), reads=[b_ps_a, b_ps_o], writes=[buf("scal")])

    def pump_mean(n, fin_sink=None):
        for _ in range(n):
            if fin_sink:
                return
            if mean_idx[0] >= len(mean_jobs):
                return
            which, w_ap, t = mean_jobs[mean_idx[0]]
            mean_idx[0] += 1
            par = mean_idx[0] % 6
            mb = mean_sb[:, 1024 * par:1024 * (par + 1)]
            bmb = buf(f"meanbuf{par}")
            p.emit("Y", lambda y, w_ap=w_ap, t=t, mb=mb: y.dma_start(
                out=mb, in_=w_ap[128 * t:128 * (t + 1), :]),
                writes=[bmb], dma_stream=f"mean{par}")
            p.emit("S", lambda sc, t=t, mb=mb: sc.activation(
                junk_sb[:, 0:1024], mb, AF.Abs, accum_out=hcoll_sb[:, t:t + 1]),
                reads=[bmb], writes=[buf("hcoll"), buf("junk")])
            if t == 127:
                if fin_sink is not None:
                    fin_sink.append(which)
                else:
                    finalize_mean(which)

    def phase_a():
        b_rk = buf("ropek")
        for kt in range(NKT):
            p.emit("Y", lambda y, kt=kt: y.dma_start(
                out=ropek_sb[:, 0:512], in_=ropekc[:, 512 * kt:512 * (kt + 1)]),
                writes=[b_rk], dma_stream="rk")
            p.emit("Y", lambda y, kt=kt: y.dma_start(
                out=ropek_sb[:, 512:1024], in_=ropeks[:, 512 * kt:512 * (kt + 1)]),
                writes=[b_rk], dma_stream="rk")
            for sub in range(4):
                par = (4 * kt + sub) % 3
                xb = buf(f"xin{par}")
                xa = xin_sb[:, D * par:D * (par + 1)]
                row0 = 512 * kt + 128 * sub
                p.emit("Y", lambda y, row0=row0, xa=xa: y.dma_start(
                    out=xa, in_=x[row0:row0 + 128, :]), writes=[xb], dma_stream=f"xA{par}")
                ln_stats(xa, xb, sub % 2)
                ln_math(sub % 2)
                ln_apply(xa, xb, xntok_sb, buf("xntok"), sub % 2)
                pump_mean(3)

                def dst_fn(g, sub=sub):
                    return xnT_sb.rearrange("p (c q) -> p c q", q=512)[:, 4 * g:4 * g + 4,
                                                                      128 * sub:128 * (sub + 1)]
                transpose_2048(xntok_sb, buf("xntok"), dst_fn, buf("xnT"))
            for lc in range(4):
                pp, bpp = (ps_a, b_ps_a) if lc % 2 == 0 else (ps_b, b_ps_b)
                for dc in range(16):
                    p.emit("T", lambda t, lc=lc, dc=dc, pp=pp: t.matmul(
                        pp, wdown_sb[:, 512 * dc + 128 * lc:512 * dc + 128 * (lc + 1)],
                        xnT_sb[:, 512 * dc:512 * (dc + 1)],
                        start=(dc == 0), stop=(dc == 15)),
                        reads=[buf("wdown"), buf("xnT")], writes=[bpp])
                p.emit("S", lambda s, lc=lc, pp=pp: s.activation(
                    ckvT_sb[:, 512 * lc:512 * (lc + 1)], pp, AF.Identity,
                    bias=bdown_sb[:, lc:lc + 1]),
                    reads=[bpp, buf("bdown")], writes=[buf("ckvT")])
            for h in range(NH):
                pp, bpp = (ps_a, b_ps_a) if h % 2 == 0 else (ps_b, b_ps_b)
                for rc in range(4):
                    p.emit("T", lambda t, h=h, rc=rc, pp=pp: t.matmul(
                        pp, wukv_sb[:, 4096 * rc + 128 * h:4096 * rc + 128 * (h + 1)],
                        ckvT_sb[:, 512 * rc:512 * (rc + 1)],
                        start=(rc == 0), stop=(rc == 3)),
                        reads=[buf("wukv"), buf("ckvT")], writes=[bpp])
                p.emit("S", lambda s, h=h, pp=pp: s.copy(KT_sb[:, 512 * h:512 * (h + 1)], pp),
                       reads=[bpp], writes=[buf(f"KT{h}")])
                p.emit("P", lambda g, kt=kt, h=h: g.dma_start(
                    out=kscr[0:NH * 128, :].rearrange("(h p) c -> p h c", p=128)
                    [:, h:h + 1, 512 * kt:512 * (kt + 1)],
                    in_=KT_sb.rearrange("p (h c) -> p h c", c=512)[:, h:h + 1, :]),
                    reads=[buf(f"KT{h}")], dma_stream=f"spK{h % 2}")
            for sub in range(4):
                for q4 in range(4):
                    pp, bpp = (ps_a, b_ps_a) if (sub * 4 + q4) % 2 == 0 else (ps_b, b_ps_b)
                    for rc in range(4):
                        p.emit("T", lambda t, sub=sub, q4=q4, rc=rc, pp=pp: t.matmul(
                            pp, ckvT_sb[:, 512 * rc + 128 * sub:512 * rc + 128 * (sub + 1)],
                            wukv_sb[:, 4096 * rc + 2048 + 512 * q4:4096 * rc + 2048 + 512 * (q4 + 1)],
                            start=(rc == 0), stop=(rc == 3)),
                            reads=[buf("wukv"), buf("ckvT")], writes=[bpp])
                    p.emit("V", lambda v, sub=sub, q4=q4, pp=pp: v.tensor_copy(
                        V_sb.rearrange("p (s c) -> p s c", c=2064)[:, sub:sub + 1, :]
                        .rearrange("p s (h c) -> p (s h) c", c=129)[:, 4 * q4:4 * (q4 + 1), 0:128],
                        pp.rearrange("p (h c) -> p h c", c=128)),
                        reads=[bpp], writes=[buf(f"Vb{sub}")])
                p.emit("P", lambda g, kt=kt, sub=sub: g.dma_start(
                    out=vscr.rearrange("(n p) c -> p n c", p=128)
                    [:, 4 * kt + sub:4 * kt + sub + 1, :],
                    in_=V_sb.rearrange("p (s c) -> p s c", c=2064)[:, sub:sub + 1, :]),
                    reads=[buf(f"Vb{sub}")], dma_stream=f"spV{sub % 2}")
            for dc in range(16):
                p.emit("T", lambda t, dc=dc: t.matmul(
                    ps_kr, wkr_sb[:, DHR * dc:DHR * (dc + 1)], xnT_sb[:, 512 * dc:512 * (dc + 1)],
                    start=(dc == 0), stop=(dc == 15)),
                    reads=[buf("wkr"), buf("xnT")], writes=[b_ps_kr])
            rope_combine(kt % 2, ps_kr, b_ps_kr, bkr_sb, buf("bkr"),
                         ropek_sb[:, 0:512], ropek_sb[:, 512:1024], b_rk,
                         krext_bld[0:DHR, :], buf("krextd"))
            if kt == 0:
                p.emit("V", lambda v: v.tensor_copy(
                    k0_sb.rearrange("p (h c) -> p h c", c=128),
                    KT_sb.rearrange("p (h c) -> p h c", c=512)[:, :, 0:128]),
                    reads=[buf(f"KT{h}") for h in range(NH)], writes=[buf("k0")])
                p.emit("V", lambda v: v.tensor_copy(kr0_sb[0:DHR, :], krext_bld[0:DHR, 0:128]),
                       reads=[buf("krextd")], writes=[buf("kr0")])
                p.emit("V", lambda v: v.memset(kr0_sb[DHR:DHR + 1, :], 1.0), writes=[buf("kr0")])
            p.emit("P", lambda g, kt=kt: g.dma_start(
                out=kscr[NH * 128:NH * 128 + DHR, 512 * kt:512 * (kt + 1)],
                in_=krext_bld[0:DHR, :]),
                reads=[buf("krextd")], dma_stream="spK0")

    def phase_a2():
        a16.release("wdown", "wukv", "xnT", "KT", "Vb", "ckvT", "xntok", "wkr")
        ownxnT = a16.alloc("ownxnT", 16 * 1024)
        qtok = a16.alloc("qtok", D)
        b_own = buf("ownxnT")
        for i in range(NSLOT):
            par = i % 2
            xb = buf(f"xin{par}")
            xa = xin_sb[:, D * par:D * (par + 1)]
            p.emit("Y", lambda y, i=i, xa=xa: y.dma_start(
                out=xa, in_=own_x[128 * i:128 * (i + 1), :]), writes=[xb], dma_stream=f"xq{par}")
            ln_tile(xa, xb, dst_ap=qtok, b_dst=buf("qtok"), slot=i % 2)

            def dst_fn(g, i=i):
                return ownxnT.rearrange("p (c q) -> p c q", q=1024)[:, 4 * g:4 * g + 4,
                                                                   128 * i:128 * (i + 1)]
            transpose_2048(qtok, buf("qtok"), dst_fn, b_own)
        wqr_sb = a16.alloc("wqr", 16 * 1024)
        qrope = a16.alloc("qrope", NH * 1024)
        p.emit("P", lambda g: g.dma_start(out=wqr_sb.rearrange("p (c l) -> p c l", l=1024),
                                          in_=wqr.rearrange("(c p) l -> p c l", p=128)),
               writes=[buf("wqr")], dma_stream="wA")
        rq2 = [buf("ropeq0"), buf("ropeq1")]
        for h in range(NH):
            for th in range(2):
                par = (h * 2 + th) % 2
                pk = ps_kr if par == 0 else ps_a[0:DHR, :]
                bpk = b_ps_kr if par == 0 else b_ps_a
                rqa = ropeq_sb[:, 1024 * 0 + 512 * par:512 * par + 512] if False else None
                ca = ropeq_sb[:, 512 * par:512 * (par + 1)]
                p.emit("Y", lambda y, h=h, th=th, par=par: y.dma_start(
                    out=ropeq_sb[:, 512 * par:512 * (par + 1)].rearrange("p c -> p c"),
                    in_=ropeqc[DHR * h:DHR * (h + 1), 512 * th:512 * (th + 1)]),
                    writes=[rq2[par]], dma_stream=f"rq{par}")
                p.emit("Y", lambda y, h=h, th=th, par=par: y.dma_start(
                    out=ropeq2_sb[:, 512 * par:512 * (par + 1)],
                    in_=ropeqs[DHR * h:DHR * (h + 1), 512 * th:512 * (th + 1)]),
                    writes=[rq2[par]], dma_stream=f"rq{par}")
                for dc in range(16):
                    p.emit("T", lambda t, h=h, th=th, dc=dc, pk=pk: t.matmul(
                        pk, wqr_sb[:, 1024 * dc + DHR * h:1024 * dc + DHR * (h + 1)],
                        ownxnT[:, 1024 * dc + 512 * th:1024 * dc + 512 * (th + 1)],
                        start=(dc == 0), stop=(dc == 15)),
                        reads=[buf("wqr"), b_own], writes=[bpk])
                rope_combine(par, pk, bpk, bqr_sb[:, h:h + 1], buf("bqr"),
                             ropeq_sb[:, 512 * par:512 * (par + 1)],
                             ropeq2_sb[:, 512 * par:512 * (par + 1)], rq2[par],
                             qrope[0:DHR, 1024 * h + 512 * th:1024 * h + 512 * (th + 1)],
                             buf("qrope"))
        p.barrier()
        a16.release("wqr", "qtok")
        wdq_sb = a16.alloc("wdownq", 16 * 512)
        cqT_sb = a16.alloc("cqT", 4 * 1024)
        p.emit("P", lambda g: g.dma_start(out=wdq_sb.rearrange("p (c l) -> p c l", l=512),
                                          in_=wdown[:, 512:1024].rearrange("(c p) l -> p c l", p=128)),
               writes=[buf("wdownq")], dma_stream="wA")
        for lc in range(4):
            for th in range(2):
                pp, bpp = (ps_a, b_ps_a) if (lc * 2 + th) % 2 == 0 else (ps_b, b_ps_b)
                for dc in range(16):
                    p.emit("T", lambda t, lc=lc, th=th, dc=dc, pp=pp: t.matmul(
                        pp, wdq_sb[:, 512 * dc + 128 * lc:512 * dc + 128 * (lc + 1)],
                        ownxnT[:, 1024 * dc + 512 * th:1024 * dc + 512 * (th + 1)],
                        start=(dc == 0), stop=(dc == 15)),
                        reads=[buf("wdownq"), b_own], writes=[bpp])
                p.emit("S", lambda s, lc=lc, th=th, pp=pp: s.activation(
                    cqT_sb[:, 1024 * lc + 512 * th:1024 * lc + 512 * (th + 1)], pp,
                    AF.Identity, bias=bdown_sb[:, 4 + lc:5 + lc]),
                    reads=[bpp, buf("bdown")], writes=[buf("cqT")])
        p.barrier()
        a16.release("ownxnT", "wdownq")
        wuq_sb = a16.alloc("wuq", 4 * 2048)
        qmainT = a16.alloc("qmainT", NH * 1024)
        p.emit("P", lambda g: g.dma_start(out=wuq_sb.rearrange("p (c l) -> p c l", l=2048),
                                          in_=wuq.rearrange("(c p) l -> p c l", p=128)),
               writes=[buf("wuq")], dma_stream="wA")
        for h in range(NH):
            for th in range(2):
                pp, bpp = (ps_a, b_ps_a) if (h * 2 + th) % 2 == 0 else (ps_b, b_ps_b)
                for rc in range(4):
                    p.emit("T", lambda t, h=h, th=th, rc=rc, pp=pp: t.matmul(
                        pp, wuq_sb[:, 2048 * rc + 128 * h:2048 * rc + 128 * (h + 1)],
                        cqT_sb[:, 1024 * rc + 512 * th:1024 * rc + 512 * (th + 1)],
                        start=(rc == 0), stop=(rc == 3)),
                        reads=[buf("wuq"), buf("cqT")], writes=[bpp])
                p.emit("S", lambda s, h=h, th=th, pp=pp: s.activation(
                    qmainT[:, 1024 * h + 512 * th:1024 * h + 512 * (th + 1)], pp,
                    AF.Identity, bias=buq_sb[:, h:h + 1]),
                    reads=[bpp, buf("buq")], writes=[buf("qmainT")])
        p.barrier()
        a16.release("wuq", "cqT")
        a32.release("ropek", "ropeq", "ropeq2")
        return qmainT, qrope

    def phase_a3(qmainT, qrope):
        for i in range(NSLOT):
            for h in range(NH):
                pp, bpp = ps_s[h % 2], b_ps_s[h % 2]
                p.emit("T", lambda t, i=i, h=h, pp=pp: t.matmul(
                    pp[:, 0:128], qmainT[:, 1024 * h + 128 * i:1024 * h + 128 * (i + 1)],
                    k0_sb[:, 128 * h:128 * (h + 1)], start=True, stop=False),
                    reads=[buf("qmainT"), buf("k0")], writes=[bpp])
                p.emit("T", lambda t, i=i, h=h, pp=pp: t.matmul(
                    pp[:, 0:128], qrope[0:DHR, 1024 * h + 128 * i:1024 * h + 128 * (i + 1)],
                    kr0_sb[0:DHR, :], start=False, stop=True),
                    reads=[buf("qrope"), buf("kr0")], writes=[bpp])
                p.emit("V", lambda v, i=i, h=h, pp=pp: v.reduce_max(
                    s0all_sb[:, 16 * i + h:16 * i + h + 1], pp[:, 0:128], AX.X),
                    reads=[bpp], writes=[buf("s0all")])
        for i in range(NSLOT):
            p.emit("T", lambda t, i=i: t.transpose(
                ps_a[0:16, 0:128], s0all_sb[:, 16 * i:16 * (i + 1)], identf_sb),
                reads=[buf("s0all"), buf("identf")], writes=[b_ps_a])
            p.emit("V", lambda v, i=i: v.tensor_scalar(
                brow_sb[:, 128 * i:128 * (i + 1)], ps_a[0:NH, 0:128], BMARGIN, -1.0,
                ALU.add, ALU.mult),
                reads=[b_ps_a], writes=[buf("brow")])
        p.emit("P", lambda g: g.dma_start(
            out=qrope[DHR:DHR + 1, :], in_=brow_sb),
            reads=[buf("brow")], writes=[buf("qrope")], dma_stream="brw")
        p.emit("P", lambda g: g.dma_start(
            out=qmscr.rearrange("(h p) c -> p h c", p=128),
            in_=qmainT.rearrange("p (h c) -> p h c", c=1024)),
            reads=[buf("qmainT")], dma_stream="spQ")
        p.emit("P", lambda g: g.dma_start(
            out=qrscr.rearrange("(h p) c -> p h c", p=DHR + 1),
            in_=qrope[0:DHR + 1, :].rearrange("p (h c) -> p h c", c=1024)),
            reads=[buf("qrope")], dma_stream="spQ")
        a16.release("k0", "qmainT", "qrope")

    def phase_b():
        KTs = [a16.alloc(f"KTs{par}", NH * 512) for par in range(2)]
        Vs = [a16.alloc(f"Vs{par}", 4 * 2064) for par in range(2)]
        msk = a16.alloc("msk", 2 * 1024)
        pT = [a16.alloc(f"pTb{par}", 512) for par in range(2)]
        Osb = a16.alloc("Osb", D)
        qm = [a16.alloc(f"qm{par}", NH * 128) for par in range(2)]
        qr = [a16.alloc(f"qr{par}", NH * 128) for par in range(2)]
        a32.release("brow")
        vbias_sb = a32.alloc("vbias", D)
        p.emit("Y", lambda y: y.dma_start(out=vbias_sb, in_=vbias_in[:, :]),
               writes=[buf("vbias")], dma_stream="ycst")
        visit = 0
        pv_pending = [None]
        fin_pend = []

        def slot_inputs(i):
            mpar = i % 2
            p.emit("Y", lambda y, i=i, mpar=mpar: y.dma_start(
                out=msk[:, 1024 * mpar:1024 * (mpar + 1)].rearrange("p (m c) -> p m c", c=512),
                in_=maskadd[256 * i:256 * (i + 1), :].rearrange("(m p) c -> p m c", p=128)),
                writes=[buf(f"msk{mpar}")], dma_stream=f"mk{mpar}")
            p.emit("Y", lambda y, i=i, mpar=mpar: y.dma_start(
                out=qm[mpar].rearrange("p (h c) -> p h c", c=128),
                in_=qmscr.rearrange("(h p) c -> p h c", p=128)[:, :, 128 * i:128 * (i + 1)]),
                writes=[buf(f"qm{mpar}")], dma_stream=f"qm{mpar}")
            p.emit("Y", lambda y, i=i, mpar=mpar: y.dma_start(
                out=qr[mpar][0:DHR + 1, :].rearrange("p (h c) -> p h c", c=128),
                in_=qrscr.rearrange("(h p) c -> p h c", p=DHR + 1)[:, :, 128 * i:128 * (i + 1)]),
                writes=[buf(f"qr{mpar}")], dma_stream=f"qr{mpar}")

        slot_inputs(0)
        slot_inputs(1)
        for i in range(NSLOT):
            ci = 2 * (i + 1)
            mpar = i % 2
            bmk = buf(f"msk{mpar}")
            bqm = buf(f"qm{mpar}")
            bqr2 = buf(f"qr{mpar}")
            for kt in range(ci):
                par = visit % 2
                bK = buf(f"KTs{par}")
                bV = buf(f"Vs{par}")
                bkrs = buf(f"krst{par}")
                p.emit("Y", lambda y, kt=kt, par=par: y.dma_start(
                    out=KTs[par].rearrange("p (h c) -> p h c", c=512),
                    in_=kscr[0:NH * 128, :].rearrange("(h p) c -> p h c", p=128)[:, :, 512 * kt:512 * (kt + 1)]),
                    writes=[bK], dma_stream=f"kv{par}")
                p.emit("Y", lambda y, kt=kt, par=par: y.dma_start(
                    out=krext_st[par][0:DHR, :],
                    in_=kscr[NH * 128:NH * 128 + DHR, 512 * kt:512 * (kt + 1)]),
                    writes=[bkrs], dma_stream=f"kv{par}")
                p.emit("Y", lambda y, kt=kt, par=par: y.dma_start(
                    out=Vs[par].rearrange("p (s c) -> p s c", c=2064),
                    in_=vscr.rearrange("(n p) c -> p n c", p=128)[:, 4 * kt:4 * (kt + 1), :]),
                    writes=[bV], dma_stream=f"kv{par}")
                masked = kt >= ci - 2
                for h in range(NH):
                    sp, bsp = ps_s[h % 2], b_ps_s[h % 2]
                    ppar = h % 2
                    bpT = buf(f"pT{ppar}")
                    for s in range(4):
                        p.emit("T", lambda t, i=i, h=h, s=s, par=par, sp=sp, mpar=mpar: t.matmul(
                            sp[:, 128 * s:128 * (s + 1)],
                            KTs[par][:, 512 * h + 128 * s:512 * h + 128 * (s + 1)],
                            qm[mpar][:, 128 * h:128 * (h + 1)],
                            start=True, stop=False),
                            reads=[bK, bqm], writes=[bsp])
                        p.emit("T", lambda t, i=i, h=h, s=s, par=par, sp=sp, mpar=mpar: t.matmul(
                            sp[:, 128 * s:128 * (s + 1)],
                            krext_st[par][:, 128 * s:128 * (s + 1)],
                            qr[mpar][0:DHR + 1, 128 * h:128 * (h + 1)],
                            start=False, stop=True),
                            reads=[bkrs, bqr2], writes=[bsp])
                    if masked:
                        mi = kt - (ci - 2)
                        p.emit("V", lambda v, mi=mi, mpar=mpar, sp=sp: v.tensor_tensor(
                            sp, sp, msk[:, 1024 * mpar + 512 * mi:1024 * mpar + 512 * (mi + 1)],
                            ALU.add),
                            reads=[bmk, bsp], writes=[bsp])
                    p.emit("S", lambda s_, ppar=ppar, sp=sp: s_.activation(
                        pT[ppar], sp, AF.Exp, scale=SM_SCALE),
                        reads=[bsp], writes=[bpT])
                    if pv_pending[0] is not None:
                        pv_pending[0]()
                    bnk = h // 3
                    sslot = h % 3
                    ob = 512 * bnk + 129 * sslot
                    last_h = min(3 * bnk + 2, NH - 1)

                    def mk_pv(kt=kt, h=h, ppar=ppar, par=par, ob=ob, bnk=bnk,
                              sslot=sslot, last_h=last_h, ci=ci, bpT=bpT, bV=bV):
                        for s in range(4):
                            first = (kt == 0 and sslot == 0 and s == 0)
                            last = (kt == ci - 1 and h == last_h and s == 3)
                            p.emit("T", lambda t, s=s, ppar=ppar, par=par, h=h, ob=ob,
                                   first=first, last=last: t.matmul(
                                ps_o[:, ob:ob + 129],
                                pT[ppar][:, 128 * s:128 * (s + 1)],
                                Vs[par][:, 2064 * s + 129 * h:2064 * s + 129 * (h + 1)],
                                start=first, stop=last),
                                reads=[bpT, bV], writes=[b_ps_o])
                    pv_pending[0] = mk_pv
                visit += 1
                pump_mean(2, fin_pend)
            if pv_pending[0] is not None:
                pv_pending[0]()
                pv_pending[0] = None
            p.emit("V", lambda v: v.tensor_copy(
                lcols_sb[:, 0:15].rearrange("p (b s) -> p b s", s=3),
                ps_o.rearrange("p (b c) -> p b c", c=512)[:, 0:5, 128:388:129]),
                reads=[b_ps_o], writes=[buf("lcols")])
            p.emit("V", lambda v: v.tensor_copy(lcols_sb[:, 15:16],
                                                ps_o[:, 512 * 5 + 128:512 * 5 + 129]),
                   reads=[b_ps_o], writes=[buf("lcols")])
            p.emit("V", lambda v: v.reciprocal(linv_sb, lcols_sb),
                   reads=[buf("lcols")], writes=[buf("linv")])
            for h in range(NH):
                ob = 512 * (h // 3) + 129 * (h % 3)
                p.emit("V", lambda v, h=h, ob=ob: v.scalar_tensor_tensor(
                    Osb[:, 128 * h:128 * (h + 1)], ps_o[:, ob:ob + 128], linv_sb[:, h:h + 1],
                    vbias_sb[:, 128 * h:128 * (h + 1)], ALU.mult, ALU.add),
                    reads=[b_ps_o, buf("linv"), buf("vbias")], writes=[buf("Osb")])
            p.emit("P", lambda g, i=i: g.dma_start(out=oscr[128 * i:128 * (i + 1), :], in_=Osb),
                   reads=[buf("Osb")], dma_stream="spO")
            for w in fin_pend:
                finalize_mean(w)
            fin_pend.clear()
            if i + 2 < NSLOT:
                slot_inputs(i + 2)
        a16.release("KTs0", "KTs1", "Vs0", "Vs1", "msk", "pTb0", "pTb1", "Osb",
                    "qm0", "qm1", "qr0", "qr1")
        a32.release("vbias")

    def phase_c():
        wout_sb = a16.alloc("wout", 16 * 2048)
        Oin = a16.alloc("Oin", D)
        OT = a16.alloc("OT", D)
        xn2T = a16.alloc("xn2T", 16 * 1024)
        xn2tok = a16.alloc("xn2tok", D)
        att_sb = a32.alloc("att", D)
        yt_sb = a32.alloc("yt", D)
        woutb_sb = a32.alloc("woutb", D)
        p.emit("Y", lambda y: y.dma_start(out=woutb_sb, in_=woutb_in[:, :]),
               writes=[buf("woutb")], dma_stream="ycst")
        p.emit("P", lambda g: g.dma_start(out=wout_sb.rearrange("p (c l) -> p c l", l=2048),
                                          in_=wout.rearrange("(c p) l -> p c l", p=128)),
               writes=[buf("wout")], dma_stream="wA")
        xt_pending = [None]
        for i in range(NSLOT):
            par = i % 2
            bO = buf("Oin")
            p.emit("Y", lambda y, i=i: y.dma_start(
                out=Oin, in_=oscr[128 * i:128 * (i + 1), :]), writes=[bO], dma_stream="oin")

            def dst_fn(g):
                return OT.rearrange("p (c q) -> p c q", q=128)[:, 4 * g:4 * g + 4, :]
            transpose_2048(Oin, bO, dst_fn, buf("OT"))
            for q4 in range(4):
                pp, bpp = (ps_a, b_ps_a) if q4 % 2 == 0 else (ps_b, b_ps_b)
                for dc in range(16):
                    p.emit("T", lambda t, q4=q4, dc=dc, pp=pp: t.matmul(
                        pp, OT[:, 128 * dc:128 * (dc + 1)],
                        wout_sb[:, 2048 * dc + 512 * q4:2048 * dc + 512 * (q4 + 1)],
                        start=(dc == 0), stop=(dc == 15)),
                        reads=[buf("OT"), buf("wout")], writes=[bpp])
                p.emit("V", lambda v, q4=q4, pp=pp: v.tensor_tensor(
                    att_sb[:, 512 * q4:512 * (q4 + 1)], pp,
                    woutb_sb[:, 512 * q4:512 * (q4 + 1)], ALU.add),
                    reads=[bpp, buf("woutb")], writes=[buf("att")])
            if xt_pending[0] is not None:
                xt_pending[0]()
            xb = buf(f"xin{par}")
            xa = xin_sb[:, D * par:D * (par + 1)]
            p.emit("Y", lambda y, i=i, xa=xa: y.dma_start(
                out=xa, in_=own_x[128 * i:128 * (i + 1), :]), writes=[xb], dma_stream=f"xq{par}")
            ln_tile(att_sb, buf("att"), out_f32=yt_sb, b_outf=buf("yt"), slot=0)
            p.emit("V", lambda v, xa=xa: v.tensor_tensor(yt_sb, yt_sb, xa, ALU.add),
                   reads=[xb, buf("yt")], writes=[buf("yt")])
            p.emit("P", lambda g, i=i: g.dma_start(out=yscr[128 * i:128 * (i + 1), :], in_=yt_sb),
                   reads=[buf("yt")], dma_stream="spY")
            ln_tile(yt_sb, buf("yt"), dst_ap=xn2tok, b_dst=buf("xn2tok"), slot=1)

            def mk_xt(i=i):
                def dst2_fn(g, i=i):
                    return xn2T.rearrange("p (c q) -> p c q", q=1024)[:, 4 * g:4 * g + 4,
                                                                     128 * i:128 * (i + 1)]
                return lambda: transpose_2048(xn2tok, buf("xn2tok"), dst2_fn, buf("xn2T"))
            xt_pending[0] = mk_xt()
            pump_mean(8)
        xt_pending[0]()
        a16.release("wout", "Oin", "OT", "xn2tok")
        a32.release("att", "yt", "woutb")
        return xn2T

    def phase_d(xn2T):
        pump_mean(len(mean_jobs))
        a32.release("xin", "meanbuf", "hcoll", "ht", "h2", "l2r")
        bscal = buf("scal")
        p.emit("V", lambda v: v.tensor_tensor(scal_sb[:, 9:10], scal_sb[:, 2:3],
                                              scal_sb[:, 4:5], ALU.mult),
               reads=[bscal], writes=[bscal])

        if DEBUG_SCAL:
            p.emit("Y", lambda y: y.dma_start(out=dbg_scal[:, :], in_=scal_sb),
                   reads=[buf("scal")], dma_stream="dbg")
        assert mean_idx[0] == len(mean_jobs), "mean jobs must be drained before D"
        a16.release("junk")
        outacc = a32.alloc("outacc", 16 * 1024)
        wst = [a32.alloc(f"wst{j}", 512) for j in range(8)]
        tupAB = [a16.alloc("tupA", 16 * 512), a16.alloc("tupB", 16 * 512)]
        tgt = a16.alloc("tgt", 16 * 512)
        tdn = a16.alloc("tdn", 4 * 2048)
        hid = a16.alloc("hid", 4 * 1024)
        q1 = a16.alloc("q1", 512)
        q2 = a16.alloc("q2", 512)
        q1p = a16.alloc("q1p", 512)
        q2p = a16.alloc("q2p", 512)
        silu_sb2 = [a16.alloc(f"silu{j}", 512) for j in range(2)]

        pd = [PS[:, 1024 + 512 * k:1536 + 512 * k] for k in range(6)]
        b_pd = [buf(f"pd{k}", True) for k in range(6)]

        def quant(dst_ap, b_dst, src_ap, b_src, rcol, use_p=False):
            a, b = (q1p, q2p) if use_p else (q1, q2)
            ba, bb2 = (buf("q1p"), buf("q2p")) if use_p else (buf("q1"), buf("q2"))
            p.emit("V", lambda v: v.tensor_scalar(
                a, src_ap, scal_sb[:, rcol:rcol + 1], 0.5, ALU.mult, ALU.is_gt),
                reads=[b_src, bscal], writes=[ba])
            p.emit("V", lambda v: v.tensor_scalar(
                b, src_ap, scal_sb[:, rcol:rcol + 1], -0.5, ALU.mult, ALU.is_lt),
                reads=[b_src, bscal], writes=[bb2])
            p.emit("P" if use_p else "V",
                   lambda e: e.tensor_tensor(dst_ap, a, b, ALU.subtract),
                   reads=[ba, bb2], writes=[b_dst])

        st_ctr = [0]

        def quant_tile(j, g):
            st = st_ctr[0] % 8
            st_ctr[0] += 1
            use_p = (st_ctr[0] % 6 == 0)
            wa, wb = wst[st], buf(f"wst{st}")
            if j < 32:
                wT = wupT if j < 16 else wgateT
                dc = j % 16
                src = lambda: wT[128 * dc:128 * (dc + 1), 512 * g:512 * (g + 1)]
                if j < 16:
                    dst = tupAB[g % 2][:, 512 * dc:512 * (dc + 1)]
                    bdst = buf(f"tup{g % 2}_{dc}")
                    rcol = 1
                else:
                    dst = tgt[:, 512 * dc:512 * (dc + 1)]
                    bdst = buf(f"tgt{dc}")
                    rcol = 3
            else:
                k = j - 32
                rc, s2 = k // 4, k % 4
                src = lambda: wdownT[512 * g + 128 * rc:512 * g + 128 * (rc + 1),
                                     512 * s2:512 * (s2 + 1)]
                dst = tdn[:, 2048 * rc + 512 * s2:2048 * rc + 512 * (s2 + 1)]
                bdst = buf(f"tdn{rc}_{s2}")
                rcol = 5
            p.emit("Y", lambda y: y.dma_start(out=wa, in_=src()),
                   writes=[wb], dma_stream=f"wD{st}")
            quant(dst, bdst, wa, wb, rcol, use_p)

        for j in range(48):
            quant_tile(j, 0)
        for g in range(16):
            tup = tupAB[g % 2]
            up_tiles = [buf(f"tup{g % 2}_{dc}") for dc in range(16)]
            gt_tiles = [buf(f"tgt{dc}") for dc in range(16)]
            for hc in range(4):
                for th in range(2):
                    idx = hc * 2 + th
                    sl, bsl = silu_sb2[idx % 2], buf(f"silu{idx % 2}")
                    for dc in range(16):
                        rds = ([buf("xn2T")] + up_tiles) if dc in (0, 15) else []
                        p.emit("T", lambda t, hc=hc, th=th, dc=dc, tup=tup: t.matmul(
                            ps_a, tup[:, 512 * dc + 128 * hc:512 * dc + 128 * (hc + 1)],
                            xn2T[:, 1024 * dc + 512 * th:1024 * dc + 512 * (th + 1)],
                            start=(dc == 0), stop=(dc == 15)),
                            reads=rds, writes=[b_ps_a])
                    for dc in range(16):
                        rds = ([buf("xn2T")] + gt_tiles) if dc in (0, 15) else []
                        p.emit("T", lambda t, hc=hc, th=th, dc=dc: t.matmul(
                            ps_b, tgt[:, 512 * dc + 128 * hc:512 * dc + 128 * (hc + 1)],
                            xn2T[:, 1024 * dc + 512 * th:1024 * dc + 512 * (th + 1)],
                            start=(dc == 0), stop=(dc == 15)),
                            reads=rds, writes=[b_ps_b])
                    p.emit("S", lambda sc, sl=sl: sc.activation(
                        sl, ps_a, AF.Silu, scale=scal_sb[:, 0:1]),
                        reads=[b_ps_a, bscal], writes=[bsl])
                    p.emit("V", lambda v, hc=hc, th=th, sl=sl: v.tensor_tensor(
                        hid[:, 1024 * hc + 512 * th:1024 * hc + 512 * (th + 1)],
                        sl, ps_b, ALU.mult),
                        reads=[bsl, b_ps_b], writes=[buf(f"hid{idx}")])
                    if g + 1 < 16:
                        quant_tile(2 * idx, g + 1)
                        quant_tile(2 * idx + 1, g + 1)
            for dm in range(16):
                for th in range(2):
                    it = dm * 2 + th
                    k = it % 6
                    dn_tiles = ([buf(f"tdn{rc}_{dm // 4}") for rc in range(4)] +
                                [buf(f"hid{rc * 2 + th}") for rc in range(4)])
                    for rc in range(4):
                        rds = dn_tiles if rc in (0, 3) else []
                        p.emit("T", lambda t, dm=dm, th=th, rc=rc, k=k: t.matmul(
                            pd[k], tdn[:, 2048 * rc + 128 * dm:2048 * rc + 128 * (dm + 1)],
                            hid[:, 1024 * rc + 512 * th:1024 * rc + 512 * (th + 1)],
                            start=(rc == 0), stop=(rc == 3)),
                            reads=rds, writes=[b_pd[k]])
                    if th == 1:
                        k2 = (2 * dm) % 6
                        if g == 0:
                            p.emit("V", lambda v, dm=dm, k2=k2: v.tensor_copy(
                                outacc[:, 1024 * dm:1024 * (dm + 1)],
                                PS[:, 1024 + 512 * k2:2048 + 512 * k2]),
                                reads=[b_pd[k2], b_pd[k2 + 1]], writes=[buf("outacc")])
                        else:
                            p.emit("V", lambda v, dm=dm, k2=k2: v.tensor_tensor(
                                outacc[:, 1024 * dm:1024 * (dm + 1)],
                                outacc[:, 1024 * dm:1024 * (dm + 1)],
                                PS[:, 1024 + 512 * k2:2048 + 512 * k2], ALU.add),
                                reads=[b_pd[k2], b_pd[k2 + 1], buf("outacc")],
                                writes=[buf("outacc")])
                    if g + 1 < 16:
                        if it < 16:
                            quant_tile(16 + it, g + 1)
                        if th == 1 and dm % 4 == 3:
                            s2 = dm // 4
                            for rc in range(4):
                                quant_tile(32 + rc * 4 + s2, g + 1)
        a16.release("tupA", "tupB", "tgt", "tdn", "hid", "q1", "q2", "q1p", "q2p", "silu0", "silu1", "xn2T")
        a32.release(*[f"wst{j}" for j in range(8)])
        return outacc

    def phase_e(outacc):
        ffn_sb = a32.alloc("ffn", D)
        fout = a32.alloc("fout", D)
        for i in range(NSLOT):
            p.emit("Y", lambda y, i=i: y.dma_start(
                out=fout, in_=yscr[128 * i:128 * (i + 1), :]), writes=[buf("fout")], dma_stream="yin")
            for g in range(4):
                for c in range(4):
                    dc = 4 * g + c
                    p.emit("T", lambda t, dc=dc, c=c, i=i: t.transpose(
                        ps_b[:, 128 * c:128 * (c + 1)],
                        outacc[:, 1024 * dc + 128 * i:1024 * dc + 128 * (i + 1)], identf_sb),
                        reads=[buf("outacc"), buf("identf")], writes=[b_ps_b])
                p.emit("V", lambda v, g=g: v.tensor_scalar(
                    ffn_sb[:, 512 * g:512 * (g + 1)], ps_b, scal_sb[:, 9:10], None, ALU.mult),
                    reads=[b_ps_b, buf("scal")], writes=[buf("ffn")])
            ln_tile(ffn_sb, buf("ffn"), out_f32=ffn_sb, b_outf=buf("ffn"), slot=i % 2)
            p.emit("V", lambda v: v.tensor_tensor(fout, fout, ffn_sb, ALU.add),
                   reads=[buf("ffn"), buf("fout")], writes=[buf("fout")])
            p.emit("Y", lambda y, i=i: y.dma_start(out=out_own[128 * i:128 * (i + 1), :], in_=fout),
                   reads=[buf("fout")], dma_stream="out")
        last_val = p.dsems["out"][1]
        p.recs["Y"].append(Rec("Y", lambda y: y.wait_ge(p.dsems["out"][0], last_val)))

    phase_a()
    if PHASES >= 2:
        p.barrier()
        qmainT, qrope = phase_a2()
        phase_a3(qmainT, qrope)
    if PHASES >= 3:
        p.barrier()
        phase_b()
    if PHASES >= 4:
        p.barrier()
        xn2T = phase_c()
    if PHASES >= 5:
        p.barrier()
        outacc = phase_d(xn2T)
    if PHASES >= 6:
        p.barrier()
        phase_e(outacc)

    p.finalize_and_replay()
    return nc


def _win_rearrange(W):
    R, C = W.shape
    nR, nC = R // 32, C // 32
    A = W.reshape(nR, 32, nC, 32).transpose(0, 2, 1, 3).reshape(nR, nC, 1024)
    B = A.reshape(nR // 32, 32, nC // 32, 32, 1024).transpose(0, 2, 1, 3, 4)
    return np.ascontiguousarray(B.reshape(16384, 1024))


def _rope_tables(n):
    pos = np.repeat(np.arange(n // 2, dtype=np.float64), 2)
    theta = np.exp(-2.0 * pos / n * np.log(10000.0))
    freq = np.arange(T, dtype=np.float64) + 1.0
    ang = freq[:, None] * theta[None, :]
    return np.cos(ang).astype(np.float32), np.sin(ang).astype(np.float32)


def kernel(L0_attn_x, L0_attn_W_down_k, L0_attn_W_down_b, L0_attn_W_uKV_k, L0_attn_W_uKV_b,
           L0_attn_W_uQ_k, L0_attn_W_uQ_b, L0_attn_Wkr_k, L0_attn_Wkr_b, L0_attn_Wqr_k,
           L0_attn_Wqr_b, L0_attn_Wout_k, L0_attn_Wout_b, L0_ffn_W_up, L0_ffn_W_gate,
           L0_ffn_W_down):
    _ensure_device()
    f32 = np.float32
    x = np.ascontiguousarray(np.asarray(L0_attn_x, f32).reshape(T, D))
    wdown = np.ascontiguousarray(np.asarray(L0_attn_W_down_k, f32))
    bdown = np.ascontiguousarray(np.asarray(L0_attn_W_down_b, f32))
    bukv = np.ascontiguousarray(np.asarray(L0_attn_W_uKV_b, f32))
    bout = np.ascontiguousarray(np.asarray(L0_attn_Wout_b, f32))

    identf = np.eye(128, dtype=f32)
    pswap = np.zeros((DHR, DHR), f32)
    for m in range(DHR // 2):
        pswap[2 * m + 1, 2 * m] = -1.0
        pswap[2 * m, 2 * m + 1] = 1.0
    ropekc, ropeks = _rope_tables(DHR)
    ropeqc_full, ropeqs_full = _rope_tables(NH * DHR)

    nc = build_nc()
    common = {
        "x": x, "wdown": wdown,
        "bdown": np.ascontiguousarray(bdown.reshape(8, 128).T),
        "wukv": np.ascontiguousarray(np.asarray(L0_attn_W_uKV_k, f32)),
        "wuq": np.ascontiguousarray(np.asarray(L0_attn_W_uQ_k, f32)),
        "buq": np.ascontiguousarray(np.asarray(L0_attn_W_uQ_b, f32).reshape(16, 128).T),
        "wkr": np.ascontiguousarray(np.asarray(L0_attn_Wkr_k, f32)),
        "bkr": np.ascontiguousarray(np.asarray(L0_attn_Wkr_b, f32)),
        "wqr": np.ascontiguousarray(np.asarray(L0_attn_Wqr_k, f32)),
        "bqr": np.ascontiguousarray(np.asarray(L0_attn_Wqr_b, f32).reshape(16, DHR).T),
        "wout": np.ascontiguousarray(np.asarray(L0_attn_Wout_k, f32)),
        "wupT": np.ascontiguousarray(np.asarray(L0_ffn_W_up, f32).T),
        "wgateT": np.ascontiguousarray(np.asarray(L0_ffn_W_gate, f32).T),
        "wdownT": np.ascontiguousarray(np.asarray(L0_ffn_W_down, f32).T),
        "wupW": _win_rearrange(np.asarray(L0_ffn_W_up, f32)),
        "wgtW": _win_rearrange(np.asarray(L0_ffn_W_gate, f32)),
        "wdnW": _win_rearrange(np.asarray(L0_ffn_W_down, f32)),
        "identf": identf, "pswapf": pswap,
        "ropekc": np.ascontiguousarray(ropekc.T), "ropeks": np.ascontiguousarray(ropeks.T),
        "vbias_in": np.ascontiguousarray(np.broadcast_to(bukv[D:2 * D], (128, D))),
        "woutb_in": np.ascontiguousarray(np.broadcast_to(bout, (128, D))),
    }
    in_maps = []
    for j in range(NCORES):
        tiles = [8 * i + j for i in range(NSLOT)]
        rows = np.concatenate([np.arange(128 * t_, 128 * t_ + 128) for t_ in tiles])
        mk = np.zeros((NSLOT, 2, 128, 512), f32)
        for ii in range(NSLOT):
            t_ = tiles[ii]
            ci = 2 * (ii + 1)
            qglob = 128 * t_ + np.arange(128)
            for m2 in range(2):
                kt = ci - 2 + m2
                for s in range(4):
                    kg = 512 * kt + 128 * s + np.arange(128)[:, None]
                    mk[ii, m2, :, 128 * s:128 * (s + 1)] = np.where(kg <= qglob[None, :], 0.0, NEG)
        im = dict(common)
        im["own_x"] = np.ascontiguousarray(x[rows])
        im["ropeqc"] = np.ascontiguousarray(ropeqc_full[rows].T)
        im["ropeqs"] = np.ascontiguousarray(ropeqs_full[rows].T)
        im["maskadd"] = mk.reshape(NSLOT * 2 * 128, 512).astype(ml_dtypes.bfloat16)
        in_maps.append(im)

    res = run_bass_kernel_spmd(nc, in_maps, core_ids=list(range(NCORES)))
    out = np.zeros((T, D), f32)
    for j in range(NCORES):
        r = res.results[j]["out_own"]
        for ii in range(NSLOT):
            t_ = 8 * ii + j
            out[128 * t_:128 * (t_ + 1)] = r[128 * ii:128 * (ii + 1)]
    return out.reshape(1, T, D)
